# version 11
# baseline (speedup 1.0000x reference)
import numpy as np
import ml_dtypes
import concourse.bass as bass
import concourse.mybir as mybir
from concourse.bass_utils import run_bass_kernel_spmd
from concourse.aot_env import allow_ici_all_to_all

F32 = mybir.dt.float32
BF16 = mybir.dt.bfloat16
AF = mybir.ActivationFunctionType
ALU = mybir.AluOpType

D = 4096
NT = 1024
HCH = 22
NOC_H = 88
EXP_BIAS = -8.0
EXP_SCALE = float(1.0 / np.sqrt(128.0))
LN_EPS = 1e-5
INV_D = 1.0 / D

DEBUG_TAPS = ()


def _r(cc):
    return cc if cc < 8 else 15 - cc


def _off(cc):
    return 0 if cc < 8 else 512


class Ct:
    def __init__(self, handle):
        self.h = handle
        self.n = 0

    def inc(self, k=1):
        self.n += k
        return self.n


def build_program():
    nc = bass.Bass("TRN2", target_bir_lowering=False)

    x_fm = nc.dram_tensor("x_fm", [D, NT], F32, kind="ExternalInput")
    wq_t = nc.dram_tensor("wq_t", [D, D], F32, kind="ExternalInput")
    wk_t = nc.dram_tensor("wk_t", [D, D], F32, kind="ExternalInput")
    wv_t = nc.dram_tensor("wv_t", [D, D], F32, kind="ExternalInput")
    wo_t = nc.dram_tensor("wo_t", [D, D], F32, kind="ExternalInput")
    bq_f = nc.dram_tensor("bq_f", [128, 32], F32, kind="ExternalInput")
    bk_f = nc.dram_tensor("bk_f", [128, 32], F32, kind="ExternalInput")
    bv_f = nc.dram_tensor("bv_f", [128, 32], F32, kind="ExternalInput")
    bo_f = nc.dram_tensor("bo_f", [128, 32], F32, kind="ExternalInput")
    w1_t = nc.dram_tensor("w1_t", [NOC_H * 128, 1024], F32, kind="ExternalInput")
    w3_t = nc.dram_tensor("w3_t", [NOC_H * 128, 1024], F32, kind="ExternalInput")
    w2_t = nc.dram_tensor("w2_t", [32 * 128, HCH * 128], F32, kind="ExternalInput")
    b1_t = nc.dram_tensor("b1_t", [NOC_H * 2, 128], F32, kind="ExternalInput")
    b3_t = nc.dram_tensor("b3_t", [NOC_H * 2, 128], F32, kind="ExternalInput")
    b2_t = nc.dram_tensor("b2_t", [2, D], F32, kind="ExternalInput")
    a1_t = nc.dram_tensor("a1_t", [128, 64], F32, kind="ExternalInput")
    a3_t = nc.dram_tensor("a3_t", [128, 64], F32, kind="ExternalInput")
    a2_t = nc.dram_tensor("a2_t", [128, NOC_H * 2], F32, kind="ExternalInput")

    out_fm = nc.dram_tensor("out_fm", [D, NT], F32, kind="ExternalOutput")

    q_l = nc.dram_tensor("q_l", [D, NT], BF16, kind="Internal")
    k_l = nc.dram_tensor("k_l", [D, NT], BF16, kind="Internal")
    q_o = nc.dram_tensor("q_o", [D, NT], BF16, kind="Internal")
    k_o = nc.dram_tensor("k_o", [D, NT], BF16, kind="Internal")
    v_l_h = [nc.dram_tensor(f"v_l_h{i}", [D // 2, NT], BF16, kind="Internal")
             for i in range(2)]
    v_o_h = [nc.dram_tensor(f"v_o_h{i}", [D // 2, NT], BF16, kind="Internal")
             for i in range(2)]
    cx_l_q = [nc.dram_tensor(f"cx_l_q{i}", [D // 4, NT], BF16, kind="Internal")
              for i in range(4)]
    cx_o_q = [nc.dram_tensor(f"cx_o_q{i}", [D // 4, NT], BF16, kind="Internal")
              for i in range(4)]

    tri_d = nc.inline_tensor(
        np.triu(np.ones((128, 128))).astype(ml_dtypes.bfloat16), "tri_d")

    internal_by_name = {"q_l": q_l, "k_l": k_l, "q_o": q_o, "k_o": k_o}
    taps = [(internal_by_name[n],
             nc.dram_tensor("tap_" + n, [D, NT], BF16, kind="ExternalOutput"))
            for n in DEBUG_TAPS]

    bufA = nc.alloc_sbuf_tensor("bufA", [128, 32768], BF16)
    bufB = nc.alloc_sbuf_tensor("bufB", [128, 32768], BF16)
    cb, _ce = nc.bump_sbuf(65536)
    qs = nc.alloc_sbuf_tensor_at("qs", [128, 8192], BF16, offset=cb + 0)
    ks = nc.alloc_sbuf_tensor_at("ks", [128, 8192], BF16, offset=cb + 16384)
    vt = nc.alloc_sbuf_tensor_at("vt", [128, 8192], BF16, offset=cb + 32768)
    rb_sb = nc.alloc_sbuf_tensor_at("rb_sb", [128, 1024], F32, offset=cb + 49152)
    p2row = nc.alloc_sbuf_tensor_at("p2row", [1, 2048], F32, offset=cb + 53248)
    ctxst = nc.alloc_sbuf_tensor_at("ctxst", [128, 2048], BF16, offset=cb + 61440)
    wbuf = nc.alloc_sbuf_tensor_at("wbuf", [128, 16384], BF16, offset=cb + 0)
    w13 = nc.alloc_sbuf_tensor_at("w13", [128, 4096], BF16, offset=cb + 16384)
    outst = nc.alloc_sbuf_tensor_at("outst", [128, 2048], F32, offset=cb + 16384)
    gst = nc.alloc_sbuf_tensor_at("gst", [128, 4096], BF16, offset=cb + 24576)
    mbuf = nc.alloc_sbuf_tensor_at("mbuf", [128, 2048], F32, offset=cb + 24576)
    xf = nc.alloc_sbuf_tensor_at("xf", [128, 2048], F32, offset=cb + 32768)
    yst = nc.alloc_sbuf_tensor_at("yst", [128, 2048], F32, offset=cb + 40960)
    hst = nc.alloc_sbuf_tensor_at("hst", [128, 2048], BF16, offset=cb + 40960)
    sqst = nc.alloc_sbuf_tensor_at("sqst", [128, 2048], F32, offset=cb + 49152)
    sqstb = nc.alloc_sbuf_tensor_at("sqstb", [128, 2048], BF16, offset=cb + 49152)
    accb = nc.alloc_sbuf_tensor_at("accb", [128, 1024], BF16, offset=cb + 63488)
    qkvst = nc.alloc_sbuf_tensor_at("qkvst", [128, 2048], BF16, offset=cb + 49152)
    tmp32 = nc.alloc_sbuf_tensor_at("tmp32", [128, 2048], F32, offset=cb + 57344)

    rowb = nc.alloc_sbuf_tensor_at("rowb", [1, 3072], F32, offset=cb + 49152)
    trow = nc.alloc_sbuf_tensor("trow", [2, 3072], BF16)
    b1s = nc.alloc_sbuf_tensor("b1s", [2, 256], BF16)
    b3s = nc.alloc_sbuf_tensor("b3s", [2, 256], BF16)
    b2s = nc.alloc_sbuf_tensor("b2s", [2, 256], BF16)
    accf = nc.alloc_sbuf_tensor("accf", [128, 512], F32)
    accp = nc.alloc_sbuf_tensor("accp", [128, 512], F32)

    tri = nc.alloc_sbuf_tensor("tri", [128, 128], BF16)
    ones_cb = nc.alloc_sbuf_tensor("ones_cb", [128, 16], BF16)
    ones_cf = nc.alloc_sbuf_tensor("ones_cf", [128, 8], F32)
    ones_rf = nc.alloc_sbuf_tensor("ones_rf", [1, 128], F32)
    ones_rb = nc.alloc_sbuf_tensor("ones_rb", [1, 128], BF16)
    expb = nc.alloc_sbuf_tensor("expb", [128, 8], F32)
    expsc = nc.alloc_sbuf_tensor("expsc", [128, 8], F32)
    bq_s = nc.alloc_sbuf_tensor("bq_s", [128, 32], F32)
    bk_s = nc.alloc_sbuf_tensor("bk_s", [128, 32], F32)
    bv_s = nc.alloc_sbuf_tensor("bv_s", [128, 32], F32)
    bo_s = nc.alloc_sbuf_tensor("bo_s", [128, 32], F32)
    a1s = nc.alloc_sbuf_tensor("a1s", [128, 64], BF16)
    a3s = nc.alloc_sbuf_tensor("a3s", [128, 64], BF16)
    a2s = nc.alloc_sbuf_tensor("a2s", [128, NOC_H * 2], BF16)

    pb = [nc.alloc_psum_tensor(f"pb{i}", [128, 512], F32) for i in range(8)]

    s_gwx = Ct(nc.alloc_semaphore("s_gwx"))
    s_gwp = [Ct(nc.alloc_semaphore(f"s_gwp{i}")) for i in range(4)]
    s_slc = Ct(nc.alloc_semaphore("s_slc"))
    s_slp = [Ct(nc.alloc_semaphore(f"s_slp{i}")) for i in range(2)]
    s_slv = [Ct(nc.alloc_semaphore(f"s_slv{i}")) for i in range(2)]
    s_ssp = [Ct(nc.alloc_semaphore(f"s_ssp{i}")) for i in range(2)]
    s_pe = Ct(nc.alloc_semaphore("s_pe"))
    s_act = Ct(nc.alloc_semaphore("s_act"))
    s_dve = Ct(nc.alloc_semaphore("s_dve"))
    s_cc = [Ct(nc.alloc_semaphore(f"s_cc{i}")) for i in range(8)]
    s_u01 = Ct(nc.alloc_semaphore("s_u01"))
    s_tr = Ct(nc.alloc_semaphore("s_tr"))
    s_pl = Ct(nc.alloc_semaphore("s_pl"))
    s_cl = [Ct(nc.alloc_semaphore(f"s_cl{i}")) for i in range(2)]

    plan = {"tensor": [], "scalar": [], "vector": [], "sync": [], "gpsimd": []}

    def em(eng, fn):
        plan[eng].append(fn)

    def wait(eng, ct, val):
        if val > 0:
            em(eng, lambda e, s=ct.h, v=val: e.wait_ge(s, v))

    em("vector", lambda e: e.memset(ones_cb[:, :], 1.0))
    em("vector", lambda e: e.memset(ones_cf[:, :], 1.0))
    em("vector", lambda e: e.memset(ones_rf[:, :], 1.0))
    em("vector", lambda e: e.memset(ones_rb[:, :], 1.0))
    em("vector", lambda e: e.memset(expb[:, :], EXP_BIAS))
    em("vector", lambda e: e.memset(expsc[:, :], EXP_SCALE).then_inc(s_dve.h, 1))
    dve_consts = s_dve.inc()

    for src, dst in ((tri_d, tri), (bq_f, bq_s), (bk_f, bk_s),
                     (bv_f, bv_s), (bo_f, bo_s)):
        em("sync", lambda e, a=src, b=dst:
           e.dma_start(out=b[:, :], in_=a[:, :]).then_inc(s_slc.h, 16))
        s_slc.inc(16)
    sl_consts = s_slc.n

    for i0 in range(4):
        def f(e, i0=i0):
            src = x_fm[i0 * 1024:(i0 + 1) * 1024, :].rearrange(
                "(i p) t -> p i t", p=128)
            dst = bufA[:, i0 * 8192:(i0 + 1) * 8192].rearrange(
                "p (i t) -> p i t", i=8)
            e.dma_start(out=dst, in_=src).then_inc(s_gwx.h, 16)
        em("gpsimd", f)
        s_gwx.inc(16)
    for src, dst in ((a1_t, a1s), (a3_t, a3s), (a2_t, a2s)):
        em("gpsimd", lambda e, a=src, b=dst:
           e.dma_start(out=b[:, :], in_=a[:, :]).then_inc(s_gwx.h, 16))
        s_gwx.inc(16)
    gw_x = s_gwx.n

    v_order = [oc for oc in range(32) if oc % 4 < 2] + \
              [oc for oc in range(32) if oc % 4 >= 2]
    p1_mats = [(wk_t, bk_s, list(range(32))),
               (wq_t, bq_s, list(range(32))),
               (wv_t, bv_s, v_order)]

    def p1_store_target(m, oc):
        if m == 0:
            return k_l, oc * 128
        if m == 1:
            return q_l, oc * 128
        half = (oc % 4) // 2
        return v_l_h[half], (oc // 4) * 256 + (oc % 4 % 2) * 128

    pe_p1, act_p1, ss_p1, gw_p1 = {}, {}, {}, {}

    def p1_load(it):
        m, pos = divmod(it, 32)
        oc = p1_mats[m][2][pos]
        wt = p1_mats[m][0]
        if it - 4 >= 0:
            wait("gpsimd", s_pe, pe_p1[it - 4])
        em("gpsimd", lambda e, w=wt, oc=oc, par=it % 4:
           e.dma_start(out=wbuf[:, par * 4096:(par + 1) * 4096],
                       in_=w[oc * 128:(oc + 1) * 128, :]).then_inc(s_gwp[par].h, 16))
        gw_p1[it] = s_gwp[it % 4].inc(16)

    def p1_collective(src, dsto, ci):
        wait("gpsimd", s_ssp[0], s_ssp[0].n)
        wait("gpsimd", s_ssp[1], s_ssp[1].n)
        def fcc(e, a=src, b=dsto, ci=ci):
            with allow_ici_all_to_all():
                e.collective_compute(
                    "AllToAll", ALU.bypass,
                    replica_groups=[[0, 1, 2, 3, 4, 5, 6, 7]],
                    ins=[a.ap()], outs=[b.ap()]).then_inc(s_cc[ci].h, 1)
        em("gpsimd", fcc)
        s_cc[ci].inc()

    for _pre in range(4):
        p1_load(_pre)
    for it in range(96):
        m, pos = divmod(it, 32)
        wt, bias, order = p1_mats[m]
        oc = order[pos]
        dst, drow = p1_store_target(m, oc)
        par4 = it % 4
        par = it % 2
        b0, b1 = 2 * par, 2 * par + 1
        wait("tensor", s_gwp[par4], gw_p1[it])
        if it == 0:
            wait("tensor", s_gwx, gw_x)
        if it - 2 >= 0:
            wait("tensor", s_dve, act_p1[it - 2])
        for i in range(32):
            st, sp = (i == 0), (i == 31)
            em("tensor", lambda e, b=b0, p=par4, i=i, st=st, sp=sp:
               e.matmul(pb[b][:, :],
                        wbuf[:, p * 4096 + i * 128:p * 4096 + (i + 1) * 128],
                        bufA[:, i * 1024:i * 1024 + 512],
                        start=st, stop=sp, skip_group_check=True))
            if not sp:
                em("tensor", lambda e, b=b1, p=par4, i=i, st=st:
                   e.matmul(pb[b][:, :],
                            wbuf[:, p * 4096 + i * 128:p * 4096 + (i + 1) * 128],
                            bufA[:, i * 1024 + 512:(i + 1) * 1024],
                            start=st, stop=False, skip_group_check=True))
            else:
                em("tensor", lambda e, b=b1, p=par4, i=i:
                   e.matmul(pb[b][:, :],
                            wbuf[:, p * 4096 + i * 128:p * 4096 + (i + 1) * 128],
                            bufA[:, i * 1024 + 512:(i + 1) * 1024],
                            start=False, stop=True,
                            skip_group_check=True).then_inc(s_pe.h, 1))
        pe_p1[it] = s_pe.inc()

        wait("vector", s_pe, pe_p1[it])
        if it == 0:
            wait("vector", s_slc, sl_consts)
        if it - 2 >= 0:
            wait("vector", s_ssp[it % 2], ss_p1[it - 2])
        em("vector", lambda e, b=b0, par=par, bias=bias, oc=oc:
           e.tensor_scalar_add(out=qkvst[:, par * 1024:par * 1024 + 512],
                               in0=pb[b][:, :], scalar1=bias[:, oc:oc + 1]))
        em("vector", lambda e, b=b1, par=par, bias=bias, oc=oc:
           e.tensor_scalar_add(out=qkvst[:, par * 1024 + 512:(par + 1) * 1024],
                               in0=pb[b][:, :],
                               scalar1=bias[:, oc:oc + 1]).then_inc(s_dve.h, 1))
        act_p1[it] = s_dve.inc()

        wait("sync", s_dve, act_p1[it])
        em("sync", lambda e, d=dst, drow=drow, par=par:
           e.dma_start(out=d[drow:drow + 128, :],
                       in_=qkvst[:, par * 1024:(par + 1) * 1024]).then_inc(s_ssp[par].h, 16))
        ss_p1[it] = s_ssp[it % 2].inc(16)

        if it + 4 < 96:
            p1_load(it + 4)
        if m == 0 and pos == 31:
            p1_collective(k_l, k_o, 0)
        elif m == 1 and pos == 31:
            p1_collective(q_l, q_o, 1)
        elif m == 2 and pos == 15:
            p1_collective(v_l_h[0], v_o_h[0], 2)
        elif m == 2 and pos == 31:
            p1_collective(v_l_h[1], v_o_h[1], 3)
    p1_pe_end = s_pe.n
    p1_ss_end = (s_ssp[0].n, s_ssp[1].n)

    units = [(h_l, b) for h_l in range(4) for b in range(2)]
    sl_qk, sl_v, pe_unit = {}, {}, {}
    pv_item, exp_glob_list = {}, []
    exp_item_last, dve_mask_item = {}, {}
    dve_s, dve_rb, dve_ctx, ss_ctx = {}, {}, {}, {}

    def u_bufs(u):
        if u < 2:
            base = u * 12288
            return (bufB, base, bufB, base + 4096, bufB, base + 8192)
        par = u % 2
        return (qs, par * 4096, ks, par * 4096, vt, par * 4096)

    def p2_load(u):
        h_l, b = units[u]
        par = u % 2
        QT, qb, KT, kb, VT, vb = u_bufs(u)
        if u < 2:
            ldq, sem = "scalar", s_u01
            wait(ldq, s_cc[0], 1)
            wait(ldq, s_cc[1], 1)
        else:
            ldq = "sync"
            if u in (2, 3):
                wait(ldq, s_pe, p1_pe_end)
            else:
                wait(ldq, s_pe, pv_item[(u - 2) * 8 + 7])
        for qc in range(8):
            cc = 8 * b + qc
            r, of = _r(cc), _off(cc)
            em(ldq, lambda e, r=r, of=of, h_l=h_l, QT=QT, qb=qb, qc=qc,
               sm=(s_u01 if u < 2 else s_slp[par]).h:
               e.dma_start(out=QT[:, qb + qc * 512:qb + (qc + 1) * 512],
                           in_=q_o[512 * r + 128 * h_l:512 * r + 128 * (h_l + 1),
                                   of:of + 512]).then_inc(sm, 16))
            (s_u01 if u < 2 else s_slp[par]).inc(16)
            em(ldq, lambda e, r=r, of=of, h_l=h_l, KT=KT, kb=kb, qc=qc,
               sm=(s_u01 if u < 2 else s_slp[par]).h:
               e.dma_start(out=KT[:, kb + qc * 512:kb + (qc + 1) * 512],
                           in_=k_o[512 * r + 128 * h_l:512 * r + 128 * (h_l + 1),
                                   of:of + 512]).then_inc(sm, 16))
            (s_u01 if u < 2 else s_slp[par]).inc(16)
        sl_qk[u] = ((s_u01 if u < 2 else s_slp[par]),
                    (s_u01 if u < 2 else s_slp[par]).n)
        if u < 2:
            wait(ldq, s_cc[2], 1)
        elif u in (2, 3) or u in (4, 5):
            wait(ldq, s_cc[2 + h_l // 2], 1)
        vhalf = v_o_h[h_l // 2]
        hl2 = h_l % 2
        for kt in range(32):
            cc = 8 * b + kt // 4
            w = kt % 4
            r, of = _r(cc), _off(cc)
            em(ldq, lambda e, r=r, of=of, w=w, hl2=hl2, vhalf=vhalf, VT=VT, vb=vb, kt=kt,
               sm=(s_u01 if u < 2 else s_slv[par]).h:
               e.dma_start(out=VT[:, vb + kt * 128:vb + (kt + 1) * 128],
                           in_=vhalf[256 * r + 128 * hl2:256 * r + 128 * (hl2 + 1),
                                     of + 128 * w:of + 128 * (w + 1)],
                           transpose=True).then_inc(sm, 16))
            (s_u01 if u < 2 else s_slv[par]).inc(16)
        sl_v[u] = ((s_u01 if u < 2 else s_slv[par]),
                   (s_u01 if u < 2 else s_slv[par]).n)

    p2_load(0)
    p2_load(1)
    for _u in (0, 1):
        sl_qk[_u] = (s_u01, s_u01.n)
        sl_v[_u] = (s_u01, s_u01.n)
    ktg = 0
    pe_S_item, pe_rb_item = {}, {}

    def it_of(k):
        u, qc = divmod(k, 8)
        h_l, b = units[u]
        return u, qc, h_l, b, u % 2, k % 2, 4 * (qc + 1), (k % 2) * 16384

    def emit_sT_exp(k):
        nonlocal ktg
        u, qc, h_l, b, par, kp, nkt, half = it_of(k)
        if qc == 0:
            wait("tensor", sl_qk[u][0], sl_qk[u][1])
        for kt in range(nkt):
            sb = 4 + (ktg % 2)
            if ktg - 2 >= 0:
                wait("tensor", s_act, exp_glob_list[ktg - 2])
            em("tensor", lambda e, sb=sb, par=par, kt=kt, qc=qc:
               e.matmul(pb[sb][:, :],
                        ks[:, par * 4096 + kt * 128:par * 4096 + (kt + 1) * 128],
                        qs[:, par * 4096 + qc * 512:par * 4096 + (qc + 1) * 512],
                        start=True, stop=True).then_inc(s_pe.h, 1))
            pe_sT = s_pe.inc()
            wait("scalar", s_pe, pe_sT)
            if kt == 0:
                wait("scalar", s_pe, pv_item.get(k - 2, 0))
                if k == 0:
                    wait("scalar", s_dve, dve_consts)
            em("scalar", lambda e, sb=sb, half=half, kt=kt:
               e.activation(out=bufA[:, half + kt * 512:half + (kt + 1) * 512],
                            in_=pb[sb][:, :], func=AF.Exp,
                            bias=expb[:, 0:1],
                            scale=expsc[:, 0:1]).then_inc(s_act.h, 1))
            exp_glob_list.append(s_act.inc())
            ktg += 1
        exp_item_last[k] = s_act.n

    def emit_masks(k):
        u, qc, h_l, b, par, kp, nkt, half = it_of(k)
        if k == 0:
            wait("vector", s_slc, sl_consts)
        wait("vector", s_act, exp_item_last[k])
        for d in range(4):
            base = half + (nkt - 4 + d) * 512
            if d > 0:
                em("vector", lambda e, base=base, d=d:
                   e.memset(bufA[:, base:base + 128 * d], 0.0))
            last = (d == 3)
            em("vector", lambda e, base=base, d=d, last=last:
               (e.tensor_mul(out=bufA[:, base + 128 * d:base + 128 * (d + 1)],
                             in0=bufA[:, base + 128 * d:base + 128 * (d + 1)],
                             in1=tri[:, :]).then_inc(s_dve.h, 1) if last else
                e.tensor_mul(out=bufA[:, base + 128 * d:base + 128 * (d + 1)],
                             in0=bufA[:, base + 128 * d:base + 128 * (d + 1)],
                             in1=tri[:, :])))
        dve_mask_item[k] = s_dve.inc()

    def emit_sum(j):
        u, qc, h_l, b, par, kp, nkt, half = it_of(j)
        sbank = 6 + kp
        wait("tensor", s_dve, tree_done[j])
        if j - 2 >= 0:
            wait("tensor", s_dve, dve_s[j - 2])
        em("tensor", lambda e, sbank=sbank, kp=kp:
           e.matmul(pb[sbank][0:1, :], ones_cb[:, 0:1],
                    accb[:, kp * 512:(kp + 1) * 512],
                    start=True, stop=True,
                    skip_group_check=True).then_inc(s_pe.h, 1))
        pe_S_item[j] = s_pe.inc()
        wait("vector", s_pe, pe_S_item[j])
        em("vector", lambda e, sbank=sbank, kp=kp:
           e.tensor_copy(out=p2row[0:1, kp * 512:(kp + 1) * 512],
                         in_=pb[sbank][0:1, :]).then_inc(s_dve.h, 1))
        dve_scp = s_dve.inc()
        wait("vector", s_dve, dve_scp)
        em("vector", lambda e, kp=kp:
           e.reciprocal(out=p2row[0:1, 1024 + kp * 512:1024 + (kp + 1) * 512],
                        in_=p2row[0:1, kp * 512:(kp + 1) * 512]).then_inc(s_dve.h, 1))
        dve_s[j] = s_dve.inc()

    def emit_rb(j):
        u, qc, h_l, b, par, kp, nkt, half = it_of(j)
        rbank = kp
        if j <= 1:
            wait("vector", s_ssp[0], p1_ss_end[0])
            wait("vector", s_ssp[1], p1_ss_end[1])
        wait("tensor", s_dve, dve_s[j])
        if j - 2 >= 0:
            wait("tensor", s_dve, dve_rb[j - 2])
        em("tensor", lambda e, rbank=rbank, kp=kp:
           e.matmul(pb[rbank][:, :], ones_rf[0:1, :],
                    p2row[0:1, 1024 + kp * 512:1024 + (kp + 1) * 512],
                    start=True, stop=True).then_inc(s_pe.h, 1))
        pe_rb_item[j] = s_pe.inc()
        wait("vector", s_pe, pe_rb_item[j])
        em("vector", lambda e, rbank=rbank, kp=kp:
           e.tensor_copy(out=rb_sb[:, kp * 512:(kp + 1) * 512],
                         in_=pb[rbank][:, :]).then_inc(s_dve.h, 1))
        dve_rb[j] = s_dve.inc()

    def emit_pv(j):
        u, qc, h_l, b, par, kp, nkt, half = it_of(j)
        cbank = 2 + kp
        QT, qb, KT, kb, VT, vb = u_bufs(u)
        wait("tensor", s_act, exp_item_last[j])
        wait("tensor", s_dve, dve_mask_item[j])
        if qc == 0:
            wait("tensor", sl_v[u][0], sl_v[u][1])
        if j - 2 >= 0:
            wait("tensor", s_dve, dve_ctx[j - 2])
        for kt in range(nkt):
            st, sp = (kt == 0), (kt == nkt - 1)
            if not sp:
                em("tensor", lambda e, cbank=cbank, VT=VT, vb=vb, half=half, kt=kt, st=st:
                   e.matmul(pb[cbank][:, :],
                            VT[:, vb + kt * 128:vb + (kt + 1) * 128],
                            bufA[:, half + kt * 512:half + (kt + 1) * 512],
                            start=st, stop=False, skip_group_check=True))
            else:
                em("tensor", lambda e, cbank=cbank, VT=VT, vb=vb, half=half, kt=kt, st=st:
                   e.matmul(pb[cbank][:, :],
                            VT[:, vb + kt * 128:vb + (kt + 1) * 128],
                            bufA[:, half + kt * 512:half + (kt + 1) * 512],
                            start=st, stop=True,
                            skip_group_check=True).then_inc(s_pe.h, 1))
        pv_item[j] = s_pe.inc()

    def emit_ctx_evac(j):
        u, qc, h_l, b, par, kp, nkt, half = it_of(j)
        cbank = 2 + kp
        cc = 8 * b + qc
        wait("vector", s_pe, pv_item[j])
        wait("vector", s_dve, dve_rb[j])
        if j - 2 >= 0:
            wait("vector", s_ssp[kp], ss_ctx[j - 2])
        em("vector", lambda e, cbank=cbank, kp=kp:
           e.tensor_tensor(out=ctxst[:, kp * 512:(kp + 1) * 512],
                           in0=pb[cbank][:, :],
                           in1=rb_sb[:, kp * 512:(kp + 1) * 512],
                           op=ALU.mult).then_inc(s_dve.h, 1))
        dve_ctx[j] = s_dve.inc()
        wait("sync", s_dve, dve_ctx[j])
        r, of = _r(cc), _off(cc)
        cq = cx_l_q[h_l]
        em("sync", lambda e, r=r, of=of, cq=cq, kp=kp:
           e.dma_start(out=cq[128 * r:128 * (r + 1), of:of + 512],
                       in_=ctxst[:, kp * 512:(kp + 1) * 512]).then_inc(s_ssp[kp].h, 16))
        ss_ctx[j] = s_ssp[kp].inc(16)

    def consumer_passes(k):
        passes = []
        if k - 1 >= 0:
            j = k - 1
            u, qc, h_l, b, par, kp, nkt, half = it_of(j)
            sbank = 6 + kp
            cbank = 2 + kp

            def sum_pass(e=None, j=j, sbank=sbank, kp=kp):
                wait("tensor", s_dve, tree_done[j])
                if j == 0:
                    wait("tensor", s_dve, dve_consts)
                em("tensor", lambda e, sbank=sbank, kp=kp:
                   e.matmul(pb[sbank][0:1, :], ones_cb[:, 0:1],
                            accb[:, kp * 512:(kp + 1) * 512],
                            start=True, stop=True,
                            skip_group_check=True).then_inc(s_pe.h, 1))
                pe_S_item[j] = s_pe.inc()
                wait("vector", s_pe, pe_S_item[j])
                em("vector", lambda e, sbank=sbank, kp=kp:
                   e.tensor_copy(out=p2row[0:1, kp * 512:(kp + 1) * 512],
                                 in_=pb[sbank][0:1, :]).then_inc(s_dve.h, 1))
                dve_scp = s_dve.inc()
                wait("vector", s_dve, dve_scp)
                em("vector", lambda e, kp=kp:
                   e.reciprocal(out=p2row[0:1, 1024 + kp * 512:1024 + (kp + 1) * 512],
                                in_=p2row[0:1, kp * 512:(kp + 1) * 512]).then_inc(s_dve.h, 1))
                dve_s[j] = s_dve.inc()
            passes.append(sum_pass)
            def pv_gate(e=None, j=j, par=par, qc=qc, u=u):
                wait("tensor", s_act, exp_item_last[j])
                wait("tensor", s_dve, dve_mask_item[j])
                if qc == 0:
                    wait("tensor", sl_v[u][0], sl_v[u][1])
                if j - 2 >= 0:
                    wait("tensor", s_dve, dve_ctx[j - 2])
            passes.append(pv_gate)
            for kt in range(nkt):
                st, sp = (kt == 0), (kt == nkt - 1)
                VTj, vbj = u_bufs(u)[4], u_bufs(u)[5]
                def f(e=None, cbank=cbank, half=half, kt=kt, st=st, sp=sp, j=j,
                      VT=VTj, vb=vbj):
                    if not sp:
                        em("tensor", lambda e, cbank=cbank, VT=VT, vb=vb, half=half, kt=kt, st=st:
                           e.matmul(pb[cbank][:, :],
                                    VT[:, vb + kt * 128:vb + (kt + 1) * 128],
                                    bufA[:, half + kt * 512:half + (kt + 1) * 512],
                                    start=st, stop=False, skip_group_check=True))
                    else:
                        em("tensor", lambda e, cbank=cbank, VT=VT, vb=vb, half=half, kt=kt, st=st:
                           e.matmul(pb[cbank][:, :],
                                    VT[:, vb + kt * 128:vb + (kt + 1) * 128],
                                    bufA[:, half + kt * 512:half + (kt + 1) * 512],
                                    start=st, stop=True,
                                    skip_group_check=True).then_inc(s_pe.h, 1))
                        pv_item[j] = s_pe.inc()
                passes.append(f)
        if k - 2 >= 0:
            passes.append(lambda e=None, j=k - 2: emit_rb(j))
        return passes

    def emit_sT_exp_interleaved(k):
        nonlocal ktg
        u, qc, h_l, b, par, kp, nkt, half = it_of(k)
        queue = consumer_passes(k)
        qi = 0
        if qc == 0:
            wait("tensor", sl_qk[u][0], sl_qk[u][1])
        for kt in range(nkt):
            sb = 4 + (ktg % 2)
            if ktg - 2 >= 0:
                wait("tensor", s_act, exp_glob_list[ktg - 2])
            QT, qb, KT, kb, VT, vb = u_bufs(u)
            em("tensor", lambda e, sb=sb, kt=kt, qc=qc, QT=QT, qb=qb, KT=KT, kb=kb:
               e.matmul(pb[sb][:, :],
                        KT[:, kb + kt * 128:kb + (kt + 1) * 128],
                        QT[:, qb + qc * 512:qb + (qc + 1) * 512],
                        start=True, stop=True,
                        skip_group_check=True).then_inc(s_pe.h, 1))
            pe_sT = s_pe.inc()
            wait("scalar", s_pe, pe_sT)
            if kt == 0:
                wait("scalar", s_pe, pv_item.get(k - 2, 0))
                if k == 0:
                    wait("scalar", s_dve, dve_consts)
            em("scalar", lambda e, sb=sb, half=half, kt=kt:
               e.activation(out=bufA[:, half + kt * 512:half + (kt + 1) * 512],
                            in_=pb[sb][:, :], func=AF.Exp,
                            bias=expb[:, 0:1],
                            scale=expsc[:, 0:1]).then_inc(s_act.h, 1))
            exp_glob_list.append(s_act.inc())
            ktg += 1
            want = ((kt + 1) * len(queue)) // nkt
            while qi < want:
                queue[qi]()
                qi += 1
        while qi < len(queue):
            queue[qi]()
            qi += 1
        exp_item_last[k] = s_act.n

    def emit_tree(j):
        u, qc, h_l, b, par, kp, nkt, half = it_of(j)
        ev = [half + kt * 512 for kt in range(0, nkt, 2)]
        od = [half + kt * 512 for kt in range(1, nkt, 2)]
        wait("vector", s_act, exp_item_last[j])
        wait("vector", s_dve, dve_mask_item[j])
        if j - 2 >= 0:
            wait("vector", s_pe, pe_S_item[j - 2])
        wait("gpsimd", s_act, exp_item_last[j])
        wait("gpsimd", s_dve, dve_mask_item[j])
        if j - 2 >= 0:
            wait("gpsimd", s_pe, pe_S_item[j - 2])
        if len(ev) == 1:
            em("vector", lambda e, c0=ev[0], kp=kp:
               e.tensor_copy(out=accf[:, 0:512],
                             in_=bufA[:, c0:c0 + 512]).then_inc(s_dve.h, 1))
        else:
            em("vector", lambda e, c0=ev[0], c1=ev[1], kp=kp:
               e.tensor_add(out=accf[:, 0:512],
                            in0=bufA[:, c0:c0 + 512],
                            in1=bufA[:, c1:c1 + 512]).then_inc(s_dve.h, 1))
        c = s_dve.inc()
        for cx in ev[2:]:
            wait("vector", s_dve, c)
            em("vector", lambda e, cx=cx, kp=kp:
               e.tensor_add(out=accf[:, 0:512],
                            in0=accf[:, 0:512],
                            in1=bufA[:, cx:cx + 512]).then_inc(s_dve.h, 1))
            c = s_dve.inc()
        if len(od) == 1:
            em("gpsimd", lambda e, c0=od[0], kp=kp:
               e.tensor_copy(out=accp[:, 0:512],
                             in_=bufA[:, c0:c0 + 512]).then_inc(s_tr.h, 1))
        else:
            em("gpsimd", lambda e, c0=od[0], c1=od[1], kp=kp:
               e.tensor_add(out=accp[:, 0:512],
                            in0=bufA[:, c0:c0 + 512],
                            in1=bufA[:, c1:c1 + 512]).then_inc(s_tr.h, 1))
        cp = s_tr.inc()
        for cx in od[2:]:
            wait("gpsimd", s_tr, cp)
            em("gpsimd", lambda e, cx=cx, kp=kp:
               e.tensor_add(out=accp[:, 0:512],
                            in0=accp[:, 0:512],
                            in1=bufA[:, cx:cx + 512]).then_inc(s_tr.h, 1))
            cp = s_tr.inc()
        wait("vector", s_dve, c)
        wait("vector", s_tr, cp)
        em("vector", lambda e, kp=kp:
           e.tensor_add(out=accb[:, kp * 512:(kp + 1) * 512],
                        in0=accf[:, 0:512],
                        in1=accp[:, 0:512]).then_inc(s_dve.h, 1))
        tree_done[j] = s_dve.inc()

    tree_done = {}
    for k in range(64):
        if k - 1 >= 0:
            emit_tree(k - 1)
        emit_sT_exp_interleaved(k)
        if k - 2 >= 0:
            emit_ctx_evac(k - 2)
        emit_masks(k)
        if k % 8 == 0 and k >= 8:
            u_prev = k // 8 - 1
            if u_prev + 2 < 8:
                p2_load(u_prev + 2)
        if k in (17, 33, 49):
            qn = (k - 17) // 16
            wait("gpsimd", s_ssp[0], ss_ctx[k - 3])
            wait("gpsimd", s_ssp[1], ss_ctx[k - 2])
            def fcc_ctxq(e, qn=qn):
                with allow_ici_all_to_all():
                    e.collective_compute("AllToAll", ALU.bypass,
                                         replica_groups=[[0, 1, 2, 3, 4, 5, 6, 7]],
                                         ins=[cx_l_q[qn].ap()],
                                         outs=[cx_o_q[qn].ap()]).then_inc(s_cc[4 + qn].h, 1)
            em("gpsimd", fcc_ctxq)
            s_cc[4 + qn].inc()
    emit_tree(63)
    emit_sum(63)
    emit_rb(62)
    emit_pv(63)
    emit_ctx_evac(62)
    emit_rb(63)
    emit_ctx_evac(63)

    wait("gpsimd", s_ssp[0], s_ssp[0].n)
    wait("gpsimd", s_ssp[1], s_ssp[1].n)
    def fcc_ctx3(e):
        with allow_ici_all_to_all():
            e.collective_compute("AllToAll", ALU.bypass,
                                 replica_groups=[[0, 1, 2, 3, 4, 5, 6, 7]],
                                 ins=[cx_l_q[3].ap()],
                                 outs=[cx_o_q[3].ap()]).then_inc(s_cc[7].h, 1)
    em("gpsimd", fcc_ctx3)
    s_cc[7].inc()
    p2_pe_end = s_pe.n
    p2_dve_end = dve_ctx[63]

    qsem = [s_slc, s_slv[0], s_cl[0], s_cl[1]]
    ctx_ld_q = []
    wait("sync", s_pe, pv_item[15])
    for qn in range(4):
        wait("sync", s_cc[4 + qn], 1)
        for oc in range(qn, 32, 4):
            m_ = oc // 4
            em("sync", lambda e, oc=oc, m_=m_, qn=qn:
               e.dma_start(out=bufB[:, oc * 1024:(oc + 1) * 1024],
                           in_=cx_o_q[qn][m_ * 128:(m_ + 1) * 128, :]).then_inc(qsem[qn].h, 16))
            qsem[qn].inc(16)
        ctx_ld_q.append(qsem[qn].n)

    gw_wo, sl_xf = {}, {}
    pe_main3, pe_st3, dve_stt3, act_sq3, act_cast3 = {}, {}, {}, {}, {}

    def p3_wo_load(oc):
        if oc - 2 >= 0:
            wait("gpsimd", s_pe, pe_main3[(oc - 2, 1)])
        else:
            wait("gpsimd", s_pe, p2_pe_end)
        em("gpsimd", lambda e, oc=oc, par=oc % 2:
           e.dma_start(out=wbuf[:, par * 4096:(par + 1) * 4096],
                       in_=wo_t[oc * 128:(oc + 1) * 128, :]).then_inc(s_gwp[oc % 2].h, 16))
        gw_wo[oc] = s_gwp[oc % 2].inc(16)

    def p3_xf_load(oc):
        if oc - 2 >= 0:
            wait("sync", s_dve, dve_stt3[(oc - 2, 1)])
        else:
            wait("sync", s_pe, p2_pe_end)
        em("sync", lambda e, oc=oc, par=oc % 2:
           e.dma_start(out=xf[:, par * 1024:(par + 1) * 1024],
                       in_=x_fm[oc * 128:(oc + 1) * 128, :]).then_inc(s_slp[oc % 2].h, 16))
        sl_xf[oc] = s_slp[oc % 2].inc(16)

    p3_wo_load(0)
    p3_wo_load(1)
    p3_xf_load(0)
    p3_xf_load(1)
    kseq = [i for qn in range(4) for i in range(qn, 32, 4)]
    for oc in range(32):
        par = oc % 2
        for t in range(2):
            br = (oc * 2 + t) % 4
            wait("tensor", s_gwp[oc % 2], gw_wo[oc])
            if oc == 0 and t == 0:
                wait("tensor", s_slc, ctx_ld_q[0])
                wait("tensor", s_dve, p2_dve_end)
            if (oc * 2 + t) - 4 >= 0:
                prev = (oc * 2 + t - 4)
                wait("tensor", s_dve, dve_stt3[(prev // 2, prev % 2)])
            for ki, i in enumerate(kseq):
                st, sp = (ki == 0), (ki == 31)
                if oc == 0 and t == 0 and ki in (8, 16, 24):
                    wait("tensor", qsem[ki // 8], ctx_ld_q[ki // 8])
                if not sp:
                    em("tensor", lambda e, br=br, par=par, i=i, t=t, st=st:
                       e.matmul(pb[br][:, :],
                                wbuf[:, par * 4096 + i * 128:par * 4096 + (i + 1) * 128],
                                bufB[:, i * 1024 + t * 512:i * 1024 + (t + 1) * 512],
                                start=st, stop=False, skip_group_check=True))
                else:
                    em("tensor", lambda e, br=br, par=par, i=i, t=t:
                       e.matmul(pb[br][:, :],
                                wbuf[:, par * 4096 + i * 128:par * 4096 + (i + 1) * 128],
                                bufB[:, i * 1024 + t * 512:i * 1024 + (t + 1) * 512],
                                start=False, stop=True,
                                skip_group_check=True).then_inc(s_pe.h, 1))
            pe_main3[(oc, t)] = s_pe.inc()
            wait("vector", s_pe, pe_main3[(oc, t)])
            wait("vector", s_slp[oc % 2], sl_xf[oc])
            if oc - 2 >= 0:
                wait("vector", s_pe, pe_st3.get((oc - 2, t), 0))
                wait("vector", s_act, act_cast3.get((oc - 2, t), 0))
            em("vector", lambda e, br=br, par=par, t=t, oc=oc:
               e.scalar_tensor_tensor(out=yst[:, par * 1024 + t * 512:par * 1024 + (t + 1) * 512],
                                      in0=pb[br][:, :], scalar=bo_s[:, oc:oc + 1],
                                      in1=xf[:, par * 1024 + t * 512:par * 1024 + (t + 1) * 512],
                                      op0=ALU.add, op1=ALU.add).then_inc(s_dve.h, 1))
            dve_stt3[(oc, t)] = s_dve.inc()
            wait("scalar", s_dve, dve_stt3[(oc, t)])
            if oc - 2 >= 0:
                wait("scalar", s_pe, pe_st3.get((oc - 2, t), 0))
            em("scalar", lambda e, par=par, t=t:
               e.activation(out=sqstb[:, par * 1024 + t * 512:par * 1024 + (t + 1) * 512],
                            in_=yst[:, par * 1024 + t * 512:par * 1024 + (t + 1) * 512],
                            func=AF.Square).then_inc(s_act.h, 1))
            act_sq3[(oc, t)] = s_act.inc()
            em("scalar", lambda e, par=par, t=t, oc=oc:
               e.activation(out=bufA[:, oc * 1024 + t * 512:oc * 1024 + (t + 1) * 512],
                            in_=yst[:, par * 1024 + t * 512:par * 1024 + (t + 1) * 512],
                            func=AF.Copy).then_inc(s_act.h, 1))
            act_cast3[(oc, t)] = s_act.inc()
        for so in ([oc - 1] if oc >= 1 else []) + ([31] if oc == 31 else []):
            spar = so % 2
            for t in range(2):
                wait("tensor", s_act, act_cast3[(so, t)])
                em("tensor", lambda e, t=t, so=so:
                   e.matmul(pb[4 + t][0:1, :], ones_cb[:, 0:1],
                            bufA[:, so * 1024 + t * 512:so * 1024 + (t + 1) * 512],
                            start=(so == 0), stop=(so == 31),
                            skip_group_check=True))
                if t == 1:
                    em("tensor", lambda e, spar=spar, t=t, so=so:
                       e.matmul(pb[6 + t][0:1, :], ones_cb[:, 0:1],
                                sqstb[:, spar * 1024 + t * 512:spar * 1024 + (t + 1) * 512],
                                start=(so == 0), stop=(so == 31),
                                skip_group_check=True).then_inc(s_pe.h, 1))
                else:
                    em("tensor", lambda e, spar=spar, t=t, so=so:
                       e.matmul(pb[6 + t][0:1, :], ones_cb[:, 0:1],
                                sqstb[:, spar * 1024 + t * 512:spar * 1024 + (t + 1) * 512],
                                start=(so == 0), stop=(so == 31),
                                skip_group_check=True))
            pe_st3[(so, 0)] = pe_st3[(so, 1)] = s_pe.inc()
        if oc + 2 < 32:
            p3_wo_load(oc + 2)
            p3_xf_load(oc + 2)

    def ln_finalize(pe_stats_done):
        wait("vector", s_pe, pe_stats_done)
        for i, bank in enumerate((4, 5, 6, 7)):
            last = (i == 3)
            em("vector", lambda e, bank=bank, i=i, last=last:
               (e.tensor_copy(out=rowb[0:1, i * 512:(i + 1) * 512],
                              in_=pb[bank][0:1, :]).then_inc(s_dve.h, 1) if last
                else e.tensor_copy(out=rowb[0:1, i * 512:(i + 1) * 512],
                                   in_=pb[bank][0:1, :])))
        c1 = s_dve.inc()
        wait("vector", s_dve, c1)
        em("vector", lambda e:
           e.tensor_scalar_mul(out=rowb[0:1, 0:1024], in0=rowb[0:1, 0:1024],
                               scalar1=INV_D))
        em("vector", lambda e:
           e.tensor_scalar_mul(out=rowb[0:1, 1024:2048], in0=rowb[0:1, 1024:2048],
                               scalar1=INV_D).then_inc(s_dve.h, 1))
        c2 = s_dve.inc()
        wait("vector", s_dve, c2)
        em("vector", lambda e:
           e.tensor_mul(out=rowb[0:1, 2048:3072], in0=rowb[0:1, 0:1024],
                        in1=rowb[0:1, 0:1024]).then_inc(s_dve.h, 1))
        c3 = s_dve.inc()
        wait("vector", s_dve, c3)
        em("vector", lambda e:
           e.tensor_sub(out=rowb[0:1, 1024:2048], in0=rowb[0:1, 1024:2048],
                        in1=rowb[0:1, 2048:3072]).then_inc(s_dve.h, 1))
        c4 = s_dve.inc()
        wait("vector", s_dve, c4)
        em("vector", lambda e:
           e.tensor_scalar_add(out=rowb[0:1, 1024:2048], in0=rowb[0:1, 1024:2048],
                               scalar1=LN_EPS).then_inc(s_dve.h, 1))
        c5 = s_dve.inc()
        wait("scalar", s_dve, c5)
        em("scalar", lambda e:
           e.activation(out=rowb[0:1, 2048:3072], in_=rowb[0:1, 1024:2048],
                        func=AF.Sqrt).then_inc(s_act.h, 1))
        c6 = s_act.inc()
        wait("vector", s_act, c6)
        em("vector", lambda e:
           e.reciprocal(out=rowb[0:1, 1024:2048],
                        in_=rowb[0:1, 2048:3072]).then_inc(s_dve.h, 1))
        c7 = s_dve.inc()
        wait("tensor", s_dve, c7)
        for t in range(2):
            em("tensor", lambda e, t=t:
               e.matmul(pb[t][:, :], ones_rf[0:1, :], rowb[0:1, t * 512:(t + 1) * 512],
                        start=True, stop=True,
                        skip_group_check=True).then_inc(s_pe.h, 1))
            s_pe.inc()
            em("tensor", lambda e, t=t:
               e.matmul(pb[2 + t][:, :], ones_rf[0:1, :],
                        rowb[0:1, 1024 + t * 512:1024 + (t + 1) * 512],
                        start=True, stop=True,
                        skip_group_check=True).then_inc(s_pe.h, 1))
            s_pe.inc()
        c8 = s_pe.n
        wait("vector", s_pe, c8)
        for t in range(2):
            em("vector", lambda e, t=t:
               e.tensor_copy(out=mbuf[:, t * 512:(t + 1) * 512], in_=pb[t][:, :]))
            last = (t == 1)
            em("vector", lambda e, t=t, last=last:
               (e.tensor_copy(out=mbuf[:, 1024 + t * 512:1024 + (t + 1) * 512],
                              in_=pb[2 + t][:, :]).then_inc(s_dve.h, 1) if last
                else e.tensor_copy(out=mbuf[:, 1024 + t * 512:1024 + (t + 1) * 512],
                                   in_=pb[2 + t][:, :])))
        return s_dve.inc()

    mbuf_done1 = ln_finalize(pe_st3[(31, 0)])

    dve_sub1, pl_mul1 = {}, {}
    for oc in range(32):
        if oc == 0:
            wait("vector", s_dve, mbuf_done1)
        if oc - 2 >= 0:
            wait("vector", s_pl, pl_mul1[oc - 2])
        em("vector", lambda e, oc=oc, par=oc % 2:
           e.tensor_sub(out=tmp32[:, par * 1024:(par + 1) * 1024],
                        in0=bufA[:, oc * 1024:(oc + 1) * 1024],
                        in1=mbuf[:, 0:1024]).then_inc(s_dve.h, 1))
        dve_sub1[oc] = s_dve.inc()
        if oc == 0:
            wait("gpsimd", s_dve, mbuf_done1)
        wait("gpsimd", s_dve, dve_sub1[oc])
        em("gpsimd", lambda e, oc=oc, par=oc % 2:
           e.tensor_mul(out=bufA[:, oc * 1024:(oc + 1) * 1024],
                        in0=tmp32[:, par * 1024:(par + 1) * 1024],
                        in1=mbuf[:, 1024:2048]).then_inc(s_pl.h, 1))
        pl_mul1[oc] = s_pl.inc()
    y1_done_pl = s_pl.n

    Y4 = 22528

    for i in range(32):
        wait("tensor", s_pl, pl_mul1[i])
        if i == 0:
            wait("tensor", s_dve, mbuf_done1)
        for (src_a, bank0) in ((a1s, 4), (a3s, 6)):
            for t in range(2):
                st, sp = (i == 0), (i == 31)
                if not sp:
                    em("tensor", lambda e, a=src_a, bank=bank0 + t, i=i, t=t, st=st:
                       e.matmul(pb[bank][0:2, :], a[:, i * 2:(i + 1) * 2],
                                bufA[:, i * 1024 + t * 512:i * 1024 + (t + 1) * 512],
                                start=st, stop=False, skip_group_check=True))
                else:
                    em("tensor", lambda e, a=src_a, bank=bank0 + t, i=i, t=t:
                       e.matmul(pb[bank][0:2, :], a[:, i * 2:(i + 1) * 2],
                                bufA[:, i * 1024 + t * 512:i * 1024 + (t + 1) * 512],
                                start=False, stop=True,
                                skip_group_check=True).then_inc(s_pe.h, 1))
    pe_t13_all = s_pe.inc(4)

    prev_comb = 0
    for ic in range(8):
        wait("vector", s_pl, pl_mul1[ic + 24])
        if prev_comb:
            wait("vector", s_dve, prev_comb)
        em("vector", lambda e, ic=ic:
           e.tensor_add(out=yst[:, 0:1024], in0=bufA[:, ic * 1024:(ic + 1) * 1024],
                        in1=bufA[:, (ic + 8) * 1024:(ic + 9) * 1024]))
        em("vector", lambda e, ic=ic:
           e.tensor_add(out=yst[:, 1024:2048],
                        in0=bufA[:, (ic + 16) * 1024:(ic + 17) * 1024],
                        in1=bufA[:, (ic + 24) * 1024:(ic + 25) * 1024]).then_inc(s_dve.h, 1))
        c = s_dve.inc()
        wait("vector", s_dve, c)
        em("vector", lambda e, ic=ic:
           e.tensor_add(out=bufB[:, Y4 + ic * 1024:Y4 + (ic + 1) * 1024],
                        in0=yst[:, 0:1024], in1=yst[:, 1024:2048]).then_inc(s_dve.h, 1))
        prev_comb = s_dve.inc()
    y4_done = s_dve.n

    wait("vector", s_pe, pe_t13_all)
    for (bank0, col0) in ((4, 0), (6, 1024)):
        for t in range(2):
            last = (bank0 == 6 and t == 1)
            em("vector", lambda e, bank=bank0 + t, col=col0 + t * 512, last=last:
               (e.tensor_scalar_mul(out=trow[0:2, col:col + 512],
                                    in0=pb[bank][0:2, :],
                                    scalar1=0.5).then_inc(s_dve.h, 1) if last else
                e.tensor_scalar_mul(out=trow[0:2, col:col + 512],
                                    in0=pb[bank][0:2, :], scalar1=0.5)))
    t13_done = s_dve.inc()

    gw_w13, pe_h, act_h, dve_g, dve_fold, pe_t2 = {}, {}, {}, {}, {}, {}

    def p4a_load(oc):
        par = oc % 2
        if oc - 2 >= 0:
            wait("gpsimd", s_pe, pe_h[oc - 2])
        else:
            wait("gpsimd", s_pe, pe_st3[(31, 0)])
        em("gpsimd", lambda e, oc=oc, par=par:
           e.dma_start(out=w13[:, par * 1024:(par + 1) * 1024],
                       in_=w1_t[oc * 128:(oc + 1) * 128, :]).then_inc(s_gwp[par].h, 16))
        s_gwp[par].inc(16)
        em("gpsimd", lambda e, oc=oc, par=par:
           e.dma_start(out=w13[:, 2048 + par * 1024:2048 + (par + 1) * 1024],
                       in_=w3_t[oc * 128:(oc + 1) * 128, :]).then_inc(s_gwp[par].h, 16))
        s_gwp[par].inc(16)
        em("gpsimd", lambda e, oc=oc, par=par:
           e.dma_start(out=b1s[0:2, par * 128:(par + 1) * 128],
                       in_=b1_t[oc * 2:(oc + 1) * 2, :]).then_inc(s_gwp[par].h, 16))
        s_gwp[par].inc(16)
        em("gpsimd", lambda e, oc=oc, par=par:
           e.dma_start(out=b3s[0:2, par * 128:(par + 1) * 128],
                       in_=b3_t[oc * 2:(oc + 1) * 2, :]).then_inc(s_gwp[par].h, 16))
        gw_w13[oc] = s_gwp[par].inc(16)

    p4a_load(0)
    p4a_load(1)
    for rc in range(HCH):
        for bo in range(4):
            oc = rc * 4 + bo
            par = oc % 2
            hb = [0, 1, 2, 3] if par == 0 else [2, 3, 0, 1]
            wait("tensor", s_gwp[par], gw_w13[oc])
            if oc == 0:
                wait("tensor", s_dve, y4_done)
                wait("tensor", s_dve, t13_done)
                wait("tensor", s_pl, y1_done_pl)
            if oc - 1 >= 0:
                wait("tensor", s_act, act_h[oc - 1])
            for (wcol, bnk0, bsrc, tcol) in ((par * 1024, hb[0], b1s, 0),
                                             (2048 + par * 1024, hb[2], b3s, 1024)):
                for i in range(8):
                    for t in range(2):
                        em("tensor", lambda e, wcol=wcol, bank=bnk0 + t, i=i, t=t:
                           e.matmul(pb[bank][:, :],
                                    w13[:, wcol + i * 128:wcol + (i + 1) * 128],
                                    bufB[:, Y4 + i * 1024 + t * 512:Y4 + i * 1024 + (t + 1) * 512],
                                    start=(i == 0), stop=False,
                                    skip_group_check=True))
                for t in range(2):
                    em("tensor", lambda e, bank=bnk0 + t, bsrc=bsrc, par=par, tcol=tcol, t=t:
                       e.matmul(pb[bank][:, :], bsrc[0:2, par * 128:(par + 1) * 128],
                                trow[0:2, tcol + t * 512:tcol + (t + 1) * 512],
                                start=False, stop=True,
                                skip_group_check=True).then_inc(s_pe.h, 1))
                    s_pe.inc()
            pe_h[oc] = s_pe.n
            wait("scalar", s_pe, pe_h[oc])
            if oc - 2 >= 0:
                wait("scalar", s_dve, dve_g[oc - 2])
            for t in range(2):
                em("scalar", lambda e, bank=hb[0] + t, t=t:
                   e.activation(out=hst[:, t * 512:(t + 1) * 512],
                                in_=pb[bank][:, :], func=AF.Silu))
                last = (t == 1)
                em("scalar", lambda e, bank=hb[2] + t, t=t, last=last:
                   (e.activation(out=hst[:, 1024 + t * 512:1024 + (t + 1) * 512],
                                 in_=pb[bank][:, :],
                                 func=AF.Copy).then_inc(s_act.h, 1) if last else
                    e.activation(out=hst[:, 1024 + t * 512:1024 + (t + 1) * 512],
                                 in_=pb[bank][:, :], func=AF.Copy)))
            act_h[oc] = s_act.inc()
            wait("vector", s_act, act_h[oc])
            if oc - 4 >= 0:
                wait("vector", s_pe, pe_t2.get(oc - 4, 0))
                wait("vector", s_dve, dve_fold.get(rc - 1, 0))
            for t in range(2):
                last = (t == 1)
                em("vector", lambda e, bo=bo, t=t, last=last:
                   (e.tensor_mul(out=gst[:, bo * 1024 + t * 512:bo * 1024 + (t + 1) * 512],
                                 in0=hst[:, t * 512:(t + 1) * 512],
                                 in1=hst[:, 1024 + t * 512:1024 + (t + 1) * 512]).then_inc(s_dve.h, 1)
                    if last else
                    e.tensor_mul(out=gst[:, bo * 1024 + t * 512:bo * 1024 + (t + 1) * 512],
                                 in0=hst[:, t * 512:(t + 1) * 512],
                                 in1=hst[:, 1024 + t * 512:1024 + (t + 1) * 512])))
            dve_g[oc] = s_dve.inc()
            wait("tensor", s_dve, dve_g[oc])
            for t in range(2):
                last = (t == 1)
                em("tensor", lambda e, oc=oc, bo=bo, t=t, last=last:
                   (e.matmul(pb[6 + t][0:2, :], a2s[:, oc * 2:(oc + 1) * 2],
                             gst[:, bo * 1024 + t * 512:bo * 1024 + (t + 1) * 512],
                             start=(oc == 0), stop=(oc == NOC_H - 1),
                             skip_group_check=True).then_inc(s_pe.h, 1) if last else
                    e.matmul(pb[6 + t][0:2, :], a2s[:, oc * 2:(oc + 1) * 2],
                             gst[:, bo * 1024 + t * 512:bo * 1024 + (t + 1) * 512],
                             start=(oc == 0), stop=(oc == NOC_H - 1),
                             skip_group_check=True)))
            pe_t2[oc] = s_pe.inc()
            if oc + 2 < NOC_H:
                p4a_load(oc + 2)
        wait("vector", s_dve, dve_g[rc * 4 + 3])
        if rc >= 1:
            wait("vector", s_dve, dve_fold[rc - 1])
        em("vector", lambda e:
           e.tensor_add(out=tmp32[:, 0:1024], in0=gst[:, 0:1024], in1=gst[:, 1024:2048]))
        em("vector", lambda e:
           e.tensor_add(out=tmp32[:, 1024:2048], in0=gst[:, 2048:3072],
                        in1=gst[:, 3072:4096]).then_inc(s_dve.h, 1))
        c = s_dve.inc()
        wait("vector", s_dve, c)
        em("vector", lambda e, rc=rc:
           e.tensor_add(out=bufB[:, rc * 1024:(rc + 1) * 1024],
                        in0=tmp32[:, 0:1024], in1=tmp32[:, 1024:2048]).then_inc(s_dve.h, 1))
        dve_fold[rc] = s_dve.inc()
    wait("vector", s_pe, pe_t2[NOC_H - 1])
    for t in range(2):
        last = (t == 1)
        em("vector", lambda e, t=t, last=last:
           (e.tensor_scalar_mul(out=trow[0:2, 2048 + t * 512:2048 + (t + 1) * 512],
                                in0=pb[6 + t][0:2, :],
                                scalar1=0.5).then_inc(s_dve.h, 1) if last else
            e.tensor_scalar_mul(out=trow[0:2, 2048 + t * 512:2048 + (t + 1) * 512],
                                in0=pb[6 + t][0:2, :], scalar1=0.5)))
    t2_done = s_dve.inc()
    p4a_pe_end = s_pe.n

    gw_w2, pe_main4, pe_st4, dve_add4, act_sq4, act_cast4 = {}, {}, {}, {}, {}, {}

    def p4b_load(o2):
        par = o2 % 2
        if o2 - 2 >= 0:
            wait("gpsimd", s_pe, pe_main4[(o2 - 2, 1)])
        else:
            wait("gpsimd", s_pe, p4a_pe_end)
        em("gpsimd", lambda e, o2=o2, par=par:
           e.dma_start(out=wbuf[:, par * 4096:par * 4096 + 2816],
                       in_=w2_t[o2 * 128:(o2 + 1) * 128, :]).then_inc(s_gwp[par].h, 16))
        s_gwp[par].inc(16)
        em("gpsimd", lambda e, o2=o2, par=par:
           e.dma_start(out=b2s[0:2, par * 128:(par + 1) * 128],
                       in_=b2_t[0:2, o2 * 128:(o2 + 1) * 128]).then_inc(s_gwp[par].h, 16))
        gw_w2[o2] = s_gwp[par].inc(16)

    p4b_load(0)
    p4b_load(1)
    for o2 in range(32):
        par = o2 % 2
        br2 = [(o2 * 2) % 4, (o2 * 2 + 1) % 4]
        wait("tensor", s_gwp[par], gw_w2[o2])
        if o2 == 0:
            wait("tensor", s_dve, t2_done)
            wait("tensor", s_dve, dve_fold[HCH - 1])
        for t in range(2):
            if (o2 * 2 + t) - 4 >= 0:
                prev = o2 * 2 + t - 4
                wait("tensor", s_dve, dve_add4[(prev // 2, prev % 2)])
        for kc in range(HCH):
            for t in range(2):
                em("tensor", lambda e, br=br2[t], par=par, kc=kc, t=t:
                   e.matmul(pb[br][:, :],
                            wbuf[:, par * 4096 + kc * 128:par * 4096 + (kc + 1) * 128],
                            bufB[:, kc * 1024 + t * 512:kc * 1024 + (t + 1) * 512],
                            start=(kc == 0), stop=False, skip_group_check=True))
        for t in range(2):
            em("tensor", lambda e, br=br2[t], par=par, t=t:
               e.matmul(pb[br][:, :], b2s[0:2, par * 128:(par + 1) * 128],
                        trow[0:2, 2048 + t * 512:2048 + (t + 1) * 512],
                        start=False, stop=True,
                        skip_group_check=True).then_inc(s_pe.h, 1))
            s_pe.inc()
        for t in range(2):
            pe_main4[(o2, t)] = s_pe.n
            wait("vector", s_pe, pe_main4[(o2, t)])
            if o2 - 2 >= 0:
                wait("vector", s_pe, pe_st4.get((o2 - 2, t), 0))
                wait("vector", s_act, act_cast4.get((o2 - 2, t), 0))
            em("vector", lambda e, br=br2[t], par=par, t=t, o2=o2:
               e.tensor_add(out=yst[:, par * 1024 + t * 512:par * 1024 + (t + 1) * 512],
                            in0=pb[br][:, :],
                            in1=bufA[:, o2 * 1024 + t * 512:o2 * 1024 + (t + 1) * 512]).then_inc(s_dve.h, 1))
            dve_add4[(o2, t)] = s_dve.inc()
            wait("scalar", s_dve, dve_add4[(o2, t)])
            if o2 - 2 >= 0:
                wait("scalar", s_pe, pe_st4.get((o2 - 2, t), 0))
            em("scalar", lambda e, par=par, t=t:
               e.activation(out=sqstb[:, par * 1024 + t * 512:par * 1024 + (t + 1) * 512],
                            in_=yst[:, par * 1024 + t * 512:par * 1024 + (t + 1) * 512],
                            func=AF.Square).then_inc(s_act.h, 1))
            act_sq4[(o2, t)] = s_act.inc()
            em("scalar", lambda e, par=par, t=t, o2=o2:
               e.activation(out=bufA[:, o2 * 1024 + t * 512:o2 * 1024 + (t + 1) * 512],
                            in_=yst[:, par * 1024 + t * 512:par * 1024 + (t + 1) * 512],
                            func=AF.Copy).then_inc(s_act.h, 1))
            act_cast4[(o2, t)] = s_act.inc()
        for so in ([o2 - 1] if o2 >= 1 else []) + ([31] if o2 == 31 else []):
            spar = so % 2
            for t in range(2):
                wait("tensor", s_act, act_cast4[(so, t)])
                em("tensor", lambda e, t=t, so=so:
                   e.matmul(pb[4 + t][0:1, :], ones_cb[:, 0:1],
                            bufA[:, so * 1024 + t * 512:so * 1024 + (t + 1) * 512],
                            start=(so == 0), stop=(so == 31),
                            skip_group_check=True))
                if t == 1:
                    em("tensor", lambda e, spar=spar, t=t, so=so:
                       e.matmul(pb[6 + t][0:1, :], ones_cb[:, 0:1],
                                sqstb[:, spar * 1024 + t * 512:spar * 1024 + (t + 1) * 512],
                                start=(so == 0), stop=(so == 31),
                                skip_group_check=True).then_inc(s_pe.h, 1))
                else:
                    em("tensor", lambda e, spar=spar, t=t, so=so:
                       e.matmul(pb[6 + t][0:1, :], ones_cb[:, 0:1],
                                sqstb[:, spar * 1024 + t * 512:spar * 1024 + (t + 1) * 512],
                                start=(so == 0), stop=(so == 31),
                                skip_group_check=True))
            pe_st4[(so, 0)] = pe_st4[(so, 1)] = s_pe.inc()
        if o2 + 2 < 32:
            p4b_load(o2 + 2)

    mbuf_done2 = ln_finalize(pe_st4[(31, 0)])

    ss_out = {}
    pl_mul2 = {}
    for oc in range(32):
        par = oc % 2
        if oc == 0:
            wait("vector", s_dve, mbuf_done2)
            wait("vector", s_act, act_cast4[(31, 1)])
        if oc - 2 >= 0:
            wait("vector", s_pl, pl_mul2[oc - 2])
        em("vector", lambda e, oc=oc, par=par:
           e.tensor_sub(out=tmp32[:, par * 1024:(par + 1) * 1024],
                        in0=bufA[:, oc * 1024:(oc + 1) * 1024],
                        in1=mbuf[:, 0:1024]).then_inc(s_dve.h, 1))
        c = s_dve.inc()
        if oc == 0:
            wait("gpsimd", s_dve, mbuf_done2)
        wait("gpsimd", s_dve, c)
        if oc - 2 >= 0:
            wait("gpsimd", s_ssp[par], ss_out[oc - 2])
        em("gpsimd", lambda e, oc=oc, par=par:
           e.tensor_mul(out=outst[:, par * 1024:(par + 1) * 1024],
                        in0=tmp32[:, par * 1024:(par + 1) * 1024],
                        in1=mbuf[:, 1024:2048]).then_inc(s_pl.h, 1))
        pl_mul2[oc] = s_pl.inc()
        wait("sync", s_pl, pl_mul2[oc])
        em("sync", lambda e, oc=oc, par=par:
           e.dma_start(out=out_fm[oc * 128:(oc + 1) * 128, :],
                       in_=outst[:, par * 1024:(par + 1) * 1024]).then_inc(s_ssp[par].h, 16))
        ss_out[oc] = s_ssp[par].inc(16)

    if taps:
        wait("gpsimd", s_cc[3], 1)
        wait("gpsimd", s_ssp[0], s_ssp[0].n)
        wait("gpsimd", s_ssp[1], s_ssp[1].n)
    for src, dst in taps:
        em("gpsimd", lambda e, a=src, b=dst:
           e.dma_start(out=b[:, :], in_=a[:, :]).then_inc(s_ssp[0].h, 16))
        s_ssp[0].inc(16)

    wait("sync", s_ssp[0], s_ssp[0].n)
    wait("sync", s_ssp[1], s_ssp[1].n)

    with nc.Block() as block:
        @block.tensor
        def _(e):
            for fn in plan["tensor"]:
                fn(e)

        @block.scalar
        def _(e):
            for fn in plan["scalar"]:
                fn(e)

        @block.vector
        def _(e):
            for fn in plan["vector"]:
                fn(e)

        @block.sync
        def _(e):
            for fn in plan["sync"]:
                fn(e)

        @block.gpsimd
        def _(e):
            for fn in plan["gpsimd"]:
                fn(e)

    return nc


def _tile_wT(w):
    n_o, n_i = w.shape[0] // 128, w.shape[1] // 128
    return np.ascontiguousarray(
        w.reshape(n_o, 128, n_i, 128).transpose(0, 3, 2, 1).reshape(n_o * 128, n_i * 128))


def _prep_w13(w):
    wr = np.zeros((4, HCH, 128, 1024), dtype=np.float32)
    for bo in range(4):
        seg = w[bo * 2752:(bo + 1) * 2752]
        wr[bo, :21] = seg[:2688].reshape(21, 128, 1024)
        wr[bo, 21, :64] = seg[2688:]
    arr = wr.reshape(4, HCH, 128, 8, 128).transpose(1, 0, 4, 3, 2)
    return np.ascontiguousarray(arr.reshape(NOC_H * 128, 1024))


def _prep_w2(w):
    wp = np.zeros((4096, HCH * 128), dtype=np.float32)
    wp[:, :2752] = w
    arr = wp.reshape(32, 128, HCH, 128).transpose(0, 3, 2, 1)
    return np.ascontiguousarray(arr.reshape(32 * 128, HCH * 128))


def _prep_b13(bmat):
    br = np.zeros((4, HCH, 128, 2), dtype=np.float32)
    for bo in range(4):
        seg = bmat[bo * 2752:(bo + 1) * 2752]
        br[bo, :21] = seg[:2688].reshape(21, 128, 2)
        br[bo, 21, :64] = seg[2688:]
    arr = br.transpose(1, 0, 3, 2)
    return np.ascontiguousarray(arr.reshape(NOC_H * 2, 128))


def _prep_a2(a2):
    ar = np.zeros((4, HCH, 128, 2), dtype=np.float32)
    for bo in range(4):
        seg = a2[:, bo * 2752:(bo + 1) * 2752].T
        ar[bo, :21] = seg[:2688].reshape(21, 128, 2)
        ar[bo, 21, :64] = seg[2688:]
    arr = ar.transpose(1, 0, 2, 3).reshape(NOC_H, 128, 2).transpose(1, 0, 2)
    return np.ascontiguousarray(arr.reshape(128, NOC_H * 2))


def kernel(**inputs):
    inp = {k: np.asarray(v) for k, v in inputs.items()}
    x = inp["L0_attn_x"]

    shared = {
        "wq_t": _tile_wT(inp["L0_attn_Wq"]),
        "wk_t": _tile_wT(inp["L0_attn_Wk"]),
        "wv_t": _tile_wT(inp["L0_attn_Wv"]),
        "wo_t": _tile_wT(inp["L0_attn_Wo"]),
        "bq_f": np.ascontiguousarray(inp["L0_attn_bq"].reshape(32, 128).T),
        "bk_f": np.ascontiguousarray(inp["L0_attn_bk"].reshape(32, 128).T),
        "bv_f": np.ascontiguousarray(inp["L0_attn_bv"].reshape(32, 128).T),
        "bo_f": np.ascontiguousarray(inp["L0_attn_bo"].reshape(32, 128).T),
        "w1_t": _prep_w13(inp["L0_ffn_w1q"]),
        "w3_t": _prep_w13(inp["L0_ffn_w3q"]),
        "w2_t": _prep_w2(inp["L0_ffn_w2q"]),
        "b1_t": _prep_b13(inp["L0_ffn_b1"]),
        "b3_t": _prep_b13(inp["L0_ffn_b3"]),
        "b2_t": np.ascontiguousarray(inp["L0_ffn_b2"].T),
        "a1_t": np.ascontiguousarray(
            inp["L0_ffn_a1"].reshape(2, 32, 128).transpose(2, 1, 0).reshape(128, 64)),
        "a3_t": np.ascontiguousarray(
            inp["L0_ffn_a3"].reshape(2, 32, 128).transpose(2, 1, 0).reshape(128, 64)),
        "a2_t": _prep_a2(inp["L0_ffn_a2"]),
    }

    xT = [np.ascontiguousarray(x[b_].T) for b_ in range(2)]
    in_maps = []
    for c in range(8):
        cc0, cc1 = c, 15 - c
        parts = []
        for cc in (cc0, cc1):
            bb, s0 = cc // 8, (cc % 8) * 512
            parts.append(xT[bb][:, s0:s0 + 512])
        x_c = np.ascontiguousarray(np.concatenate(parts, axis=1))
        m = {"x_fm": x_c}
        m.update(shared)
        in_maps.append(m)

    nc = build_program()
    res = run_bass_kernel_spmd(nc, in_maps, core_ids=list(range(8)))

    out = np.empty((2, 4096, 4096), dtype=np.float32)
    for c in range(8):
        oc_fm = res.results[c]["out_fm"]
        for i, cc in enumerate((c, 15 - c)):
            bb, s0 = cc // 8, (cc % 8) * 512
            out[bb, s0:s0 + 512, :] = oc_fm[:, i * 512:(i + 1) * 512].T
    if DEBUG_TAPS:
        kernel._tap_results = res
    return out


# version 12
# speedup vs baseline: 1.1791x; 1.1791x over previous
import numpy as np
import ml_dtypes
import concourse.bass as bass
import concourse.mybir as mybir
from concourse.bass_utils import run_bass_kernel_spmd
from concourse.aot_env import allow_ici_all_to_all

F32 = mybir.dt.float32
BF16 = mybir.dt.bfloat16
AF = mybir.ActivationFunctionType
ALU = mybir.AluOpType

D = 4096
NT = 1024
HCH = 22
NOC_H = 88
EXP_BIAS = -8.0
EXP_SCALE = float(1.0 / np.sqrt(128.0))
LN_EPS = 1e-5
INV_D = 1.0 / D

DEBUG_TAPS = ()


def _r(cc):
    return cc if cc < 8 else 15 - cc


def _off(cc):
    return 0 if cc < 8 else 512


class Ct:
    def __init__(self, handle):
        self.h = handle
        self.n = 0

    def inc(self, k=1):
        self.n += k
        return self.n


def build_program():
    nc = bass.Bass("TRN2", target_bir_lowering=False)

    x_fm = nc.dram_tensor("x_fm", [D, NT], F32, kind="ExternalInput")
    wq_t = nc.dram_tensor("wq_t", [D, D], F32, kind="ExternalInput")
    wk_t = nc.dram_tensor("wk_t", [D, D], F32, kind="ExternalInput")
    wv_t = nc.dram_tensor("wv_t", [D, D], F32, kind="ExternalInput")
    wo_t = nc.dram_tensor("wo_t", [D, D], F32, kind="ExternalInput")
    bq_f = nc.dram_tensor("bq_f", [128, 32], F32, kind="ExternalInput")
    bk_f = nc.dram_tensor("bk_f", [128, 32], F32, kind="ExternalInput")
    bv_f = nc.dram_tensor("bv_f", [128, 32], F32, kind="ExternalInput")
    bo_f = nc.dram_tensor("bo_f", [128, 32], F32, kind="ExternalInput")
    w1_t = nc.dram_tensor("w1_t", [NOC_H * 128, 1024], F32, kind="ExternalInput")
    w3_t = nc.dram_tensor("w3_t", [NOC_H * 128, 1024], F32, kind="ExternalInput")
    w2_t = nc.dram_tensor("w2_t", [32 * 128, HCH * 128], F32, kind="ExternalInput")
    b1_t = nc.dram_tensor("b1_t", [NOC_H * 2, 128], F32, kind="ExternalInput")
    b3_t = nc.dram_tensor("b3_t", [NOC_H * 2, 128], F32, kind="ExternalInput")
    b2_t = nc.dram_tensor("b2_t", [2, D], F32, kind="ExternalInput")
    a1_t = nc.dram_tensor("a1_t", [128, 64], F32, kind="ExternalInput")
    a3_t = nc.dram_tensor("a3_t", [128, 64], F32, kind="ExternalInput")
    a2_t = nc.dram_tensor("a2_t", [128, NOC_H * 2], F32, kind="ExternalInput")

    out_fm = nc.dram_tensor("out_fm", [D, NT], F32, kind="ExternalOutput")

    q_l = nc.dram_tensor("q_l", [D, NT], BF16, kind="Internal")
    k_l = nc.dram_tensor("k_l", [D, NT], BF16, kind="Internal")
    q_o = nc.dram_tensor("q_o", [D, NT], BF16, kind="Internal")
    k_o = nc.dram_tensor("k_o", [D, NT], BF16, kind="Internal")
    v_l_h = [nc.dram_tensor(f"v_l_h{i}", [D // 2, NT], BF16, kind="Internal")
             for i in range(2)]
    v_o_h = [nc.dram_tensor(f"v_o_h{i}", [D // 2, NT], BF16, kind="Internal")
             for i in range(2)]
    cx_l_q = [nc.dram_tensor(f"cx_l_q{i}", [D // 4, NT], BF16, kind="Internal")
              for i in range(4)]
    cx_o_q = [nc.dram_tensor(f"cx_o_q{i}", [D // 4, NT], BF16, kind="Internal")
              for i in range(4)]

    tri_d = nc.inline_tensor(
        np.triu(np.ones((128, 128))).astype(ml_dtypes.bfloat16), "tri_d")

    internal_by_name = {"q_l": q_l, "k_l": k_l, "q_o": q_o, "k_o": k_o}
    taps = [(internal_by_name[n],
             nc.dram_tensor("tap_" + n, [D, NT], BF16, kind="ExternalOutput"))
            for n in DEBUG_TAPS]

    bufA = nc.alloc_sbuf_tensor("bufA", [128, 32768], BF16)
    bufB = nc.alloc_sbuf_tensor("bufB", [128, 32768], BF16)
    cb, _ce = nc.bump_sbuf(65536)
    qs = nc.alloc_sbuf_tensor_at("qs", [128, 8192], BF16, offset=cb + 0)
    ks = nc.alloc_sbuf_tensor_at("ks", [128, 8192], BF16, offset=cb + 16384)
    vt = nc.alloc_sbuf_tensor_at("vt", [128, 8192], BF16, offset=cb + 32768)
    rb_sb = nc.alloc_sbuf_tensor_at("rb_sb", [128, 1024], F32, offset=cb + 49152)
    p2row = nc.alloc_sbuf_tensor_at("p2row", [1, 2048], F32, offset=cb + 53248)
    ctxst = nc.alloc_sbuf_tensor_at("ctxst", [128, 2048], BF16, offset=cb + 61440)
    wbuf = nc.alloc_sbuf_tensor_at("wbuf", [128, 16384], BF16, offset=cb + 0)
    w13 = nc.alloc_sbuf_tensor_at("w13", [128, 4096], BF16, offset=cb + 16384)
    outst = nc.alloc_sbuf_tensor_at("outst", [128, 2048], F32, offset=cb + 16384)
    gst = nc.alloc_sbuf_tensor_at("gst", [128, 4096], BF16, offset=cb + 24576)
    mbuf = nc.alloc_sbuf_tensor_at("mbuf", [128, 2048], F32, offset=cb + 24576)
    xf = nc.alloc_sbuf_tensor_at("xf", [128, 2048], F32, offset=cb + 32768)
    yst = nc.alloc_sbuf_tensor_at("yst", [128, 2048], F32, offset=cb + 40960)
    hst = nc.alloc_sbuf_tensor_at("hst", [128, 2048], BF16, offset=cb + 40960)
    sqst = nc.alloc_sbuf_tensor_at("sqst", [128, 2048], F32, offset=cb + 49152)
    sqstb = nc.alloc_sbuf_tensor_at("sqstb", [128, 2048], BF16, offset=cb + 49152)
    rbf = nc.alloc_sbuf_tensor_at("rbf", [1, 1024], BF16, offset=cb + 63488)
    qkvst = nc.alloc_sbuf_tensor_at("qkvst", [128, 2048], BF16, offset=cb + 49152)
    tmp32 = nc.alloc_sbuf_tensor_at("tmp32", [128, 2048], F32, offset=cb + 57344)

    rowb = nc.alloc_sbuf_tensor_at("rowb", [1, 3072], F32, offset=cb + 49152)
    trow = nc.alloc_sbuf_tensor("trow", [2, 3072], BF16)
    b1s = nc.alloc_sbuf_tensor("b1s", [2, 256], BF16)
    b3s = nc.alloc_sbuf_tensor("b3s", [2, 256], BF16)
    b2s = nc.alloc_sbuf_tensor("b2s", [2, 256], BF16)
    tri = nc.alloc_sbuf_tensor("tri", [128, 128], BF16)
    ones_cb = nc.alloc_sbuf_tensor("ones_cb", [128, 16], BF16)
    ones_cf = nc.alloc_sbuf_tensor("ones_cf", [128, 8], F32)
    ones_rf = nc.alloc_sbuf_tensor("ones_rf", [1, 128], F32)
    ones_rb = nc.alloc_sbuf_tensor("ones_rb", [1, 128], BF16)
    expb = nc.alloc_sbuf_tensor("expb", [128, 8], F32)
    expsc = nc.alloc_sbuf_tensor("expsc", [128, 8], F32)
    bq_s = nc.alloc_sbuf_tensor("bq_s", [128, 32], F32)
    bk_s = nc.alloc_sbuf_tensor("bk_s", [128, 32], F32)
    bv_s = nc.alloc_sbuf_tensor("bv_s", [128, 32], F32)
    bo_s = nc.alloc_sbuf_tensor("bo_s", [128, 32], F32)
    a1s = nc.alloc_sbuf_tensor("a1s", [128, 64], BF16)
    a3s = nc.alloc_sbuf_tensor("a3s", [128, 64], BF16)
    a2s = nc.alloc_sbuf_tensor("a2s", [128, NOC_H * 2], BF16)

    pb = [nc.alloc_psum_tensor(f"pb{i}", [128, 512], F32) for i in range(8)]

    s_gwx = Ct(nc.alloc_semaphore("s_gwx"))
    s_gwp = [Ct(nc.alloc_semaphore(f"s_gwp{i}")) for i in range(4)]
    s_slc = Ct(nc.alloc_semaphore("s_slc"))
    s_slp = [Ct(nc.alloc_semaphore(f"s_slp{i}")) for i in range(2)]
    s_slv = [Ct(nc.alloc_semaphore(f"s_slv{i}")) for i in range(2)]
    s_ssp = [Ct(nc.alloc_semaphore(f"s_ssp{i}")) for i in range(2)]
    s_pe = Ct(nc.alloc_semaphore("s_pe"))
    s_act = Ct(nc.alloc_semaphore("s_act"))
    s_dve = Ct(nc.alloc_semaphore("s_dve"))
    s_cc = [Ct(nc.alloc_semaphore(f"s_cc{i}")) for i in range(8)]
    s_u01 = Ct(nc.alloc_semaphore("s_u01"))
    s_pl = Ct(nc.alloc_semaphore("s_pl"))
    s_cl = [Ct(nc.alloc_semaphore(f"s_cl{i}")) for i in range(2)]

    plan = {"tensor": [], "scalar": [], "vector": [], "sync": [], "gpsimd": []}

    def em(eng, fn):
        plan[eng].append(fn)

    def wait(eng, ct, val):
        if val > 0:
            em(eng, lambda e, s=ct.h, v=val: e.wait_ge(s, v))

    em("vector", lambda e: e.memset(ones_cb[:, :], 1.0))
    em("vector", lambda e: e.memset(ones_cf[:, :], 1.0))
    em("vector", lambda e: e.memset(ones_rf[:, :], 1.0))
    em("vector", lambda e: e.memset(ones_rb[:, :], 1.0))
    em("vector", lambda e: e.memset(rbf[:, :], 0.0))
    em("vector", lambda e: e.memset(expb[:, :], EXP_BIAS))
    em("vector", lambda e: e.memset(expsc[:, :], EXP_SCALE).then_inc(s_dve.h, 1))
    dve_consts = s_dve.inc()

    for src, dst in ((tri_d, tri), (bq_f, bq_s), (bk_f, bk_s),
                     (bv_f, bv_s), (bo_f, bo_s)):
        em("sync", lambda e, a=src, b=dst:
           e.dma_start(out=b[:, :], in_=a[:, :]).then_inc(s_slc.h, 16))
        s_slc.inc(16)
    sl_consts = s_slc.n

    for i0 in range(4):
        def f(e, i0=i0):
            src = x_fm[i0 * 1024:(i0 + 1) * 1024, :].rearrange(
                "(i p) t -> p i t", p=128)
            dst = bufA[:, i0 * 8192:(i0 + 1) * 8192].rearrange(
                "p (i t) -> p i t", i=8)
            e.dma_start(out=dst, in_=src).then_inc(s_gwx.h, 16)
        em("gpsimd", f)
        s_gwx.inc(16)
    for src, dst in ((a1_t, a1s), (a3_t, a3s), (a2_t, a2s)):
        em("gpsimd", lambda e, a=src, b=dst:
           e.dma_start(out=b[:, :], in_=a[:, :]).then_inc(s_gwx.h, 16))
        s_gwx.inc(16)
    gw_x = s_gwx.n

    v_order = [oc for oc in range(32) if oc % 4 < 2] + \
              [oc for oc in range(32) if oc % 4 >= 2]
    p1_mats = [(wk_t, bk_s, list(range(32))),
               (wq_t, bq_s, list(range(32))),
               (wv_t, bv_s, v_order)]

    def p1_store_target(m, oc):
        if m == 0:
            return k_l, oc * 128
        if m == 1:
            return q_l, oc * 128
        half = (oc % 4) // 2
        return v_l_h[half], (oc // 4) * 256 + (oc % 4 % 2) * 128

    pe_p1, act_p1, ss_p1, gw_p1 = {}, {}, {}, {}

    def p1_load(it):
        m, pos = divmod(it, 32)
        oc = p1_mats[m][2][pos]
        wt = p1_mats[m][0]
        if it - 4 >= 0:
            wait("gpsimd", s_pe, pe_p1[it - 4])
        em("gpsimd", lambda e, w=wt, oc=oc, par=it % 4:
           e.dma_start(out=wbuf[:, par * 4096:(par + 1) * 4096],
                       in_=w[oc * 128:(oc + 1) * 128, :]).then_inc(s_gwp[par].h, 16))
        gw_p1[it] = s_gwp[it % 4].inc(16)

    def p1_collective(src, dsto, ci):
        wait("gpsimd", s_ssp[0], s_ssp[0].n)
        wait("gpsimd", s_ssp[1], s_ssp[1].n)
        def fcc(e, a=src, b=dsto, ci=ci):
            with allow_ici_all_to_all():
                e.collective_compute(
                    "AllToAll", ALU.bypass,
                    replica_groups=[[0, 1, 2, 3, 4, 5, 6, 7]],
                    ins=[a.ap()], outs=[b.ap()]).then_inc(s_cc[ci].h, 1)
        em("gpsimd", fcc)
        s_cc[ci].inc()

    for _pre in range(4):
        p1_load(_pre)
    for it in range(96):
        m, pos = divmod(it, 32)
        wt, bias, order = p1_mats[m]
        oc = order[pos]
        dst, drow = p1_store_target(m, oc)
        par4 = it % 4
        par = it % 2
        b0, b1 = 2 * par, 2 * par + 1
        wait("tensor", s_gwp[par4], gw_p1[it])
        if it == 0:
            wait("tensor", s_gwx, gw_x)
        if it - 2 >= 0:
            wait("tensor", s_dve, act_p1[it - 2])
        for i in range(32):
            st, sp = (i == 0), (i == 31)
            em("tensor", lambda e, b=b0, p=par4, i=i, st=st, sp=sp:
               e.matmul(pb[b][:, :],
                        wbuf[:, p * 4096 + i * 128:p * 4096 + (i + 1) * 128],
                        bufA[:, i * 1024:i * 1024 + 512],
                        start=st, stop=sp, skip_group_check=True))
            if not sp:
                em("tensor", lambda e, b=b1, p=par4, i=i, st=st:
                   e.matmul(pb[b][:, :],
                            wbuf[:, p * 4096 + i * 128:p * 4096 + (i + 1) * 128],
                            bufA[:, i * 1024 + 512:(i + 1) * 1024],
                            start=st, stop=False, skip_group_check=True))
            else:
                em("tensor", lambda e, b=b1, p=par4, i=i:
                   e.matmul(pb[b][:, :],
                            wbuf[:, p * 4096 + i * 128:p * 4096 + (i + 1) * 128],
                            bufA[:, i * 1024 + 512:(i + 1) * 1024],
                            start=False, stop=True,
                            skip_group_check=True).then_inc(s_pe.h, 1))
        pe_p1[it] = s_pe.inc()

        wait("vector", s_pe, pe_p1[it])
        if it == 0:
            wait("vector", s_slc, sl_consts)
        if it - 2 >= 0:
            wait("vector", s_ssp[it % 2], ss_p1[it - 2])
        em("vector", lambda e, b=b0, par=par, bias=bias, oc=oc:
           e.tensor_scalar_add(out=qkvst[:, par * 1024:par * 1024 + 512],
                               in0=pb[b][:, :], scalar1=bias[:, oc:oc + 1]))
        em("vector", lambda e, b=b1, par=par, bias=bias, oc=oc:
           e.tensor_scalar_add(out=qkvst[:, par * 1024 + 512:(par + 1) * 1024],
                               in0=pb[b][:, :],
                               scalar1=bias[:, oc:oc + 1]).then_inc(s_dve.h, 1))
        act_p1[it] = s_dve.inc()

        wait("sync", s_dve, act_p1[it])
        em("sync", lambda e, d=dst, drow=drow, par=par:
           e.dma_start(out=d[drow:drow + 128, :],
                       in_=qkvst[:, par * 1024:(par + 1) * 1024]).then_inc(s_ssp[par].h, 16))
        ss_p1[it] = s_ssp[it % 2].inc(16)

        if it + 4 < 96:
            p1_load(it + 4)
        if m == 0 and pos == 31:
            p1_collective(k_l, k_o, 0)
        elif m == 1 and pos == 31:
            p1_collective(q_l, q_o, 1)
        elif m == 2 and pos == 15:
            p1_collective(v_l_h[0], v_o_h[0], 2)
        elif m == 2 and pos == 31:
            p1_collective(v_l_h[1], v_o_h[1], 3)
    p1_pe_end = s_pe.n
    p1_ss_end = (s_ssp[0].n, s_ssp[1].n)

    units = [(h_l, b) for h_l in range(4) for b in range(2)]
    sl_qk, sl_v, pe_unit = {}, {}, {}
    pv_item, exp_glob_list = {}, []
    exp_item_last, dve_mask_item = {}, {}
    dve_s, dve_rb, dve_ctx, ss_ctx = {}, {}, {}, {}

    def u_bufs(u):
        if u < 2:
            base = u * 12288
            return (bufB, base, bufB, base + 4096, bufB, base + 8192)
        par = u % 2
        return (qs, par * 4096, ks, par * 4096, vt, par * 4096)

    def p2_load(u):
        h_l, b = units[u]
        par = u % 2
        QT, qb, KT, kb, VT, vb = u_bufs(u)
        if u < 2:
            ldq, sem = "scalar", s_u01
            wait(ldq, s_cc[0], 1)
            wait(ldq, s_cc[1], 1)
        else:
            ldq = "sync"
            if u in (2, 3):
                wait(ldq, s_pe, p1_pe_end)
            else:
                wait(ldq, s_pe, pv_item[(u - 2) * 8 + 7])
        for qc in range(8):
            cc = 8 * b + qc
            r, of = _r(cc), _off(cc)
            em(ldq, lambda e, r=r, of=of, h_l=h_l, QT=QT, qb=qb, qc=qc,
               sm=(s_u01 if u < 2 else s_slp[par]).h:
               e.dma_start(out=QT[:, qb + qc * 512:qb + (qc + 1) * 512],
                           in_=q_o[512 * r + 128 * h_l:512 * r + 128 * (h_l + 1),
                                   of:of + 512]).then_inc(sm, 16))
            (s_u01 if u < 2 else s_slp[par]).inc(16)
            em(ldq, lambda e, r=r, of=of, h_l=h_l, KT=KT, kb=kb, qc=qc,
               sm=(s_u01 if u < 2 else s_slp[par]).h:
               e.dma_start(out=KT[:, kb + qc * 512:kb + (qc + 1) * 512],
                           in_=k_o[512 * r + 128 * h_l:512 * r + 128 * (h_l + 1),
                                   of:of + 512]).then_inc(sm, 16))
            (s_u01 if u < 2 else s_slp[par]).inc(16)
        sl_qk[u] = ((s_u01 if u < 2 else s_slp[par]),
                    (s_u01 if u < 2 else s_slp[par]).n)
        ldv = "scalar"
        if u < 2:
            wait(ldv, s_cc[2], 1)
        else:
            if u in (2, 3):
                wait(ldv, s_pe, p1_pe_end)
            else:
                wait(ldv, s_pe, pv_item[(u - 2) * 8 + 7])
            wait(ldv, s_cc[2 + h_l // 2], 1)
        vhalf = v_o_h[h_l // 2]
        hl2 = h_l % 2
        for kt in range(32):
            cc = 8 * b + kt // 4
            w = kt % 4
            r, of = _r(cc), _off(cc)
            em(ldv, lambda e, r=r, of=of, w=w, hl2=hl2, vhalf=vhalf, VT=VT, vb=vb, kt=kt,
               sm=(s_u01 if u < 2 else s_slv[par]).h:
               e.dma_start(out=VT[:, vb + kt * 128:vb + (kt + 1) * 128],
                           in_=vhalf[256 * r + 128 * hl2:256 * r + 128 * (hl2 + 1),
                                     of + 128 * w:of + 128 * (w + 1)],
                           transpose=True).then_inc(sm, 16))
            (s_u01 if u < 2 else s_slv[par]).inc(16)
        sl_v[u] = ((s_u01 if u < 2 else s_slv[par]),
                   (s_u01 if u < 2 else s_slv[par]).n)

    p2_load(0)
    p2_load(1)
    for _u in (0, 1):
        sl_qk[_u] = (s_u01, s_u01.n)
        sl_v[_u] = (s_u01, s_u01.n)
    ktg = 0
    pe_S_item, pe_rb_item = {}, {}

    def it_of(k):
        u, qc = divmod(k, 8)
        h_l, b = units[u]
        return u, qc, h_l, b, u % 2, k % 2, 4 * (qc + 1), (k % 2) * 16384

    def emit_sT_exp(k):
        nonlocal ktg
        u, qc, h_l, b, par, kp, nkt, half = it_of(k)
        if qc == 0:
            wait("tensor", sl_qk[u][0], sl_qk[u][1])
        for kt in range(nkt):
            sb = 4 + (ktg % 2)
            if ktg - 2 >= 0:
                wait("tensor", s_act, exp_glob_list[ktg - 2])
            em("tensor", lambda e, sb=sb, par=par, kt=kt, qc=qc:
               e.matmul(pb[sb][:, :],
                        ks[:, par * 4096 + kt * 128:par * 4096 + (kt + 1) * 128],
                        qs[:, par * 4096 + qc * 512:par * 4096 + (qc + 1) * 512],
                        start=True, stop=True).then_inc(s_pe.h, 1))
            pe_sT = s_pe.inc()
            wait("scalar", s_pe, pe_sT)
            if kt == 0:
                wait("scalar", s_pe, pv_item.get(k - 2, 0))
                if k == 0:
                    wait("scalar", s_dve, dve_consts)
            em("scalar", lambda e, sb=sb, half=half, kt=kt:
               e.activation(out=bufA[:, half + kt * 512:half + (kt + 1) * 512],
                            in_=pb[sb][:, :], func=AF.Exp,
                            bias=expb[:, 0:1],
                            scale=expsc[:, 0:1]).then_inc(s_act.h, 1))
            exp_glob_list.append(s_act.inc())
            ktg += 1
        exp_item_last[k] = s_act.n

    def emit_masks(k):
        u, qc, h_l, b, par, kp, nkt, half = it_of(k)
        if k == 0:
            wait("vector", s_slc, sl_consts)
        wait("vector", s_act, exp_item_last[k])
        for d in range(4):
            base = half + (nkt - 4 + d) * 512
            if d > 0:
                em("vector", lambda e, base=base, d=d:
                   e.memset(bufA[:, base:base + 128 * d], 0.0))
            last = (d == 3)
            em("vector", lambda e, base=base, d=d, last=last:
               (e.tensor_mul(out=bufA[:, base + 128 * d:base + 128 * (d + 1)],
                             in0=bufA[:, base + 128 * d:base + 128 * (d + 1)],
                             in1=tri[:, :]).then_inc(s_dve.h, 1) if last else
                e.tensor_mul(out=bufA[:, base + 128 * d:base + 128 * (d + 1)],
                             in0=bufA[:, base + 128 * d:base + 128 * (d + 1)],
                             in1=tri[:, :])))
        dve_mask_item[k] = s_dve.inc()

    def emit_sum(j):
        u, qc, h_l, b, par, kp, nkt, half = it_of(j)
        sbank = 6 + kp
        wait("tensor", s_act, exp_item_last[j])
        wait("tensor", s_dve, dve_mask_item[j])
        if j == 0:
            wait("tensor", s_dve, dve_consts)
        if j - 2 >= 0:
            wait("tensor", s_dve, dve_s[j - 2])
        for kt in range(nkt):
            st, sp = (kt == 0), (kt == nkt - 1)
            if not sp:
                em("tensor", lambda e, sbank=sbank, half=half, kt=kt, st=st:
                   e.matmul(pb[sbank][0:1, :], ones_cb[:, 0:1],
                            bufA[:, half + kt * 512:half + (kt + 1) * 512],
                            start=st, stop=False))
            else:
                em("tensor", lambda e, sbank=sbank, half=half, kt=kt, st=st:
                   e.matmul(pb[sbank][0:1, :], ones_cb[:, 0:1],
                            bufA[:, half + kt * 512:half + (kt + 1) * 512],
                            start=st, stop=True).then_inc(s_pe.h, 1))
        pe_S_item[j] = s_pe.inc()
        wait("vector", s_pe, pe_S_item[j])
        em("vector", lambda e, sbank=sbank, kp=kp:
           e.tensor_copy(out=p2row[0:1, kp * 512:(kp + 1) * 512],
                         in_=pb[sbank][0:1, :]).then_inc(s_dve.h, 1))
        dve_scp = s_dve.inc()
        wait("vector", s_dve, dve_scp)
        em("vector", lambda e, kp=kp:
           e.reciprocal(out=p2row[0:1, 1024 + kp * 512:1024 + (kp + 1) * 512],
                        in_=p2row[0:1, kp * 512:(kp + 1) * 512]).then_inc(s_dve.h, 1))
        dve_s[j] = s_dve.inc()

    def emit_rb(j):
        u, qc, h_l, b, par, kp, nkt, half = it_of(j)
        rbank = kp
        if j <= 1:
            wait("vector", s_ssp[0], p1_ss_end[0])
            wait("vector", s_ssp[1], p1_ss_end[1])
        wait("tensor", s_dve, dve_s[j])
        if j - 2 >= 0:
            wait("tensor", s_dve, dve_rb[j - 2])
        em("tensor", lambda e, rbank=rbank, kp=kp:
           e.matmul(pb[rbank][:, :], ones_rf[0:1, :],
                    p2row[0:1, 1024 + kp * 512:1024 + (kp + 1) * 512],
                    start=True, stop=True).then_inc(s_pe.h, 1))
        pe_rb_item[j] = s_pe.inc()
        wait("vector", s_pe, pe_rb_item[j])
        em("vector", lambda e, rbank=rbank, kp=kp:
           e.tensor_copy(out=rb_sb[:, kp * 512:(kp + 1) * 512],
                         in_=pb[rbank][:, :]).then_inc(s_dve.h, 1))
        dve_rb[j] = s_dve.inc()

    def emit_pv(j):
        u, qc, h_l, b, par, kp, nkt, half = it_of(j)
        cbank = 2 + kp
        QT, qb, KT, kb, VT, vb = u_bufs(u)
        wait("tensor", s_act, exp_item_last[j])
        wait("tensor", s_dve, dve_mask_item[j])
        if qc == 0:
            wait("tensor", sl_v[u][0], sl_v[u][1])
        if j - 2 >= 0:
            wait("tensor", s_dve, dve_ctx[j - 2])
        for kt in range(nkt):
            st, sp = (kt == 0), (kt == nkt - 1)
            if not sp:
                em("tensor", lambda e, cbank=cbank, VT=VT, vb=vb, half=half, kt=kt, st=st:
                   e.matmul(pb[cbank][:, :],
                            VT[:, vb + kt * 128:vb + (kt + 1) * 128],
                            bufA[:, half + kt * 512:half + (kt + 1) * 512],
                            start=st, stop=False, skip_group_check=True))
            else:
                em("tensor", lambda e, cbank=cbank, VT=VT, vb=vb, half=half, kt=kt, st=st:
                   e.matmul(pb[cbank][:, :],
                            VT[:, vb + kt * 128:vb + (kt + 1) * 128],
                            bufA[:, half + kt * 512:half + (kt + 1) * 512],
                            start=st, stop=True,
                            skip_group_check=True).then_inc(s_pe.h, 1))
        pv_item[j] = s_pe.inc()

    def emit_ctx_evac(j):
        u, qc, h_l, b, par, kp, nkt, half = it_of(j)
        cbank = 2 + kp
        cc = 8 * b + qc
        wait("vector", s_pe, pv_item[j])
        wait("vector", s_dve, dve_rb[j])
        if j - 2 >= 0:
            wait("vector", s_ssp[kp], ss_ctx[j - 2])
        em("vector", lambda e, cbank=cbank, kp=kp:
           e.tensor_tensor(out=ctxst[:, kp * 512:(kp + 1) * 512],
                           in0=pb[cbank][:, :],
                           in1=rb_sb[:, kp * 512:(kp + 1) * 512],
                           op=ALU.mult).then_inc(s_dve.h, 1))
        dve_ctx[j] = s_dve.inc()
        wait("sync", s_dve, dve_ctx[j])
        r, of = _r(cc), _off(cc)
        cq = cx_l_q[h_l]
        em("sync", lambda e, r=r, of=of, cq=cq, kp=kp:
           e.dma_start(out=cq[128 * r:128 * (r + 1), of:of + 512],
                       in_=ctxst[:, kp * 512:(kp + 1) * 512]).then_inc(s_ssp[kp].h, 16))
        ss_ctx[j] = s_ssp[kp].inc(16)

    def consumer_passes(k):
        passes = []
        if k - 1 >= 0:
            j = k - 1
            u, qc, h_l, b, par, kp, nkt, half = it_of(j)
            sbank = 6 + kp
            cbank = 2 + kp

            def sum_gate(e=None, j=j):
                wait("tensor", s_act, exp_item_last[j])
                wait("tensor", s_dve, dve_mask_item[j])
                if j == 0:
                    wait("tensor", s_dve, dve_consts)
                if j - 2 >= 0:
                    wait("tensor", s_dve, dve_s[j - 2])
            passes.append(sum_gate)
            for kt in range(nkt):
                st, sp = (kt == 0), (kt == nkt - 1)
                def f(e=None, sbank=sbank, half=half, kt=kt, st=st, sp=sp, j=j):
                    if not sp:
                        em("tensor", lambda e, sbank=sbank, half=half, kt=kt, st=st:
                           e.matmul(pb[sbank][0:1, :], ones_cb[:, 0:1],
                                    bufA[:, half + kt * 512:half + (kt + 1) * 512],
                                    start=st, stop=False, skip_group_check=True))
                    else:
                        em("tensor", lambda e, sbank=sbank, half=half, kt=kt, st=st:
                           e.matmul(pb[sbank][0:1, :], ones_cb[:, 0:1],
                                    bufA[:, half + kt * 512:half + (kt + 1) * 512],
                                    start=st, stop=True,
                                    skip_group_check=True).then_inc(s_pe.h, 1))
                        pe_S_item[j] = s_pe.inc()
                        wait("vector", s_pe, pe_S_item[j])
                        em("vector", lambda e, sbank=sbank, kp=j % 2:
                           e.tensor_copy(out=p2row[0:1, (j % 2) * 512:(j % 2 + 1) * 512],
                                         in_=pb[sbank][0:1, :]).then_inc(s_dve.h, 1))
                        dve_scp = s_dve.inc()
                        wait("vector", s_dve, dve_scp)
                        em("vector", lambda e, kp=j % 2:
                           e.reciprocal(out=p2row[0:1, 1024 + kp * 512:1024 + (kp + 1) * 512],
                                        in_=p2row[0:1, kp * 512:(kp + 1) * 512]).then_inc(s_dve.h, 1))
                        dve_s[j] = s_dve.inc()
                passes.append(f)
            def pv_gate(e=None, j=j, par=par, qc=qc, u=u):
                wait("tensor", s_act, exp_item_last[j])
                wait("tensor", s_dve, dve_mask_item[j])
                if qc == 0:
                    wait("tensor", sl_v[u][0], sl_v[u][1])
                if j - 2 >= 0:
                    wait("tensor", s_dve, dve_ctx[j - 2])
            passes.append(pv_gate)
            for kt in range(nkt):
                st, sp = (kt == 0), (kt == nkt - 1)
                VTj, vbj = u_bufs(u)[4], u_bufs(u)[5]
                def f(e=None, cbank=cbank, half=half, kt=kt, st=st, sp=sp, j=j,
                      VT=VTj, vb=vbj):
                    if not sp:
                        em("tensor", lambda e, cbank=cbank, VT=VT, vb=vb, half=half, kt=kt, st=st:
                           e.matmul(pb[cbank][:, :],
                                    VT[:, vb + kt * 128:vb + (kt + 1) * 128],
                                    bufA[:, half + kt * 512:half + (kt + 1) * 512],
                                    start=st, stop=False, skip_group_check=True))
                    else:
                        em("tensor", lambda e, cbank=cbank, VT=VT, vb=vb, half=half, kt=kt, st=st:
                           e.matmul(pb[cbank][:, :],
                                    VT[:, vb + kt * 128:vb + (kt + 1) * 128],
                                    bufA[:, half + kt * 512:half + (kt + 1) * 512],
                                    start=st, stop=True,
                                    skip_group_check=True).then_inc(s_pe.h, 1))
                        pv_item[j] = s_pe.inc()
                passes.append(f)
        if k - 2 >= 0:
            passes.append(lambda e=None, j=k - 2: emit_rb(j))
        return passes

    def emit_sT_exp_interleaved(k):
        nonlocal ktg
        u, qc, h_l, b, par, kp, nkt, half = it_of(k)
        queue = consumer_passes(k)
        qi = 0
        if qc == 0:
            wait("tensor", sl_qk[u][0], sl_qk[u][1])
        for kt in range(nkt):
            sb = 4 + (ktg % 2)
            if ktg - 2 >= 0:
                wait("tensor", s_act, exp_glob_list[ktg - 2])
            QT, qb, KT, kb, VT, vb = u_bufs(u)
            em("tensor", lambda e, sb=sb, kt=kt, qc=qc, QT=QT, qb=qb, KT=KT, kb=kb:
               e.matmul(pb[sb][:, :],
                        KT[:, kb + kt * 128:kb + (kt + 1) * 128],
                        QT[:, qb + qc * 512:qb + (qc + 1) * 512],
                        start=True, stop=True,
                        skip_group_check=True).then_inc(s_pe.h, 1))
            pe_sT = s_pe.inc()
            wait("scalar", s_pe, pe_sT)
            if kt == 0:
                wait("scalar", s_pe, pv_item.get(k - 2, 0))
                if k == 0:
                    wait("scalar", s_dve, dve_consts)
            em("scalar", lambda e, sb=sb, half=half, kt=kt:
               e.activation(out=bufA[:, half + kt * 512:half + (kt + 1) * 512],
                            in_=pb[sb][:, :], func=AF.Exp,
                            bias=expb[:, 0:1],
                            scale=expsc[:, 0:1]).then_inc(s_act.h, 1))
            exp_glob_list.append(s_act.inc())
            ktg += 1
            want = ((kt + 1) * len(queue)) // nkt
            while qi < want:
                queue[qi]()
                qi += 1
        while qi < len(queue):
            queue[qi]()
            qi += 1
        exp_item_last[k] = s_act.n

    for k in range(64):
        emit_sT_exp_interleaved(k)
        if k - 2 >= 0:
            emit_ctx_evac(k - 2)
        emit_masks(k)
        if k % 8 == 0 and k >= 8:
            u_prev = k // 8 - 1
            if u_prev + 2 < 8:
                p2_load(u_prev + 2)
        if k in (17, 33, 49):
            qn = (k - 17) // 16
            wait("gpsimd", s_ssp[0], ss_ctx[k - 3])
            wait("gpsimd", s_ssp[1], ss_ctx[k - 2])
            def fcc_ctxq(e, qn=qn):
                with allow_ici_all_to_all():
                    e.collective_compute("AllToAll", ALU.bypass,
                                         replica_groups=[[0, 1, 2, 3, 4, 5, 6, 7]],
                                         ins=[cx_l_q[qn].ap()],
                                         outs=[cx_o_q[qn].ap()]).then_inc(s_cc[4 + qn].h, 1)
            em("gpsimd", fcc_ctxq)
            s_cc[4 + qn].inc()
    emit_sum(63)
    emit_rb(62)
    emit_pv(63)
    emit_ctx_evac(62)
    emit_rb(63)
    emit_ctx_evac(63)

    wait("gpsimd", s_ssp[0], s_ssp[0].n)
    wait("gpsimd", s_ssp[1], s_ssp[1].n)
    def fcc_ctx3(e):
        with allow_ici_all_to_all():
            e.collective_compute("AllToAll", ALU.bypass,
                                 replica_groups=[[0, 1, 2, 3, 4, 5, 6, 7]],
                                 ins=[cx_l_q[3].ap()],
                                 outs=[cx_o_q[3].ap()]).then_inc(s_cc[7].h, 1)
    em("gpsimd", fcc_ctx3)
    s_cc[7].inc()
    p2_pe_end = s_pe.n
    p2_dve_end = dve_ctx[63]

    qsem = [s_slc, s_slv[0], s_cl[0], s_cl[1]]
    ctx_ld_q = []
    wait("sync", s_pe, pv_item[15])
    for qn in range(4):
        wait("sync", s_cc[4 + qn], 1)
        for oc in range(qn, 32, 4):
            m_ = oc // 4
            em("sync", lambda e, oc=oc, m_=m_, qn=qn:
               e.dma_start(out=bufB[:, oc * 1024:(oc + 1) * 1024],
                           in_=cx_o_q[qn][m_ * 128:(m_ + 1) * 128, :]).then_inc(qsem[qn].h, 16))
            qsem[qn].inc(16)
        ctx_ld_q.append(qsem[qn].n)

    gw_wo, sl_xf = {}, {}
    pe_main3, pe_st3, dve_stt3, act_sq3, act_cast3 = {}, {}, {}, {}, {}

    def p3_wo_load(oc):
        if oc - 2 >= 0:
            wait("gpsimd", s_pe, pe_main3[(oc - 2, 1)])
        else:
            wait("gpsimd", s_pe, p2_pe_end)
        em("gpsimd", lambda e, oc=oc, par=oc % 2:
           e.dma_start(out=wbuf[:, par * 4096:(par + 1) * 4096],
                       in_=wo_t[oc * 128:(oc + 1) * 128, :]).then_inc(s_gwp[oc % 2].h, 16))
        gw_wo[oc] = s_gwp[oc % 2].inc(16)

    def p3_xf_load(oc):
        if oc - 2 >= 0:
            wait("scalar", s_dve, dve_stt3[(oc - 2, 1)])
        else:
            wait("scalar", s_pe, p2_pe_end)
        em("scalar", lambda e, oc=oc, par=oc % 2:
           e.dma_start(out=xf[:, par * 1024:(par + 1) * 1024],
                       in_=x_fm[oc * 128:(oc + 1) * 128, :]).then_inc(s_slp[oc % 2].h, 16))
        sl_xf[oc] = s_slp[oc % 2].inc(16)

    p3_wo_load(0)
    p3_wo_load(1)
    p3_xf_load(0)
    p3_xf_load(1)
    kseq = [i for qn in range(4) for i in range(qn, 32, 4)]
    for oc in range(32):
        par = oc % 2
        for t in range(2):
            br = (oc * 2 + t) % 4
            wait("tensor", s_gwp[oc % 2], gw_wo[oc])
            if oc == 0 and t == 0:
                wait("tensor", s_slc, ctx_ld_q[0])
                wait("tensor", s_dve, p2_dve_end)
            if (oc * 2 + t) - 4 >= 0:
                prev = (oc * 2 + t - 4)
                wait("tensor", s_dve, dve_stt3[(prev // 2, prev % 2)])
            for ki, i in enumerate(kseq):
                st, sp = (ki == 0), (ki == 31)
                if oc == 0 and t == 0 and ki in (8, 16, 24):
                    wait("tensor", qsem[ki // 8], ctx_ld_q[ki // 8])
                if not sp:
                    em("tensor", lambda e, br=br, par=par, i=i, t=t, st=st:
                       e.matmul(pb[br][:, :],
                                wbuf[:, par * 4096 + i * 128:par * 4096 + (i + 1) * 128],
                                bufB[:, i * 1024 + t * 512:i * 1024 + (t + 1) * 512],
                                start=st, stop=False, skip_group_check=True))
                else:
                    em("tensor", lambda e, br=br, par=par, i=i, t=t:
                       e.matmul(pb[br][:, :],
                                wbuf[:, par * 4096 + i * 128:par * 4096 + (i + 1) * 128],
                                bufB[:, i * 1024 + t * 512:i * 1024 + (t + 1) * 512],
                                start=False, stop=True,
                                skip_group_check=True).then_inc(s_pe.h, 1))
            pe_main3[(oc, t)] = s_pe.inc()
            wait("vector", s_pe, pe_main3[(oc, t)])
            wait("vector", s_slp[oc % 2], sl_xf[oc])
            if oc - 2 >= 0:
                wait("vector", s_pe, pe_st3.get((oc - 2, t), 0))
                wait("vector", s_act, act_cast3.get((oc - 2, t), 0))
            em("vector", lambda e, br=br, par=par, t=t, oc=oc:
               e.scalar_tensor_tensor(out=yst[:, par * 1024 + t * 512:par * 1024 + (t + 1) * 512],
                                      in0=pb[br][:, :], scalar=bo_s[:, oc:oc + 1],
                                      in1=xf[:, par * 1024 + t * 512:par * 1024 + (t + 1) * 512],
                                      op0=ALU.add, op1=ALU.add).then_inc(s_dve.h, 1))
            dve_stt3[(oc, t)] = s_dve.inc()
            wait("scalar", s_dve, dve_stt3[(oc, t)])
            if oc - 2 >= 0:
                wait("scalar", s_pe, pe_st3.get((oc - 2, t), 0))
            em("scalar", lambda e, par=par, t=t:
               e.activation(out=sqstb[:, par * 1024 + t * 512:par * 1024 + (t + 1) * 512],
                            in_=yst[:, par * 1024 + t * 512:par * 1024 + (t + 1) * 512],
                            func=AF.Square).then_inc(s_act.h, 1))
            act_sq3[(oc, t)] = s_act.inc()
            em("scalar", lambda e, par=par, t=t, oc=oc:
               e.activation(out=bufA[:, oc * 1024 + t * 512:oc * 1024 + (t + 1) * 512],
                            in_=yst[:, par * 1024 + t * 512:par * 1024 + (t + 1) * 512],
                            func=AF.Copy).then_inc(s_act.h, 1))
            act_cast3[(oc, t)] = s_act.inc()
        for so in ([oc - 1] if oc >= 1 else []) + ([31] if oc == 31 else []):
            spar = so % 2
            for t in range(2):
                wait("tensor", s_act, act_cast3[(so, t)])
                em("tensor", lambda e, t=t, so=so:
                   e.matmul(pb[4 + t][0:1, :], ones_cb[:, 0:1],
                            bufA[:, so * 1024 + t * 512:so * 1024 + (t + 1) * 512],
                            start=(so == 0), stop=(so == 31),
                            skip_group_check=True))
                if t == 1:
                    em("tensor", lambda e, spar=spar, t=t, so=so:
                       e.matmul(pb[6 + t][0:1, :], ones_cb[:, 0:1],
                                sqstb[:, spar * 1024 + t * 512:spar * 1024 + (t + 1) * 512],
                                start=(so == 0), stop=(so == 31),
                                skip_group_check=True).then_inc(s_pe.h, 1))
                else:
                    em("tensor", lambda e, spar=spar, t=t, so=so:
                       e.matmul(pb[6 + t][0:1, :], ones_cb[:, 0:1],
                                sqstb[:, spar * 1024 + t * 512:spar * 1024 + (t + 1) * 512],
                                start=(so == 0), stop=(so == 31),
                                skip_group_check=True))
            pe_st3[(so, 0)] = pe_st3[(so, 1)] = s_pe.inc()
        if oc + 2 < 32:
            p3_wo_load(oc + 2)
            p3_xf_load(oc + 2)

    def ln_finalize(pe_stats_done):
        wait("vector", s_pe, pe_stats_done)
        for i, bank in enumerate((4, 5, 6, 7)):
            last = (i == 3)
            em("vector", lambda e, bank=bank, i=i, last=last:
               (e.tensor_copy(out=rowb[0:1, i * 512:(i + 1) * 512],
                              in_=pb[bank][0:1, :]).then_inc(s_dve.h, 1) if last
                else e.tensor_copy(out=rowb[0:1, i * 512:(i + 1) * 512],
                                   in_=pb[bank][0:1, :])))
        c1 = s_dve.inc()
        wait("vector", s_dve, c1)
        em("vector", lambda e:
           e.tensor_scalar_mul(out=rowb[0:1, 0:1024], in0=rowb[0:1, 0:1024],
                               scalar1=INV_D))
        em("vector", lambda e:
           e.tensor_scalar_mul(out=rowb[0:1, 1024:2048], in0=rowb[0:1, 1024:2048],
                               scalar1=INV_D).then_inc(s_dve.h, 1))
        c2 = s_dve.inc()
        wait("vector", s_dve, c2)
        em("vector", lambda e:
           e.tensor_mul(out=rowb[0:1, 2048:3072], in0=rowb[0:1, 0:1024],
                        in1=rowb[0:1, 0:1024]).then_inc(s_dve.h, 1))
        c3 = s_dve.inc()
        wait("vector", s_dve, c3)
        em("vector", lambda e:
           e.tensor_sub(out=rowb[0:1, 1024:2048], in0=rowb[0:1, 1024:2048],
                        in1=rowb[0:1, 2048:3072]).then_inc(s_dve.h, 1))
        c4 = s_dve.inc()
        wait("vector", s_dve, c4)
        em("vector", lambda e:
           e.tensor_scalar_add(out=rowb[0:1, 1024:2048], in0=rowb[0:1, 1024:2048],
                               scalar1=LN_EPS).then_inc(s_dve.h, 1))
        c5 = s_dve.inc()
        wait("scalar", s_dve, c5)
        em("scalar", lambda e:
           e.activation(out=rowb[0:1, 2048:3072], in_=rowb[0:1, 1024:2048],
                        func=AF.Sqrt).then_inc(s_act.h, 1))
        c6 = s_act.inc()
        wait("vector", s_act, c6)
        em("vector", lambda e:
           e.reciprocal(out=rowb[0:1, 1024:2048],
                        in_=rowb[0:1, 2048:3072]).then_inc(s_dve.h, 1))
        c7 = s_dve.inc()
        wait("tensor", s_dve, c7)
        for t in range(2):
            em("tensor", lambda e, t=t:
               e.matmul(pb[t][:, :], ones_rf[0:1, :], rowb[0:1, t * 512:(t + 1) * 512],
                        start=True, stop=True,
                        skip_group_check=True).then_inc(s_pe.h, 1))
            s_pe.inc()
            em("tensor", lambda e, t=t:
               e.matmul(pb[2 + t][:, :], ones_rf[0:1, :],
                        rowb[0:1, 1024 + t * 512:1024 + (t + 1) * 512],
                        start=True, stop=True,
                        skip_group_check=True).then_inc(s_pe.h, 1))
            s_pe.inc()
        c8 = s_pe.n
        wait("vector", s_pe, c8)
        for t in range(2):
            em("vector", lambda e, t=t:
               e.tensor_copy(out=mbuf[:, t * 512:(t + 1) * 512], in_=pb[t][:, :]))
            last = (t == 1)
            em("vector", lambda e, t=t, last=last:
               (e.tensor_copy(out=mbuf[:, 1024 + t * 512:1024 + (t + 1) * 512],
                              in_=pb[2 + t][:, :]).then_inc(s_dve.h, 1) if last
                else e.tensor_copy(out=mbuf[:, 1024 + t * 512:1024 + (t + 1) * 512],
                                   in_=pb[2 + t][:, :])))
        return s_dve.inc()

    mbuf_done1 = ln_finalize(pe_st3[(31, 0)])

    dve_sub1, pl_mul1 = {}, {}
    for oc in range(32):
        if oc == 0:
            wait("vector", s_dve, mbuf_done1)
        if oc - 2 >= 0:
            wait("vector", s_pl, pl_mul1[oc - 2])
        em("vector", lambda e, oc=oc, par=oc % 2:
           e.tensor_sub(out=tmp32[:, par * 1024:(par + 1) * 1024],
                        in0=bufA[:, oc * 1024:(oc + 1) * 1024],
                        in1=mbuf[:, 0:1024]).then_inc(s_dve.h, 1))
        dve_sub1[oc] = s_dve.inc()
        if oc == 0:
            wait("gpsimd", s_dve, mbuf_done1)
        wait("gpsimd", s_dve, dve_sub1[oc])
        em("gpsimd", lambda e, oc=oc, par=oc % 2:
           e.tensor_mul(out=bufA[:, oc * 1024:(oc + 1) * 1024],
                        in0=tmp32[:, par * 1024:(par + 1) * 1024],
                        in1=mbuf[:, 1024:2048]).then_inc(s_pl.h, 1))
        pl_mul1[oc] = s_pl.inc()
    y1_done_pl = s_pl.n

    Y4 = 22528

    for i in range(32):
        wait("tensor", s_pl, pl_mul1[i])
        if i == 0:
            wait("tensor", s_dve, mbuf_done1)
        for (src_a, bank0) in ((a1s, 4), (a3s, 6)):
            for t in range(2):
                st, sp = (i == 0), (i == 31)
                if not sp:
                    em("tensor", lambda e, a=src_a, bank=bank0 + t, i=i, t=t, st=st:
                       e.matmul(pb[bank][0:2, :], a[:, i * 2:(i + 1) * 2],
                                bufA[:, i * 1024 + t * 512:i * 1024 + (t + 1) * 512],
                                start=st, stop=False, skip_group_check=True))
                else:
                    em("tensor", lambda e, a=src_a, bank=bank0 + t, i=i, t=t:
                       e.matmul(pb[bank][0:2, :], a[:, i * 2:(i + 1) * 2],
                                bufA[:, i * 1024 + t * 512:i * 1024 + (t + 1) * 512],
                                start=False, stop=True,
                                skip_group_check=True).then_inc(s_pe.h, 1))
    pe_t13_all = s_pe.inc(4)

    prev_comb = 0
    for ic in range(8):
        wait("vector", s_pl, pl_mul1[ic + 24])
        if prev_comb:
            wait("vector", s_dve, prev_comb)
        em("vector", lambda e, ic=ic:
           e.tensor_add(out=yst[:, 0:1024], in0=bufA[:, ic * 1024:(ic + 1) * 1024],
                        in1=bufA[:, (ic + 8) * 1024:(ic + 9) * 1024]))
        em("vector", lambda e, ic=ic:
           e.tensor_add(out=yst[:, 1024:2048],
                        in0=bufA[:, (ic + 16) * 1024:(ic + 17) * 1024],
                        in1=bufA[:, (ic + 24) * 1024:(ic + 25) * 1024]).then_inc(s_dve.h, 1))
        c = s_dve.inc()
        wait("vector", s_dve, c)
        em("vector", lambda e, ic=ic:
           e.tensor_add(out=bufB[:, Y4 + ic * 1024:Y4 + (ic + 1) * 1024],
                        in0=yst[:, 0:1024], in1=yst[:, 1024:2048]).then_inc(s_dve.h, 1))
        prev_comb = s_dve.inc()
    y4_done = s_dve.n

    wait("vector", s_pe, pe_t13_all)
    for (bank0, col0) in ((4, 0), (6, 1024)):
        for t in range(2):
            last = (bank0 == 6 and t == 1)
            em("vector", lambda e, bank=bank0 + t, col=col0 + t * 512, last=last:
               (e.tensor_scalar_mul(out=trow[0:2, col:col + 512],
                                    in0=pb[bank][0:2, :],
                                    scalar1=0.5).then_inc(s_dve.h, 1) if last else
                e.tensor_scalar_mul(out=trow[0:2, col:col + 512],
                                    in0=pb[bank][0:2, :], scalar1=0.5)))
    t13_done = s_dve.inc()

    gw_w13, pe_h, act_h, dve_g, dve_fold, pe_t2 = {}, {}, {}, {}, {}, {}

    def p4a_load(oc):
        par = oc % 2
        if oc - 2 >= 0:
            wait("gpsimd", s_pe, pe_h[oc - 2])
        else:
            wait("gpsimd", s_pe, pe_st3[(31, 0)])
        em("gpsimd", lambda e, oc=oc, par=par:
           e.dma_start(out=w13[:, par * 1024:(par + 1) * 1024],
                       in_=w1_t[oc * 128:(oc + 1) * 128, :]).then_inc(s_gwp[par].h, 16))
        s_gwp[par].inc(16)
        em("gpsimd", lambda e, oc=oc, par=par:
           e.dma_start(out=w13[:, 2048 + par * 1024:2048 + (par + 1) * 1024],
                       in_=w3_t[oc * 128:(oc + 1) * 128, :]).then_inc(s_gwp[par].h, 16))
        s_gwp[par].inc(16)
        em("gpsimd", lambda e, oc=oc, par=par:
           e.dma_start(out=b1s[0:2, par * 128:(par + 1) * 128],
                       in_=b1_t[oc * 2:(oc + 1) * 2, :]).then_inc(s_gwp[par].h, 16))
        s_gwp[par].inc(16)
        em("gpsimd", lambda e, oc=oc, par=par:
           e.dma_start(out=b3s[0:2, par * 128:(par + 1) * 128],
                       in_=b3_t[oc * 2:(oc + 1) * 2, :]).then_inc(s_gwp[par].h, 16))
        gw_w13[oc] = s_gwp[par].inc(16)

    p4a_load(0)
    p4a_load(1)
    for rc in range(HCH):
        for bo in range(4):
            oc = rc * 4 + bo
            par = oc % 2
            hb = [0, 1, 2, 3] if par == 0 else [2, 3, 0, 1]
            wait("tensor", s_gwp[par], gw_w13[oc])
            if oc == 0:
                wait("tensor", s_dve, y4_done)
                wait("tensor", s_dve, t13_done)
                wait("tensor", s_pl, y1_done_pl)
            if oc - 1 >= 0:
                wait("tensor", s_act, act_h[oc - 1])
            for (wcol, bnk0, bsrc, tcol) in ((par * 1024, hb[0], b1s, 0),
                                             (2048 + par * 1024, hb[2], b3s, 1024)):
                for i in range(8):
                    for t in range(2):
                        em("tensor", lambda e, wcol=wcol, bank=bnk0 + t, i=i, t=t:
                           e.matmul(pb[bank][:, :],
                                    w13[:, wcol + i * 128:wcol + (i + 1) * 128],
                                    bufB[:, Y4 + i * 1024 + t * 512:Y4 + i * 1024 + (t + 1) * 512],
                                    start=(i == 0), stop=False,
                                    skip_group_check=True))
                for t in range(2):
                    em("tensor", lambda e, bank=bnk0 + t, bsrc=bsrc, par=par, tcol=tcol, t=t:
                       e.matmul(pb[bank][:, :], bsrc[0:2, par * 128:(par + 1) * 128],
                                trow[0:2, tcol + t * 512:tcol + (t + 1) * 512],
                                start=False, stop=True,
                                skip_group_check=True).then_inc(s_pe.h, 1))
                    s_pe.inc()
            pe_h[oc] = s_pe.n
            wait("scalar", s_pe, pe_h[oc])
            if oc - 2 >= 0:
                wait("scalar", s_dve, dve_g[oc - 2])
            for t in range(2):
                em("scalar", lambda e, bank=hb[0] + t, t=t:
                   e.activation(out=hst[:, t * 512:(t + 1) * 512],
                                in_=pb[bank][:, :], func=AF.Silu))
                last = (t == 1)
                em("scalar", lambda e, bank=hb[2] + t, t=t, last=last:
                   (e.activation(out=hst[:, 1024 + t * 512:1024 + (t + 1) * 512],
                                 in_=pb[bank][:, :],
                                 func=AF.Copy).then_inc(s_act.h, 1) if last else
                    e.activation(out=hst[:, 1024 + t * 512:1024 + (t + 1) * 512],
                                 in_=pb[bank][:, :], func=AF.Copy)))
            act_h[oc] = s_act.inc()
            wait("vector", s_act, act_h[oc])
            if oc - 4 >= 0:
                wait("vector", s_pe, pe_t2.get(oc - 4, 0))
                wait("vector", s_dve, dve_fold.get(rc - 1, 0))
            for t in range(2):
                last = (t == 1)
                em("vector", lambda e, bo=bo, t=t, last=last:
                   (e.tensor_mul(out=gst[:, bo * 1024 + t * 512:bo * 1024 + (t + 1) * 512],
                                 in0=hst[:, t * 512:(t + 1) * 512],
                                 in1=hst[:, 1024 + t * 512:1024 + (t + 1) * 512]).then_inc(s_dve.h, 1)
                    if last else
                    e.tensor_mul(out=gst[:, bo * 1024 + t * 512:bo * 1024 + (t + 1) * 512],
                                 in0=hst[:, t * 512:(t + 1) * 512],
                                 in1=hst[:, 1024 + t * 512:1024 + (t + 1) * 512])))
            dve_g[oc] = s_dve.inc()
            wait("tensor", s_dve, dve_g[oc])
            for t in range(2):
                last = (t == 1)
                em("tensor", lambda e, oc=oc, bo=bo, t=t, last=last:
                   (e.matmul(pb[6 + t][0:2, :], a2s[:, oc * 2:(oc + 1) * 2],
                             gst[:, bo * 1024 + t * 512:bo * 1024 + (t + 1) * 512],
                             start=(oc == 0), stop=(oc == NOC_H - 1),
                             skip_group_check=True).then_inc(s_pe.h, 1) if last else
                    e.matmul(pb[6 + t][0:2, :], a2s[:, oc * 2:(oc + 1) * 2],
                             gst[:, bo * 1024 + t * 512:bo * 1024 + (t + 1) * 512],
                             start=(oc == 0), stop=(oc == NOC_H - 1),
                             skip_group_check=True)))
            pe_t2[oc] = s_pe.inc()
            if oc + 2 < NOC_H:
                p4a_load(oc + 2)
        wait("vector", s_dve, dve_g[rc * 4 + 3])
        if rc >= 1:
            wait("vector", s_dve, dve_fold[rc - 1])
        em("vector", lambda e:
           e.tensor_add(out=tmp32[:, 0:1024], in0=gst[:, 0:1024], in1=gst[:, 1024:2048]))
        em("vector", lambda e:
           e.tensor_add(out=tmp32[:, 1024:2048], in0=gst[:, 2048:3072],
                        in1=gst[:, 3072:4096]).then_inc(s_dve.h, 1))
        c = s_dve.inc()
        wait("vector", s_dve, c)
        em("vector", lambda e, rc=rc:
           e.tensor_add(out=bufB[:, rc * 1024:(rc + 1) * 1024],
                        in0=tmp32[:, 0:1024], in1=tmp32[:, 1024:2048]).then_inc(s_dve.h, 1))
        dve_fold[rc] = s_dve.inc()
    wait("vector", s_pe, pe_t2[NOC_H - 1])
    for t in range(2):
        last = (t == 1)
        em("vector", lambda e, t=t, last=last:
           (e.tensor_scalar_mul(out=trow[0:2, 2048 + t * 512:2048 + (t + 1) * 512],
                                in0=pb[6 + t][0:2, :],
                                scalar1=0.5).then_inc(s_dve.h, 1) if last else
            e.tensor_scalar_mul(out=trow[0:2, 2048 + t * 512:2048 + (t + 1) * 512],
                                in0=pb[6 + t][0:2, :], scalar1=0.5)))
    t2_done = s_dve.inc()
    p4a_pe_end = s_pe.n

    gw_w2, pe_main4, pe_st4, dve_add4, act_sq4, act_cast4 = {}, {}, {}, {}, {}, {}

    def p4b_load(o2):
        par = o2 % 2
        if o2 - 2 >= 0:
            wait("gpsimd", s_pe, pe_main4[(o2 - 2, 1)])
        else:
            wait("gpsimd", s_pe, p4a_pe_end)
        em("gpsimd", lambda e, o2=o2, par=par:
           e.dma_start(out=wbuf[:, par * 4096:par * 4096 + 2816],
                       in_=w2_t[o2 * 128:(o2 + 1) * 128, :]).then_inc(s_gwp[par].h, 16))
        s_gwp[par].inc(16)
        em("gpsimd", lambda e, o2=o2, par=par:
           e.dma_start(out=b2s[0:2, par * 128:(par + 1) * 128],
                       in_=b2_t[0:2, o2 * 128:(o2 + 1) * 128]).then_inc(s_gwp[par].h, 16))
        gw_w2[o2] = s_gwp[par].inc(16)

    p4b_load(0)
    p4b_load(1)
    for o2 in range(32):
        par = o2 % 2
        br2 = [(o2 * 2) % 4, (o2 * 2 + 1) % 4]
        wait("tensor", s_gwp[par], gw_w2[o2])
        if o2 == 0:
            wait("tensor", s_dve, t2_done)
            wait("tensor", s_dve, dve_fold[HCH - 1])
        for t in range(2):
            if (o2 * 2 + t) - 4 >= 0:
                prev = o2 * 2 + t - 4
                wait("tensor", s_dve, dve_add4[(prev // 2, prev % 2)])
        for kc in range(HCH):
            for t in range(2):
                em("tensor", lambda e, br=br2[t], par=par, kc=kc, t=t:
                   e.matmul(pb[br][:, :],
                            wbuf[:, par * 4096 + kc * 128:par * 4096 + (kc + 1) * 128],
                            bufB[:, kc * 1024 + t * 512:kc * 1024 + (t + 1) * 512],
                            start=(kc == 0), stop=False, skip_group_check=True))
        for t in range(2):
            em("tensor", lambda e, br=br2[t], par=par, t=t:
               e.matmul(pb[br][:, :], b2s[0:2, par * 128:(par + 1) * 128],
                        trow[0:2, 2048 + t * 512:2048 + (t + 1) * 512],
                        start=False, stop=True,
                        skip_group_check=True).then_inc(s_pe.h, 1))
            s_pe.inc()
        for t in range(2):
            pe_main4[(o2, t)] = s_pe.n
            wait("vector", s_pe, pe_main4[(o2, t)])
            if o2 - 2 >= 0:
                wait("vector", s_pe, pe_st4.get((o2 - 2, t), 0))
                wait("vector", s_act, act_cast4.get((o2 - 2, t), 0))
            em("vector", lambda e, br=br2[t], par=par, t=t, o2=o2:
               e.tensor_add(out=yst[:, par * 1024 + t * 512:par * 1024 + (t + 1) * 512],
                            in0=pb[br][:, :],
                            in1=bufA[:, o2 * 1024 + t * 512:o2 * 1024 + (t + 1) * 512]).then_inc(s_dve.h, 1))
            dve_add4[(o2, t)] = s_dve.inc()
            wait("scalar", s_dve, dve_add4[(o2, t)])
            if o2 - 2 >= 0:
                wait("scalar", s_pe, pe_st4.get((o2 - 2, t), 0))
            em("scalar", lambda e, par=par, t=t:
               e.activation(out=sqstb[:, par * 1024 + t * 512:par * 1024 + (t + 1) * 512],
                            in_=yst[:, par * 1024 + t * 512:par * 1024 + (t + 1) * 512],
                            func=AF.Square).then_inc(s_act.h, 1))
            act_sq4[(o2, t)] = s_act.inc()
            em("scalar", lambda e, par=par, t=t, o2=o2:
               e.activation(out=bufA[:, o2 * 1024 + t * 512:o2 * 1024 + (t + 1) * 512],
                            in_=yst[:, par * 1024 + t * 512:par * 1024 + (t + 1) * 512],
                            func=AF.Copy).then_inc(s_act.h, 1))
            act_cast4[(o2, t)] = s_act.inc()
        for so in ([o2 - 1] if o2 >= 1 else []) + ([31] if o2 == 31 else []):
            spar = so % 2
            for t in range(2):
                wait("tensor", s_act, act_cast4[(so, t)])
                em("tensor", lambda e, t=t, so=so:
                   e.matmul(pb[4 + t][0:1, :], ones_cb[:, 0:1],
                            bufA[:, so * 1024 + t * 512:so * 1024 + (t + 1) * 512],
                            start=(so == 0), stop=(so == 31),
                            skip_group_check=True))
                if t == 1:
                    em("tensor", lambda e, spar=spar, t=t, so=so:
                       e.matmul(pb[6 + t][0:1, :], ones_cb[:, 0:1],
                                sqstb[:, spar * 1024 + t * 512:spar * 1024 + (t + 1) * 512],
                                start=(so == 0), stop=(so == 31),
                                skip_group_check=True).then_inc(s_pe.h, 1))
                else:
                    em("tensor", lambda e, spar=spar, t=t, so=so:
                       e.matmul(pb[6 + t][0:1, :], ones_cb[:, 0:1],
                                sqstb[:, spar * 1024 + t * 512:spar * 1024 + (t + 1) * 512],
                                start=(so == 0), stop=(so == 31),
                                skip_group_check=True))
            pe_st4[(so, 0)] = pe_st4[(so, 1)] = s_pe.inc()
        if o2 + 2 < 32:
            p4b_load(o2 + 2)

    mbuf_done2 = ln_finalize(pe_st4[(31, 0)])

    ss_out = {}
    pl_mul2 = {}
    for oc in range(32):
        par = oc % 2
        if oc == 0:
            wait("vector", s_dve, mbuf_done2)
            wait("vector", s_act, act_cast4[(31, 1)])
        if oc - 2 >= 0:
            wait("vector", s_pl, pl_mul2[oc - 2])
        em("vector", lambda e, oc=oc, par=par:
           e.tensor_sub(out=tmp32[:, par * 1024:(par + 1) * 1024],
                        in0=bufA[:, oc * 1024:(oc + 1) * 1024],
                        in1=mbuf[:, 0:1024]).then_inc(s_dve.h, 1))
        c = s_dve.inc()
        if oc == 0:
            wait("gpsimd", s_dve, mbuf_done2)
        wait("gpsimd", s_dve, c)
        if oc - 2 >= 0:
            wait("gpsimd", s_ssp[par], ss_out[oc - 2])
        em("gpsimd", lambda e, oc=oc, par=par:
           e.tensor_mul(out=outst[:, par * 1024:(par + 1) * 1024],
                        in0=tmp32[:, par * 1024:(par + 1) * 1024],
                        in1=mbuf[:, 1024:2048]).then_inc(s_pl.h, 1))
        pl_mul2[oc] = s_pl.inc()
        wait("sync", s_pl, pl_mul2[oc])
        em("sync", lambda e, oc=oc, par=par:
           e.dma_start(out=out_fm[oc * 128:(oc + 1) * 128, :],
                       in_=outst[:, par * 1024:(par + 1) * 1024]).then_inc(s_ssp[par].h, 16))
        ss_out[oc] = s_ssp[par].inc(16)

    if taps:
        wait("gpsimd", s_cc[3], 1)
        wait("gpsimd", s_ssp[0], s_ssp[0].n)
        wait("gpsimd", s_ssp[1], s_ssp[1].n)
    for src, dst in taps:
        em("gpsimd", lambda e, a=src, b=dst:
           e.dma_start(out=b[:, :], in_=a[:, :]).then_inc(s_ssp[0].h, 16))
        s_ssp[0].inc(16)

    wait("sync", s_ssp[0], s_ssp[0].n)
    wait("sync", s_ssp[1], s_ssp[1].n)

    with nc.Block() as block:
        @block.tensor
        def _(e):
            for fn in plan["tensor"]:
                fn(e)

        @block.scalar
        def _(e):
            for fn in plan["scalar"]:
                fn(e)

        @block.vector
        def _(e):
            for fn in plan["vector"]:
                fn(e)

        @block.sync
        def _(e):
            for fn in plan["sync"]:
                fn(e)

        @block.gpsimd
        def _(e):
            for fn in plan["gpsimd"]:
                fn(e)

    return nc


def _tile_wT(w):
    n_o, n_i = w.shape[0] // 128, w.shape[1] // 128
    return np.ascontiguousarray(
        w.reshape(n_o, 128, n_i, 128).transpose(0, 3, 2, 1).reshape(n_o * 128, n_i * 128))


def _prep_w13(w):
    wr = np.zeros((4, HCH, 128, 1024), dtype=np.float32)
    for bo in range(4):
        seg = w[bo * 2752:(bo + 1) * 2752]
        wr[bo, :21] = seg[:2688].reshape(21, 128, 1024)
        wr[bo, 21, :64] = seg[2688:]
    arr = wr.reshape(4, HCH, 128, 8, 128).transpose(1, 0, 4, 3, 2)
    return np.ascontiguousarray(arr.reshape(NOC_H * 128, 1024))


def _prep_w2(w):
    wp = np.zeros((4096, HCH * 128), dtype=np.float32)
    wp[:, :2752] = w
    arr = wp.reshape(32, 128, HCH, 128).transpose(0, 3, 2, 1)
    return np.ascontiguousarray(arr.reshape(32 * 128, HCH * 128))


def _prep_b13(bmat):
    br = np.zeros((4, HCH, 128, 2), dtype=np.float32)
    for bo in range(4):
        seg = bmat[bo * 2752:(bo + 1) * 2752]
        br[bo, :21] = seg[:2688].reshape(21, 128, 2)
        br[bo, 21, :64] = seg[2688:]
    arr = br.transpose(1, 0, 3, 2)
    return np.ascontiguousarray(arr.reshape(NOC_H * 2, 128))


def _prep_a2(a2):
    ar = np.zeros((4, HCH, 128, 2), dtype=np.float32)
    for bo in range(4):
        seg = a2[:, bo * 2752:(bo + 1) * 2752].T
        ar[bo, :21] = seg[:2688].reshape(21, 128, 2)
        ar[bo, 21, :64] = seg[2688:]
    arr = ar.transpose(1, 0, 2, 3).reshape(NOC_H, 128, 2).transpose(1, 0, 2)
    return np.ascontiguousarray(arr.reshape(128, NOC_H * 2))


def kernel(**inputs):
    inp = {k: np.asarray(v) for k, v in inputs.items()}
    x = inp["L0_attn_x"]

    shared = {
        "wq_t": _tile_wT(inp["L0_attn_Wq"]),
        "wk_t": _tile_wT(inp["L0_attn_Wk"]),
        "wv_t": _tile_wT(inp["L0_attn_Wv"]),
        "wo_t": _tile_wT(inp["L0_attn_Wo"]),
        "bq_f": np.ascontiguousarray(inp["L0_attn_bq"].reshape(32, 128).T),
        "bk_f": np.ascontiguousarray(inp["L0_attn_bk"].reshape(32, 128).T),
        "bv_f": np.ascontiguousarray(inp["L0_attn_bv"].reshape(32, 128).T),
        "bo_f": np.ascontiguousarray(inp["L0_attn_bo"].reshape(32, 128).T),
        "w1_t": _prep_w13(inp["L0_ffn_w1q"]),
        "w3_t": _prep_w13(inp["L0_ffn_w3q"]),
        "w2_t": _prep_w2(inp["L0_ffn_w2q"]),
        "b1_t": _prep_b13(inp["L0_ffn_b1"]),
        "b3_t": _prep_b13(inp["L0_ffn_b3"]),
        "b2_t": np.ascontiguousarray(inp["L0_ffn_b2"].T),
        "a1_t": np.ascontiguousarray(
            inp["L0_ffn_a1"].reshape(2, 32, 128).transpose(2, 1, 0).reshape(128, 64)),
        "a3_t": np.ascontiguousarray(
            inp["L0_ffn_a3"].reshape(2, 32, 128).transpose(2, 1, 0).reshape(128, 64)),
        "a2_t": _prep_a2(inp["L0_ffn_a2"]),
    }

    xT = [np.ascontiguousarray(x[b_].T) for b_ in range(2)]
    in_maps = []
    for c in range(8):
        cc0, cc1 = c, 15 - c
        parts = []
        for cc in (cc0, cc1):
            bb, s0 = cc // 8, (cc % 8) * 512
            parts.append(xT[bb][:, s0:s0 + 512])
        x_c = np.ascontiguousarray(np.concatenate(parts, axis=1))
        m = {"x_fm": x_c}
        m.update(shared)
        in_maps.append(m)

    nc = build_program()
    res = run_bass_kernel_spmd(nc, in_maps, core_ids=list(range(8)))

    out = np.empty((2, 4096, 4096), dtype=np.float32)
    for c in range(8):
        oc_fm = res.results[c]["out_fm"]
        for i, cc in enumerate((c, 15 - c)):
            bb, s0 = cc // 8, (cc % 8) * 512
            out[bb, s0:s0 + 512, :] = oc_fm[:, i * 512:(i + 1) * 512].T
    if DEBUG_TAPS:
        kernel._tap_results = res
    return out


# version 13 (final)
# speedup vs baseline: 1.2083x; 1.0247x over previous
import numpy as np
import ml_dtypes
import concourse.bass as bass
import concourse.mybir as mybir
from concourse.bass_utils import run_bass_kernel_spmd
from concourse.aot_env import allow_ici_all_to_all

F32 = mybir.dt.float32
BF16 = mybir.dt.bfloat16
AF = mybir.ActivationFunctionType
ALU = mybir.AluOpType

D = 4096
NT = 1024
HCH = 22
NOC_H = 88
EXP_BIAS = -8.0
EXP_SCALE = float(1.0 / np.sqrt(128.0))
LN_EPS = 1e-5
INV_D = 1.0 / D

DEBUG_TAPS = ()


def _r(cc):
    return cc if cc < 8 else 15 - cc


def _off(cc):
    return 0 if cc < 8 else 512


class Ct:
    def __init__(self, handle):
        self.h = handle
        self.n = 0

    def inc(self, k=1):
        self.n += k
        return self.n


def build_program():
    nc = bass.Bass("TRN2", target_bir_lowering=False)

    x_fm = nc.dram_tensor("x_fm", [D, NT], F32, kind="ExternalInput")
    wq_t = nc.dram_tensor("wq_t", [D, D], F32, kind="ExternalInput")
    wk_t = nc.dram_tensor("wk_t", [D, D], F32, kind="ExternalInput")
    wv_t = nc.dram_tensor("wv_t", [D, D], F32, kind="ExternalInput")
    wo_t = nc.dram_tensor("wo_t", [D, D], F32, kind="ExternalInput")
    bq_f = nc.dram_tensor("bq_f", [128, 32], F32, kind="ExternalInput")
    bk_f = nc.dram_tensor("bk_f", [128, 32], F32, kind="ExternalInput")
    bv_f = nc.dram_tensor("bv_f", [128, 32], F32, kind="ExternalInput")
    bo_f = nc.dram_tensor("bo_f", [128, 32], F32, kind="ExternalInput")
    w1_t = nc.dram_tensor("w1_t", [NOC_H * 128, 1024], F32, kind="ExternalInput")
    w3_t = nc.dram_tensor("w3_t", [NOC_H * 128, 1024], F32, kind="ExternalInput")
    w2_t = nc.dram_tensor("w2_t", [32 * 128, HCH * 128], F32, kind="ExternalInput")
    b1_t = nc.dram_tensor("b1_t", [NOC_H * 2, 128], F32, kind="ExternalInput")
    b3_t = nc.dram_tensor("b3_t", [NOC_H * 2, 128], F32, kind="ExternalInput")
    b2_t = nc.dram_tensor("b2_t", [2, D], F32, kind="ExternalInput")
    a1_t = nc.dram_tensor("a1_t", [128, 64], F32, kind="ExternalInput")
    a3_t = nc.dram_tensor("a3_t", [128, 64], F32, kind="ExternalInput")
    a2_t = nc.dram_tensor("a2_t", [128, NOC_H * 2], F32, kind="ExternalInput")

    out_fm = nc.dram_tensor("out_fm", [D, NT], F32, kind="ExternalOutput")

    q_l = nc.dram_tensor("q_l", [D, NT], BF16, kind="Internal")
    k_l = nc.dram_tensor("k_l", [D, NT], BF16, kind="Internal")
    q_o = nc.dram_tensor("q_o", [D, NT], BF16, kind="Internal")
    k_o = nc.dram_tensor("k_o", [D, NT], BF16, kind="Internal")
    v_l_h = [nc.dram_tensor(f"v_l_h{i}", [D // 2, NT], BF16, kind="Internal")
             for i in range(2)]
    v_o_h = [nc.dram_tensor(f"v_o_h{i}", [D // 2, NT], BF16, kind="Internal")
             for i in range(2)]
    cx_l_q = [nc.dram_tensor(f"cx_l_q{i}", [D // 4, NT], BF16, kind="Internal")
              for i in range(4)]
    cx_o_q = [nc.dram_tensor(f"cx_o_q{i}", [D // 4, NT], BF16, kind="Internal")
              for i in range(4)]

    tri_d = nc.inline_tensor(
        np.triu(np.ones((128, 128))).astype(ml_dtypes.bfloat16), "tri_d")

    internal_by_name = {"q_l": q_l, "k_l": k_l, "q_o": q_o, "k_o": k_o}
    taps = [(internal_by_name[n],
             nc.dram_tensor("tap_" + n, [D, NT], BF16, kind="ExternalOutput"))
            for n in DEBUG_TAPS]

    bufA = nc.alloc_sbuf_tensor("bufA", [128, 32768], BF16)
    bufB = nc.alloc_sbuf_tensor("bufB", [128, 32768], BF16)
    cb, _ce = nc.bump_sbuf(65536)
    qs = nc.alloc_sbuf_tensor_at("qs", [128, 8192], BF16, offset=cb + 0)
    ks = nc.alloc_sbuf_tensor_at("ks", [128, 8192], BF16, offset=cb + 16384)
    vt = nc.alloc_sbuf_tensor_at("vt", [128, 8192], BF16, offset=cb + 32768)
    rb_sb = nc.alloc_sbuf_tensor_at("rb_sb", [128, 1024], F32, offset=cb + 49152)
    p2row = nc.alloc_sbuf_tensor_at("p2row", [1, 2048], F32, offset=cb + 53248)
    ctxst = nc.alloc_sbuf_tensor_at("ctxst", [128, 2048], BF16, offset=cb + 61440)
    wbuf = nc.alloc_sbuf_tensor_at("wbuf", [128, 16384], BF16, offset=cb + 0)
    w13 = nc.alloc_sbuf_tensor_at("w13", [128, 4096], BF16, offset=cb + 16384)
    outst = nc.alloc_sbuf_tensor_at("outst", [128, 2048], F32, offset=cb + 16384)
    gst = nc.alloc_sbuf_tensor_at("gst", [128, 4096], BF16, offset=cb + 24576)
    mbuf = nc.alloc_sbuf_tensor_at("mbuf", [128, 2048], F32, offset=cb + 24576)
    xf = nc.alloc_sbuf_tensor_at("xf", [128, 2048], F32, offset=cb + 32768)
    yst = nc.alloc_sbuf_tensor_at("yst", [128, 2048], F32, offset=cb + 40960)
    hst = nc.alloc_sbuf_tensor_at("hst", [128, 2048], BF16, offset=cb + 40960)
    sqst = nc.alloc_sbuf_tensor_at("sqst", [128, 2048], F32, offset=cb + 49152)
    sqstb = nc.alloc_sbuf_tensor_at("sqstb", [128, 2048], BF16, offset=cb + 49152)
    rbf = nc.alloc_sbuf_tensor_at("rbf", [1, 1024], BF16, offset=cb + 63488)
    qkvst = nc.alloc_sbuf_tensor_at("qkvst", [128, 2048], BF16, offset=cb + 49152)
    tmp32 = nc.alloc_sbuf_tensor_at("tmp32", [128, 2048], F32, offset=cb + 57344)

    rowb = nc.alloc_sbuf_tensor_at("rowb", [1, 3072], F32, offset=cb + 49152)
    trow = nc.alloc_sbuf_tensor("trow", [2, 3072], BF16)
    b1s = nc.alloc_sbuf_tensor("b1s", [2, 256], BF16)
    b3s = nc.alloc_sbuf_tensor("b3s", [2, 256], BF16)
    b2s = nc.alloc_sbuf_tensor("b2s", [2, 256], BF16)
    tri = nc.alloc_sbuf_tensor("tri", [128, 128], BF16)
    ones_cb = nc.alloc_sbuf_tensor("ones_cb", [128, 16], BF16)
    ones_cf = nc.alloc_sbuf_tensor("ones_cf", [128, 8], F32)
    ones_rf = nc.alloc_sbuf_tensor("ones_rf", [1, 128], F32)
    ones_rb = nc.alloc_sbuf_tensor("ones_rb", [1, 128], BF16)
    expb = nc.alloc_sbuf_tensor("expb", [128, 8], F32)
    expsc = nc.alloc_sbuf_tensor("expsc", [128, 8], F32)
    bq_s = nc.alloc_sbuf_tensor("bq_s", [128, 32], F32)
    bk_s = nc.alloc_sbuf_tensor("bk_s", [128, 32], F32)
    bv_s = nc.alloc_sbuf_tensor("bv_s", [128, 32], F32)
    bo_s = nc.alloc_sbuf_tensor("bo_s", [128, 32], F32)
    a1s = nc.alloc_sbuf_tensor("a1s", [128, 64], BF16)
    a3s = nc.alloc_sbuf_tensor("a3s", [128, 64], BF16)
    a2s = nc.alloc_sbuf_tensor("a2s", [128, NOC_H * 2], BF16)

    pb = [nc.alloc_psum_tensor(f"pb{i}", [128, 512], F32) for i in range(8)]

    s_gwx = Ct(nc.alloc_semaphore("s_gwx"))
    s_gwp = [Ct(nc.alloc_semaphore(f"s_gwp{i}")) for i in range(4)]
    s_slc = Ct(nc.alloc_semaphore("s_slc"))
    s_slp = [Ct(nc.alloc_semaphore(f"s_slp{i}")) for i in range(2)]
    s_slv = [Ct(nc.alloc_semaphore(f"s_slv{i}")) for i in range(2)]
    s_ssp = [Ct(nc.alloc_semaphore(f"s_ssp{i}")) for i in range(2)]
    s_pe = Ct(nc.alloc_semaphore("s_pe"))
    s_act = Ct(nc.alloc_semaphore("s_act"))
    s_dve = Ct(nc.alloc_semaphore("s_dve"))
    s_cc = [Ct(nc.alloc_semaphore(f"s_cc{i}")) for i in range(8)]
    s_u01 = Ct(nc.alloc_semaphore("s_u01"))
    s_pl = Ct(nc.alloc_semaphore("s_pl"))
    s_cl = [Ct(nc.alloc_semaphore(f"s_cl{i}")) for i in range(2)]

    plan = {"tensor": [], "scalar": [], "vector": [], "sync": [], "gpsimd": []}

    def em(eng, fn):
        plan[eng].append(fn)

    def wait(eng, ct, val):
        if val > 0:
            em(eng, lambda e, s=ct.h, v=val: e.wait_ge(s, v))

    em("vector", lambda e: e.memset(ones_cb[:, :], 1.0))
    em("vector", lambda e: e.memset(ones_cf[:, :], 1.0))
    em("vector", lambda e: e.memset(ones_rf[:, :], 1.0))
    em("vector", lambda e: e.memset(ones_rb[:, :], 1.0))
    em("vector", lambda e: e.memset(rbf[:, :], 0.0))
    em("vector", lambda e: e.memset(expb[:, :], EXP_BIAS))
    em("vector", lambda e: e.memset(expsc[:, :], EXP_SCALE).then_inc(s_dve.h, 1))
    dve_consts = s_dve.inc()

    for src, dst in ((tri_d, tri), (bq_f, bq_s), (bk_f, bk_s),
                     (bv_f, bv_s), (bo_f, bo_s)):
        em("sync", lambda e, a=src, b=dst:
           e.dma_start(out=b[:, :], in_=a[:, :]).then_inc(s_slc.h, 16))
        s_slc.inc(16)
    sl_consts = s_slc.n

    for i0 in range(4):
        def f(e, i0=i0):
            src = x_fm[i0 * 1024:(i0 + 1) * 1024, :].rearrange(
                "(i p) t -> p i t", p=128)
            dst = bufA[:, i0 * 8192:(i0 + 1) * 8192].rearrange(
                "p (i t) -> p i t", i=8)
            e.dma_start(out=dst, in_=src).then_inc(s_gwx.h, 16)
        em("gpsimd", f)
        s_gwx.inc(16)
    for src, dst in ((a1_t, a1s), (a3_t, a3s), (a2_t, a2s)):
        em("gpsimd", lambda e, a=src, b=dst:
           e.dma_start(out=b[:, :], in_=a[:, :]).then_inc(s_gwx.h, 16))
        s_gwx.inc(16)
    gw_x = s_gwx.n

    v_order = [oc for oc in range(32) if oc % 4 < 2] + \
              [oc for oc in range(32) if oc % 4 >= 2]
    p1_mats = [(wk_t, bk_s, list(range(32))),
               (wq_t, bq_s, list(range(32))),
               (wv_t, bv_s, v_order)]

    def p1_store_target(m, oc):
        if m == 0:
            return k_l, oc * 128
        if m == 1:
            return q_l, oc * 128
        half = (oc % 4) // 2
        return v_l_h[half], (oc // 4) * 256 + (oc % 4 % 2) * 128

    pe_p1, act_p1, ss_p1, gw_p1 = {}, {}, {}, {}

    def p1_load(it):
        m, pos = divmod(it, 32)
        oc = p1_mats[m][2][pos]
        wt = p1_mats[m][0]
        if it - 4 >= 0:
            wait("gpsimd", s_pe, pe_p1[it - 4])
        em("gpsimd", lambda e, w=wt, oc=oc, par=it % 4:
           e.dma_start(out=wbuf[:, par * 4096:(par + 1) * 4096],
                       in_=w[oc * 128:(oc + 1) * 128, :]).then_inc(s_gwp[par].h, 16))
        gw_p1[it] = s_gwp[it % 4].inc(16)

    def p1_collective(src, dsto, ci):
        wait("gpsimd", s_ssp[0], s_ssp[0].n)
        wait("gpsimd", s_ssp[1], s_ssp[1].n)
        def fcc(e, a=src, b=dsto, ci=ci):
            with allow_ici_all_to_all():
                e.collective_compute(
                    "AllToAll", ALU.bypass,
                    replica_groups=[[0, 1, 2, 3, 4, 5, 6, 7]],
                    ins=[a.ap()], outs=[b.ap()]).then_inc(s_cc[ci].h, 1)
        em("gpsimd", fcc)
        s_cc[ci].inc()

    for _pre in range(4):
        p1_load(_pre)
    for it in range(96):
        m, pos = divmod(it, 32)
        wt, bias, order = p1_mats[m]
        oc = order[pos]
        dst, drow = p1_store_target(m, oc)
        par4 = it % 4
        par = it % 2
        b0, b1 = 2 * par, 2 * par + 1
        wait("tensor", s_gwp[par4], gw_p1[it])
        if it == 0:
            wait("tensor", s_gwx, gw_x)
        if it - 2 >= 0:
            wait("tensor", s_dve, act_p1[it - 2])
        for i in range(32):
            st, sp = (i == 0), (i == 31)
            em("tensor", lambda e, b=b0, p=par4, i=i, st=st, sp=sp:
               e.matmul(pb[b][:, :],
                        wbuf[:, p * 4096 + i * 128:p * 4096 + (i + 1) * 128],
                        bufA[:, i * 1024:i * 1024 + 512],
                        start=st, stop=sp, skip_group_check=True))
            if not sp:
                em("tensor", lambda e, b=b1, p=par4, i=i, st=st:
                   e.matmul(pb[b][:, :],
                            wbuf[:, p * 4096 + i * 128:p * 4096 + (i + 1) * 128],
                            bufA[:, i * 1024 + 512:(i + 1) * 1024],
                            start=st, stop=False, skip_group_check=True))
            else:
                em("tensor", lambda e, b=b1, p=par4, i=i:
                   e.matmul(pb[b][:, :],
                            wbuf[:, p * 4096 + i * 128:p * 4096 + (i + 1) * 128],
                            bufA[:, i * 1024 + 512:(i + 1) * 1024],
                            start=False, stop=True,
                            skip_group_check=True).then_inc(s_pe.h, 1))
        pe_p1[it] = s_pe.inc()

        wait("vector", s_pe, pe_p1[it])
        if it == 0:
            wait("vector", s_slc, sl_consts)
        if it - 2 >= 0:
            wait("vector", s_ssp[it % 2], ss_p1[it - 2])
        em("vector", lambda e, b=b0, par=par, bias=bias, oc=oc:
           e.tensor_scalar_add(out=qkvst[:, par * 1024:par * 1024 + 512],
                               in0=pb[b][:, :], scalar1=bias[:, oc:oc + 1]))
        em("vector", lambda e, b=b1, par=par, bias=bias, oc=oc:
           e.tensor_scalar_add(out=qkvst[:, par * 1024 + 512:(par + 1) * 1024],
                               in0=pb[b][:, :],
                               scalar1=bias[:, oc:oc + 1]).then_inc(s_dve.h, 1))
        act_p1[it] = s_dve.inc()

        wait("sync", s_dve, act_p1[it])
        em("sync", lambda e, d=dst, drow=drow, par=par:
           e.dma_start(out=d[drow:drow + 128, :],
                       in_=qkvst[:, par * 1024:(par + 1) * 1024]).then_inc(s_ssp[par].h, 16))
        ss_p1[it] = s_ssp[it % 2].inc(16)

        if it + 4 < 96:
            p1_load(it + 4)
        if m == 0 and pos == 31:
            p1_collective(k_l, k_o, 0)
        elif m == 1 and pos == 31:
            p1_collective(q_l, q_o, 1)
        elif m == 2 and pos == 15:
            p1_collective(v_l_h[0], v_o_h[0], 2)
        elif m == 2 and pos == 31:
            p1_collective(v_l_h[1], v_o_h[1], 3)
    p1_pe_end = s_pe.n
    p1_ss_end = (s_ssp[0].n, s_ssp[1].n)

    units = [(h_l, b) for h_l in range(4) for b in range(2)]
    sl_qk, sl_v, pe_unit = {}, {}, {}
    pv_item, exp_glob_list = {}, []
    exp_item_last, dve_mask_item = {}, {}
    dve_s, dve_rb, dve_ctx, ss_ctx = {}, {}, {}, {}

    def u_bufs(u):
        if u < 2:
            base = u * 12288
            return (bufB, base, bufB, base + 4096, bufB, base + 8192)
        par = u % 2
        return (qs, par * 4096, ks, par * 4096, vt, par * 4096)

    def p2_load(u):
        h_l, b = units[u]
        par = u % 2
        QT, qb, KT, kb, VT, vb = u_bufs(u)
        if u < 2:
            ldq, sem = "scalar", s_u01
            wait(ldq, s_cc[0], 1)
            wait(ldq, s_cc[1], 1)
        else:
            ldq = "sync"
            if u in (2, 3):
                wait(ldq, s_pe, p1_pe_end)
            else:
                wait(ldq, s_pe, pv_item[(u - 2) * 8 + 7])
        for qc in range(8):
            cc = 8 * b + qc
            r, of = _r(cc), _off(cc)
            em(ldq, lambda e, r=r, of=of, h_l=h_l, QT=QT, qb=qb, qc=qc,
               sm=(s_u01 if u < 2 else s_slp[par]).h:
               e.dma_start(out=QT[:, qb + qc * 512:qb + (qc + 1) * 512],
                           in_=q_o[512 * r + 128 * h_l:512 * r + 128 * (h_l + 1),
                                   of:of + 512]).then_inc(sm, 16))
            (s_u01 if u < 2 else s_slp[par]).inc(16)
            em(ldq, lambda e, r=r, of=of, h_l=h_l, KT=KT, kb=kb, qc=qc,
               sm=(s_u01 if u < 2 else s_slp[par]).h:
               e.dma_start(out=KT[:, kb + qc * 512:kb + (qc + 1) * 512],
                           in_=k_o[512 * r + 128 * h_l:512 * r + 128 * (h_l + 1),
                                   of:of + 512]).then_inc(sm, 16))
            (s_u01 if u < 2 else s_slp[par]).inc(16)
        sl_qk[u] = ((s_u01 if u < 2 else s_slp[par]),
                    (s_u01 if u < 2 else s_slp[par]).n)
        if u < 2:
            wait(ldq, s_cc[2], 1)
        elif u in (2, 3) or u in (4, 5):
            wait(ldq, s_cc[2 + h_l // 2], 1)
        vhalf = v_o_h[h_l // 2]
        hl2 = h_l % 2
        for kt in range(32):
            cc = 8 * b + kt // 4
            w = kt % 4
            r, of = _r(cc), _off(cc)
            em(ldq, lambda e, r=r, of=of, w=w, hl2=hl2, vhalf=vhalf, VT=VT, vb=vb, kt=kt,
               sm=(s_u01 if u < 2 else s_slv[par]).h:
               e.dma_start(out=VT[:, vb + kt * 128:vb + (kt + 1) * 128],
                           in_=vhalf[256 * r + 128 * hl2:256 * r + 128 * (hl2 + 1),
                                     of + 128 * w:of + 128 * (w + 1)],
                           transpose=True).then_inc(sm, 16))
            (s_u01 if u < 2 else s_slv[par]).inc(16)
        sl_v[u] = ((s_u01 if u < 2 else s_slv[par]),
                   (s_u01 if u < 2 else s_slv[par]).n)

    p2_load(0)
    p2_load(1)
    for _u in (0, 1):
        sl_qk[_u] = (s_u01, s_u01.n)
        sl_v[_u] = (s_u01, s_u01.n)
    ktg = 0
    pe_S_item, pe_rb_item = {}, {}

    def it_of(k):
        u, qc = divmod(k, 8)
        h_l, b = units[u]
        return u, qc, h_l, b, u % 2, k % 2, 4 * (qc + 1), (k % 2) * 16384

    def emit_sT_exp(k):
        nonlocal ktg
        u, qc, h_l, b, par, kp, nkt, half = it_of(k)
        if qc == 0:
            wait("tensor", sl_qk[u][0], sl_qk[u][1])
        for kt in range(nkt):
            sb = 4 + (ktg % 2)
            if ktg - 2 >= 0:
                wait("tensor", s_act, exp_glob_list[ktg - 2])
            em("tensor", lambda e, sb=sb, par=par, kt=kt, qc=qc:
               e.matmul(pb[sb][:, :],
                        ks[:, par * 4096 + kt * 128:par * 4096 + (kt + 1) * 128],
                        qs[:, par * 4096 + qc * 512:par * 4096 + (qc + 1) * 512],
                        start=True, stop=True).then_inc(s_pe.h, 1))
            pe_sT = s_pe.inc()
            wait("scalar", s_pe, pe_sT)
            if kt == 0:
                wait("scalar", s_pe, pv_item.get(k - 2, 0))
                if k == 0:
                    wait("scalar", s_dve, dve_consts)
            em("scalar", lambda e, sb=sb, half=half, kt=kt:
               e.activation(out=bufA[:, half + kt * 512:half + (kt + 1) * 512],
                            in_=pb[sb][:, :], func=AF.Exp,
                            bias=expb[:, 0:1],
                            scale=expsc[:, 0:1]).then_inc(s_act.h, 1))
            exp_glob_list.append(s_act.inc())
            ktg += 1
        exp_item_last[k] = s_act.n

    def emit_masks(k):
        u, qc, h_l, b, par, kp, nkt, half = it_of(k)
        if k == 0:
            wait("vector", s_slc, sl_consts)
        wait("vector", s_act, exp_item_last[k])
        for d in range(4):
            base = half + (nkt - 4 + d) * 512
            if d > 0:
                em("vector", lambda e, base=base, d=d:
                   e.memset(bufA[:, base:base + 128 * d], 0.0))
            last = (d == 3)
            em("vector", lambda e, base=base, d=d, last=last:
               (e.tensor_mul(out=bufA[:, base + 128 * d:base + 128 * (d + 1)],
                             in0=bufA[:, base + 128 * d:base + 128 * (d + 1)],
                             in1=tri[:, :]).then_inc(s_dve.h, 1) if last else
                e.tensor_mul(out=bufA[:, base + 128 * d:base + 128 * (d + 1)],
                             in0=bufA[:, base + 128 * d:base + 128 * (d + 1)],
                             in1=tri[:, :])))
        dve_mask_item[k] = s_dve.inc()

    def emit_sum(j):
        u, qc, h_l, b, par, kp, nkt, half = it_of(j)
        sbank = 6 + kp
        wait("tensor", s_act, exp_item_last[j])
        wait("tensor", s_dve, dve_mask_item[j])
        if j == 0:
            wait("tensor", s_dve, dve_consts)
        if j - 2 >= 0:
            wait("tensor", s_dve, dve_s[j - 2])
        for kt in range(nkt):
            st, sp = (kt == 0), (kt == nkt - 1)
            if not sp:
                em("tensor", lambda e, sbank=sbank, half=half, kt=kt, st=st:
                   e.matmul(pb[sbank][0:1, :], ones_cb[:, 0:1],
                            bufA[:, half + kt * 512:half + (kt + 1) * 512],
                            start=st, stop=False))
            else:
                em("tensor", lambda e, sbank=sbank, half=half, kt=kt, st=st:
                   e.matmul(pb[sbank][0:1, :], ones_cb[:, 0:1],
                            bufA[:, half + kt * 512:half + (kt + 1) * 512],
                            start=st, stop=True).then_inc(s_pe.h, 1))
        pe_S_item[j] = s_pe.inc()
        wait("vector", s_pe, pe_S_item[j])
        em("vector", lambda e, sbank=sbank, kp=kp:
           e.tensor_copy(out=p2row[0:1, kp * 512:(kp + 1) * 512],
                         in_=pb[sbank][0:1, :]).then_inc(s_dve.h, 1))
        dve_scp = s_dve.inc()
        wait("vector", s_dve, dve_scp)
        em("vector", lambda e, kp=kp:
           e.reciprocal(out=p2row[0:1, 1024 + kp * 512:1024 + (kp + 1) * 512],
                        in_=p2row[0:1, kp * 512:(kp + 1) * 512]).then_inc(s_dve.h, 1))
        dve_s[j] = s_dve.inc()

    def emit_rb(j):
        u, qc, h_l, b, par, kp, nkt, half = it_of(j)
        rbank = kp
        if j <= 1:
            wait("vector", s_ssp[0], p1_ss_end[0])
            wait("vector", s_ssp[1], p1_ss_end[1])
        wait("tensor", s_dve, dve_s[j])
        if j - 2 >= 0:
            wait("tensor", s_dve, dve_rb[j - 2])
        em("tensor", lambda e, rbank=rbank, kp=kp:
           e.matmul(pb[rbank][:, :], ones_rf[0:1, :],
                    p2row[0:1, 1024 + kp * 512:1024 + (kp + 1) * 512],
                    start=True, stop=True).then_inc(s_pe.h, 1))
        pe_rb_item[j] = s_pe.inc()
        wait("vector", s_pe, pe_rb_item[j])
        em("vector", lambda e, rbank=rbank, kp=kp:
           e.tensor_copy(out=rb_sb[:, kp * 512:(kp + 1) * 512],
                         in_=pb[rbank][:, :]).then_inc(s_dve.h, 1))
        dve_rb[j] = s_dve.inc()

    def emit_pv(j):
        u, qc, h_l, b, par, kp, nkt, half = it_of(j)
        cbank = 2 + kp
        QT, qb, KT, kb, VT, vb = u_bufs(u)
        wait("tensor", s_act, exp_item_last[j])
        wait("tensor", s_dve, dve_mask_item[j])
        if qc == 0:
            wait("tensor", sl_v[u][0], sl_v[u][1])
        if j - 2 >= 0:
            wait("tensor", s_dve, dve_ctx[j - 2])
        for kt in range(nkt):
            st, sp = (kt == 0), (kt == nkt - 1)
            if not sp:
                em("tensor", lambda e, cbank=cbank, VT=VT, vb=vb, half=half, kt=kt, st=st:
                   e.matmul(pb[cbank][:, :],
                            VT[:, vb + kt * 128:vb + (kt + 1) * 128],
                            bufA[:, half + kt * 512:half + (kt + 1) * 512],
                            start=st, stop=False, skip_group_check=True))
            else:
                em("tensor", lambda e, cbank=cbank, VT=VT, vb=vb, half=half, kt=kt, st=st:
                   e.matmul(pb[cbank][:, :],
                            VT[:, vb + kt * 128:vb + (kt + 1) * 128],
                            bufA[:, half + kt * 512:half + (kt + 1) * 512],
                            start=st, stop=True,
                            skip_group_check=True).then_inc(s_pe.h, 1))
        pv_item[j] = s_pe.inc()

    def emit_ctx_evac(j):
        u, qc, h_l, b, par, kp, nkt, half = it_of(j)
        cbank = 2 + kp
        cc = 8 * b + qc
        wait("vector", s_pe, pv_item[j])
        wait("vector", s_dve, dve_rb[j])
        if j - 2 >= 0:
            wait("vector", s_ssp[kp], ss_ctx[j - 2])
        em("vector", lambda e, cbank=cbank, kp=kp:
           e.tensor_tensor(out=ctxst[:, kp * 512:(kp + 1) * 512],
                           in0=pb[cbank][:, :],
                           in1=rb_sb[:, kp * 512:(kp + 1) * 512],
                           op=ALU.mult).then_inc(s_dve.h, 1))
        dve_ctx[j] = s_dve.inc()
        wait("sync", s_dve, dve_ctx[j])
        r, of = _r(cc), _off(cc)
        cq = cx_l_q[h_l]
        em("sync", lambda e, r=r, of=of, cq=cq, kp=kp:
           e.dma_start(out=cq[128 * r:128 * (r + 1), of:of + 512],
                       in_=ctxst[:, kp * 512:(kp + 1) * 512]).then_inc(s_ssp[kp].h, 16))
        ss_ctx[j] = s_ssp[kp].inc(16)

    def consumer_passes(k):
        passes = []
        if k - 1 >= 0:
            j = k - 1
            u, qc, h_l, b, par, kp, nkt, half = it_of(j)
            sbank = 6 + kp
            cbank = 2 + kp

            def sum_gate(e=None, j=j):
                wait("tensor", s_act, exp_item_last[j])
                wait("tensor", s_dve, dve_mask_item[j])
                if j == 0:
                    wait("tensor", s_dve, dve_consts)
                if j - 2 >= 0:
                    wait("tensor", s_dve, dve_s[j - 2])
            passes.append(sum_gate)
            for kt in range(nkt):
                st, sp = (kt == 0), (kt == nkt - 1)
                def f(e=None, sbank=sbank, half=half, kt=kt, st=st, sp=sp, j=j):
                    if not sp:
                        em("tensor", lambda e, sbank=sbank, half=half, kt=kt, st=st:
                           e.matmul(pb[sbank][0:1, :], ones_cb[:, 0:1],
                                    bufA[:, half + kt * 512:half + (kt + 1) * 512],
                                    start=st, stop=False, skip_group_check=True))
                    else:
                        em("tensor", lambda e, sbank=sbank, half=half, kt=kt, st=st:
                           e.matmul(pb[sbank][0:1, :], ones_cb[:, 0:1],
                                    bufA[:, half + kt * 512:half + (kt + 1) * 512],
                                    start=st, stop=True,
                                    skip_group_check=True).then_inc(s_pe.h, 1))
                        pe_S_item[j] = s_pe.inc()
                        wait("vector", s_pe, pe_S_item[j])
                        em("vector", lambda e, sbank=sbank, kp=j % 2:
                           e.tensor_copy(out=p2row[0:1, (j % 2) * 512:(j % 2 + 1) * 512],
                                         in_=pb[sbank][0:1, :]).then_inc(s_dve.h, 1))
                        dve_scp = s_dve.inc()
                        wait("vector", s_dve, dve_scp)
                        em("vector", lambda e, kp=j % 2:
                           e.reciprocal(out=p2row[0:1, 1024 + kp * 512:1024 + (kp + 1) * 512],
                                        in_=p2row[0:1, kp * 512:(kp + 1) * 512]).then_inc(s_dve.h, 1))
                        dve_s[j] = s_dve.inc()
                passes.append(f)
            def pv_gate(e=None, j=j, par=par, qc=qc, u=u):
                wait("tensor", s_act, exp_item_last[j])
                wait("tensor", s_dve, dve_mask_item[j])
                if qc == 0:
                    wait("tensor", sl_v[u][0], sl_v[u][1])
                if j - 2 >= 0:
                    wait("tensor", s_dve, dve_ctx[j - 2])
            passes.append(pv_gate)
            for kt in range(nkt):
                st, sp = (kt == 0), (kt == nkt - 1)
                VTj, vbj = u_bufs(u)[4], u_bufs(u)[5]
                def f(e=None, cbank=cbank, half=half, kt=kt, st=st, sp=sp, j=j,
                      VT=VTj, vb=vbj):
                    if not sp:
                        em("tensor", lambda e, cbank=cbank, VT=VT, vb=vb, half=half, kt=kt, st=st:
                           e.matmul(pb[cbank][:, :],
                                    VT[:, vb + kt * 128:vb + (kt + 1) * 128],
                                    bufA[:, half + kt * 512:half + (kt + 1) * 512],
                                    start=st, stop=False, skip_group_check=True))
                    else:
                        em("tensor", lambda e, cbank=cbank, VT=VT, vb=vb, half=half, kt=kt, st=st:
                           e.matmul(pb[cbank][:, :],
                                    VT[:, vb + kt * 128:vb + (kt + 1) * 128],
                                    bufA[:, half + kt * 512:half + (kt + 1) * 512],
                                    start=st, stop=True,
                                    skip_group_check=True).then_inc(s_pe.h, 1))
                        pv_item[j] = s_pe.inc()
                passes.append(f)
        if k - 2 >= 0:
            passes.append(lambda e=None, j=k - 2: emit_rb(j))
        return passes

    def emit_sT_exp_interleaved(k):
        nonlocal ktg
        u, qc, h_l, b, par, kp, nkt, half = it_of(k)
        queue = consumer_passes(k)
        qi = 0
        if qc == 0:
            wait("tensor", sl_qk[u][0], sl_qk[u][1])
        for kt in range(nkt):
            sb = 4 + (ktg % 2)
            if ktg - 2 >= 0:
                wait("tensor", s_act, exp_glob_list[ktg - 2])
            QT, qb, KT, kb, VT, vb = u_bufs(u)
            em("tensor", lambda e, sb=sb, kt=kt, qc=qc, QT=QT, qb=qb, KT=KT, kb=kb:
               e.matmul(pb[sb][:, :],
                        KT[:, kb + kt * 128:kb + (kt + 1) * 128],
                        QT[:, qb + qc * 512:qb + (qc + 1) * 512],
                        start=True, stop=True,
                        skip_group_check=True).then_inc(s_pe.h, 1))
            pe_sT = s_pe.inc()
            wait("scalar", s_pe, pe_sT)
            if kt == 0:
                wait("scalar", s_pe, pv_item.get(k - 2, 0))
                if k == 0:
                    wait("scalar", s_dve, dve_consts)
            em("scalar", lambda e, sb=sb, half=half, kt=kt:
               e.activation(out=bufA[:, half + kt * 512:half + (kt + 1) * 512],
                            in_=pb[sb][:, :], func=AF.Exp,
                            bias=expb[:, 0:1],
                            scale=expsc[:, 0:1]).then_inc(s_act.h, 1))
            exp_glob_list.append(s_act.inc())
            ktg += 1
            want = ((kt + 1) * len(queue)) // nkt
            while qi < want:
                queue[qi]()
                qi += 1
        while qi < len(queue):
            queue[qi]()
            qi += 1
        exp_item_last[k] = s_act.n

    for k in range(64):
        emit_sT_exp_interleaved(k)
        if k - 2 >= 0:
            emit_ctx_evac(k - 2)
        emit_masks(k)
        if k % 8 == 0 and k >= 8:
            u_prev = k // 8 - 1
            if u_prev + 2 < 8:
                p2_load(u_prev + 2)
        if k in (17, 33, 49):
            qn = (k - 17) // 16
            wait("gpsimd", s_ssp[0], ss_ctx[k - 3])
            wait("gpsimd", s_ssp[1], ss_ctx[k - 2])
            def fcc_ctxq(e, qn=qn):
                with allow_ici_all_to_all():
                    e.collective_compute("AllToAll", ALU.bypass,
                                         replica_groups=[[0, 1, 2, 3, 4, 5, 6, 7]],
                                         ins=[cx_l_q[qn].ap()],
                                         outs=[cx_o_q[qn].ap()]).then_inc(s_cc[4 + qn].h, 1)
            em("gpsimd", fcc_ctxq)
            s_cc[4 + qn].inc()
    emit_sum(63)
    emit_rb(62)
    emit_pv(63)
    emit_ctx_evac(62)
    emit_rb(63)
    emit_ctx_evac(63)

    wait("gpsimd", s_ssp[0], s_ssp[0].n)
    wait("gpsimd", s_ssp[1], s_ssp[1].n)
    def fcc_ctx3(e):
        with allow_ici_all_to_all():
            e.collective_compute("AllToAll", ALU.bypass,
                                 replica_groups=[[0, 1, 2, 3, 4, 5, 6, 7]],
                                 ins=[cx_l_q[3].ap()],
                                 outs=[cx_o_q[3].ap()]).then_inc(s_cc[7].h, 1)
    em("gpsimd", fcc_ctx3)
    s_cc[7].inc()
    p2_pe_end = s_pe.n
    p2_dve_end = dve_ctx[63]

    qsem = [s_slc, s_slv[0], s_cl[0], s_cl[1]]
    ctx_ld_q = []
    wait("sync", s_pe, pv_item[15])
    for qn in range(4):
        wait("sync", s_cc[4 + qn], 1)
        for oc in range(qn, 32, 4):
            m_ = oc // 4
            em("sync", lambda e, oc=oc, m_=m_, qn=qn:
               e.dma_start(out=bufB[:, oc * 1024:(oc + 1) * 1024],
                           in_=cx_o_q[qn][m_ * 128:(m_ + 1) * 128, :]).then_inc(qsem[qn].h, 16))
            qsem[qn].inc(16)
        ctx_ld_q.append(qsem[qn].n)

    gw_wo, sl_xf = {}, {}
    pe_main3, pe_st3, dve_stt3, act_sq3, act_cast3 = {}, {}, {}, {}, {}

    def p3_wo_load(oc):
        if oc - 2 >= 0:
            wait("gpsimd", s_pe, pe_main3[(oc - 2, 1)])
        else:
            wait("gpsimd", s_pe, p2_pe_end)
        em("gpsimd", lambda e, oc=oc, par=oc % 2:
           e.dma_start(out=wbuf[:, par * 4096:(par + 1) * 4096],
                       in_=wo_t[oc * 128:(oc + 1) * 128, :]).then_inc(s_gwp[oc % 2].h, 16))
        gw_wo[oc] = s_gwp[oc % 2].inc(16)

    def p3_xf_load(oc):
        if oc - 2 >= 0:
            wait("scalar", s_dve, dve_stt3[(oc - 2, 1)])
        else:
            wait("scalar", s_pe, p2_pe_end)
        em("scalar", lambda e, oc=oc, par=oc % 2:
           e.dma_start(out=xf[:, par * 1024:(par + 1) * 1024],
                       in_=x_fm[oc * 128:(oc + 1) * 128, :]).then_inc(s_slp[oc % 2].h, 16))
        sl_xf[oc] = s_slp[oc % 2].inc(16)

    p3_wo_load(0)
    p3_wo_load(1)
    p3_xf_load(0)
    p3_xf_load(1)
    kseq = [i for qn in range(4) for i in range(qn, 32, 4)]
    for oc in range(32):
        par = oc % 2
        for t in range(2):
            br = (oc * 2 + t) % 4
            wait("tensor", s_gwp[oc % 2], gw_wo[oc])
            if oc == 0 and t == 0:
                wait("tensor", s_slc, ctx_ld_q[0])
                wait("tensor", s_dve, p2_dve_end)
            if (oc * 2 + t) - 4 >= 0:
                prev = (oc * 2 + t - 4)
                wait("tensor", s_dve, dve_stt3[(prev // 2, prev % 2)])
            for ki, i in enumerate(kseq):
                st, sp = (ki == 0), (ki == 31)
                if oc == 0 and t == 0 and ki in (8, 16, 24):
                    wait("tensor", qsem[ki // 8], ctx_ld_q[ki // 8])
                if not sp:
                    em("tensor", lambda e, br=br, par=par, i=i, t=t, st=st:
                       e.matmul(pb[br][:, :],
                                wbuf[:, par * 4096 + i * 128:par * 4096 + (i + 1) * 128],
                                bufB[:, i * 1024 + t * 512:i * 1024 + (t + 1) * 512],
                                start=st, stop=False, skip_group_check=True))
                else:
                    em("tensor", lambda e, br=br, par=par, i=i, t=t:
                       e.matmul(pb[br][:, :],
                                wbuf[:, par * 4096 + i * 128:par * 4096 + (i + 1) * 128],
                                bufB[:, i * 1024 + t * 512:i * 1024 + (t + 1) * 512],
                                start=False, stop=True,
                                skip_group_check=True).then_inc(s_pe.h, 1))
            pe_main3[(oc, t)] = s_pe.inc()
            wait("vector", s_pe, pe_main3[(oc, t)])
            wait("vector", s_slp[oc % 2], sl_xf[oc])
            if oc - 2 >= 0:
                wait("vector", s_pe, pe_st3.get((oc - 2, t), 0))
                wait("vector", s_act, act_cast3.get((oc - 2, t), 0))
            em("vector", lambda e, br=br, par=par, t=t, oc=oc:
               e.scalar_tensor_tensor(out=yst[:, par * 1024 + t * 512:par * 1024 + (t + 1) * 512],
                                      in0=pb[br][:, :], scalar=bo_s[:, oc:oc + 1],
                                      in1=xf[:, par * 1024 + t * 512:par * 1024 + (t + 1) * 512],
                                      op0=ALU.add, op1=ALU.add).then_inc(s_dve.h, 1))
            dve_stt3[(oc, t)] = s_dve.inc()
            wait("scalar", s_dve, dve_stt3[(oc, t)])
            if oc - 2 >= 0:
                wait("scalar", s_pe, pe_st3.get((oc - 2, t), 0))
            em("scalar", lambda e, par=par, t=t:
               e.activation(out=sqstb[:, par * 1024 + t * 512:par * 1024 + (t + 1) * 512],
                            in_=yst[:, par * 1024 + t * 512:par * 1024 + (t + 1) * 512],
                            func=AF.Square).then_inc(s_act.h, 1))
            act_sq3[(oc, t)] = s_act.inc()
            em("scalar", lambda e, par=par, t=t, oc=oc:
               e.activation(out=bufA[:, oc * 1024 + t * 512:oc * 1024 + (t + 1) * 512],
                            in_=yst[:, par * 1024 + t * 512:par * 1024 + (t + 1) * 512],
                            func=AF.Copy).then_inc(s_act.h, 1))
            act_cast3[(oc, t)] = s_act.inc()
        for so in ([oc - 1] if oc >= 1 else []) + ([31] if oc == 31 else []):
            spar = so % 2
            for t in range(2):
                wait("tensor", s_act, act_cast3[(so, t)])
                em("tensor", lambda e, t=t, so=so:
                   e.matmul(pb[4 + t][0:1, :], ones_cb[:, 0:1],
                            bufA[:, so * 1024 + t * 512:so * 1024 + (t + 1) * 512],
                            start=(so == 0), stop=(so == 31),
                            skip_group_check=True))
                if t == 1:
                    em("tensor", lambda e, spar=spar, t=t, so=so:
                       e.matmul(pb[6 + t][0:1, :], ones_cb[:, 0:1],
                                sqstb[:, spar * 1024 + t * 512:spar * 1024 + (t + 1) * 512],
                                start=(so == 0), stop=(so == 31),
                                skip_group_check=True).then_inc(s_pe.h, 1))
                else:
                    em("tensor", lambda e, spar=spar, t=t, so=so:
                       e.matmul(pb[6 + t][0:1, :], ones_cb[:, 0:1],
                                sqstb[:, spar * 1024 + t * 512:spar * 1024 + (t + 1) * 512],
                                start=(so == 0), stop=(so == 31),
                                skip_group_check=True))
            pe_st3[(so, 0)] = pe_st3[(so, 1)] = s_pe.inc()
        if oc + 2 < 32:
            p3_wo_load(oc + 2)
            p3_xf_load(oc + 2)

    def ln_finalize(pe_stats_done):
        wait("vector", s_pe, pe_stats_done)
        for i, bank in enumerate((4, 5, 6, 7)):
            last = (i == 3)
            em("vector", lambda e, bank=bank, i=i, last=last:
               (e.tensor_copy(out=rowb[0:1, i * 512:(i + 1) * 512],
                              in_=pb[bank][0:1, :]).then_inc(s_dve.h, 1) if last
                else e.tensor_copy(out=rowb[0:1, i * 512:(i + 1) * 512],
                                   in_=pb[bank][0:1, :])))
        c1 = s_dve.inc()
        wait("vector", s_dve, c1)
        em("vector", lambda e:
           e.tensor_scalar_mul(out=rowb[0:1, 0:1024], in0=rowb[0:1, 0:1024],
                               scalar1=INV_D))
        em("vector", lambda e:
           e.tensor_scalar_mul(out=rowb[0:1, 1024:2048], in0=rowb[0:1, 1024:2048],
                               scalar1=INV_D).then_inc(s_dve.h, 1))
        c2 = s_dve.inc()
        wait("vector", s_dve, c2)
        em("vector", lambda e:
           e.tensor_mul(out=rowb[0:1, 2048:3072], in0=rowb[0:1, 0:1024],
                        in1=rowb[0:1, 0:1024]).then_inc(s_dve.h, 1))
        c3 = s_dve.inc()
        wait("vector", s_dve, c3)
        em("vector", lambda e:
           e.tensor_sub(out=rowb[0:1, 1024:2048], in0=rowb[0:1, 1024:2048],
                        in1=rowb[0:1, 2048:3072]).then_inc(s_dve.h, 1))
        c4 = s_dve.inc()
        wait("vector", s_dve, c4)
        em("vector", lambda e:
           e.tensor_scalar_add(out=rowb[0:1, 1024:2048], in0=rowb[0:1, 1024:2048],
                               scalar1=LN_EPS).then_inc(s_dve.h, 1))
        c5 = s_dve.inc()
        wait("scalar", s_dve, c5)
        em("scalar", lambda e:
           e.activation(out=rowb[0:1, 2048:3072], in_=rowb[0:1, 1024:2048],
                        func=AF.Sqrt).then_inc(s_act.h, 1))
        c6 = s_act.inc()
        wait("vector", s_act, c6)
        em("vector", lambda e:
           e.reciprocal(out=rowb[0:1, 1024:2048],
                        in_=rowb[0:1, 2048:3072]).then_inc(s_dve.h, 1))
        c7 = s_dve.inc()
        wait("tensor", s_dve, c7)
        for t in range(2):
            em("tensor", lambda e, t=t:
               e.matmul(pb[t][:, :], ones_rf[0:1, :], rowb[0:1, t * 512:(t + 1) * 512],
                        start=True, stop=True,
                        skip_group_check=True).then_inc(s_pe.h, 1))
            s_pe.inc()
            em("tensor", lambda e, t=t:
               e.matmul(pb[2 + t][:, :], ones_rf[0:1, :],
                        rowb[0:1, 1024 + t * 512:1024 + (t + 1) * 512],
                        start=True, stop=True,
                        skip_group_check=True).then_inc(s_pe.h, 1))
            s_pe.inc()
        c8 = s_pe.n
        wait("vector", s_pe, c8)
        for t in range(2):
            em("vector", lambda e, t=t:
               e.tensor_copy(out=mbuf[:, t * 512:(t + 1) * 512], in_=pb[t][:, :]))
            last = (t == 1)
            em("vector", lambda e, t=t, last=last:
               (e.tensor_copy(out=mbuf[:, 1024 + t * 512:1024 + (t + 1) * 512],
                              in_=pb[2 + t][:, :]).then_inc(s_dve.h, 1) if last
                else e.tensor_copy(out=mbuf[:, 1024 + t * 512:1024 + (t + 1) * 512],
                                   in_=pb[2 + t][:, :])))
        return s_dve.inc()

    mbuf_done1 = ln_finalize(pe_st3[(31, 0)])

    dve_sub1, pl_mul1 = {}, {}
    for oc in range(32):
        if oc == 0:
            wait("vector", s_dve, mbuf_done1)
        if oc - 2 >= 0:
            wait("vector", s_pl, pl_mul1[oc - 2])
        em("vector", lambda e, oc=oc, par=oc % 2:
           e.tensor_sub(out=tmp32[:, par * 1024:(par + 1) * 1024],
                        in0=bufA[:, oc * 1024:(oc + 1) * 1024],
                        in1=mbuf[:, 0:1024]).then_inc(s_dve.h, 1))
        dve_sub1[oc] = s_dve.inc()
        if oc == 0:
            wait("gpsimd", s_dve, mbuf_done1)
        wait("gpsimd", s_dve, dve_sub1[oc])
        em("gpsimd", lambda e, oc=oc, par=oc % 2:
           e.tensor_mul(out=bufA[:, oc * 1024:(oc + 1) * 1024],
                        in0=tmp32[:, par * 1024:(par + 1) * 1024],
                        in1=mbuf[:, 1024:2048]).then_inc(s_pl.h, 1))
        pl_mul1[oc] = s_pl.inc()
    y1_done_pl = s_pl.n

    Y4 = 22528

    for i in range(32):
        wait("tensor", s_pl, pl_mul1[i])
        if i == 0:
            wait("tensor", s_dve, mbuf_done1)
        for (src_a, bank0) in ((a1s, 4), (a3s, 6)):
            for t in range(2):
                st, sp = (i == 0), (i == 31)
                if not sp:
                    em("tensor", lambda e, a=src_a, bank=bank0 + t, i=i, t=t, st=st:
                       e.matmul(pb[bank][0:2, :], a[:, i * 2:(i + 1) * 2],
                                bufA[:, i * 1024 + t * 512:i * 1024 + (t + 1) * 512],
                                start=st, stop=False, skip_group_check=True))
                else:
                    em("tensor", lambda e, a=src_a, bank=bank0 + t, i=i, t=t:
                       e.matmul(pb[bank][0:2, :], a[:, i * 2:(i + 1) * 2],
                                bufA[:, i * 1024 + t * 512:i * 1024 + (t + 1) * 512],
                                start=False, stop=True,
                                skip_group_check=True).then_inc(s_pe.h, 1))
    pe_t13_all = s_pe.inc(4)

    prev_comb = 0
    for ic in range(8):
        wait("vector", s_pl, pl_mul1[ic + 24])
        if prev_comb:
            wait("vector", s_dve, prev_comb)
        em("vector", lambda e, ic=ic:
           e.tensor_add(out=yst[:, 0:1024], in0=bufA[:, ic * 1024:(ic + 1) * 1024],
                        in1=bufA[:, (ic + 8) * 1024:(ic + 9) * 1024]))
        em("vector", lambda e, ic=ic:
           e.tensor_add(out=yst[:, 1024:2048],
                        in0=bufA[:, (ic + 16) * 1024:(ic + 17) * 1024],
                        in1=bufA[:, (ic + 24) * 1024:(ic + 25) * 1024]).then_inc(s_dve.h, 1))
        c = s_dve.inc()
        wait("vector", s_dve, c)
        em("vector", lambda e, ic=ic:
           e.tensor_add(out=bufB[:, Y4 + ic * 1024:Y4 + (ic + 1) * 1024],
                        in0=yst[:, 0:1024], in1=yst[:, 1024:2048]).then_inc(s_dve.h, 1))
        prev_comb = s_dve.inc()
    y4_done = s_dve.n

    wait("vector", s_pe, pe_t13_all)
    for (bank0, col0) in ((4, 0), (6, 1024)):
        for t in range(2):
            last = (bank0 == 6 and t == 1)
            em("vector", lambda e, bank=bank0 + t, col=col0 + t * 512, last=last:
               (e.tensor_scalar_mul(out=trow[0:2, col:col + 512],
                                    in0=pb[bank][0:2, :],
                                    scalar1=0.5).then_inc(s_dve.h, 1) if last else
                e.tensor_scalar_mul(out=trow[0:2, col:col + 512],
                                    in0=pb[bank][0:2, :], scalar1=0.5)))
    t13_done = s_dve.inc()

    gw_w13, pe_h, act_h, dve_g, dve_fold, pe_t2 = {}, {}, {}, {}, {}, {}

    def p4a_load(oc):
        par = oc % 2
        if oc - 2 >= 0:
            wait("gpsimd", s_pe, pe_h[oc - 2])
        else:
            wait("gpsimd", s_pe, pe_st3[(31, 0)])
        em("gpsimd", lambda e, oc=oc, par=par:
           e.dma_start(out=w13[:, par * 1024:(par + 1) * 1024],
                       in_=w1_t[oc * 128:(oc + 1) * 128, :]).then_inc(s_gwp[par].h, 16))
        s_gwp[par].inc(16)
        em("gpsimd", lambda e, oc=oc, par=par:
           e.dma_start(out=w13[:, 2048 + par * 1024:2048 + (par + 1) * 1024],
                       in_=w3_t[oc * 128:(oc + 1) * 128, :]).then_inc(s_gwp[par].h, 16))
        s_gwp[par].inc(16)
        em("gpsimd", lambda e, oc=oc, par=par:
           e.dma_start(out=b1s[0:2, par * 128:(par + 1) * 128],
                       in_=b1_t[oc * 2:(oc + 1) * 2, :]).then_inc(s_gwp[par].h, 16))
        s_gwp[par].inc(16)
        em("gpsimd", lambda e, oc=oc, par=par:
           e.dma_start(out=b3s[0:2, par * 128:(par + 1) * 128],
                       in_=b3_t[oc * 2:(oc + 1) * 2, :]).then_inc(s_gwp[par].h, 16))
        gw_w13[oc] = s_gwp[par].inc(16)

    p4a_load(0)
    p4a_load(1)
    for rc in range(HCH):
        for bo in range(4):
            oc = rc * 4 + bo
            par = oc % 2
            hb = [0, 1, 2, 3] if par == 0 else [2, 3, 0, 1]
            wait("tensor", s_gwp[par], gw_w13[oc])
            if oc == 0:
                wait("tensor", s_dve, y4_done)
                wait("tensor", s_dve, t13_done)
                wait("tensor", s_pl, y1_done_pl)
            if oc - 1 >= 0:
                wait("tensor", s_act, act_h[oc - 1])
            for (wcol, bnk0, bsrc, tcol) in ((par * 1024, hb[0], b1s, 0),
                                             (2048 + par * 1024, hb[2], b3s, 1024)):
                for i in range(8):
                    for t in range(2):
                        em("tensor", lambda e, wcol=wcol, bank=bnk0 + t, i=i, t=t:
                           e.matmul(pb[bank][:, :],
                                    w13[:, wcol + i * 128:wcol + (i + 1) * 128],
                                    bufB[:, Y4 + i * 1024 + t * 512:Y4 + i * 1024 + (t + 1) * 512],
                                    start=(i == 0), stop=False,
                                    skip_group_check=True))
                for t in range(2):
                    em("tensor", lambda e, bank=bnk0 + t, bsrc=bsrc, par=par, tcol=tcol, t=t:
                       e.matmul(pb[bank][:, :], bsrc[0:2, par * 128:(par + 1) * 128],
                                trow[0:2, tcol + t * 512:tcol + (t + 1) * 512],
                                start=False, stop=True,
                                skip_group_check=True).then_inc(s_pe.h, 1))
                    s_pe.inc()
            pe_h[oc] = s_pe.n
            wait("scalar", s_pe, pe_h[oc])
            if oc - 2 >= 0:
                wait("scalar", s_dve, dve_g[oc - 2])
            for t in range(2):
                em("scalar", lambda e, bank=hb[0] + t, t=t:
                   e.activation(out=hst[:, t * 512:(t + 1) * 512],
                                in_=pb[bank][:, :], func=AF.Silu))
                last = (t == 1)
                em("scalar", lambda e, bank=hb[2] + t, t=t, last=last:
                   (e.activation(out=hst[:, 1024 + t * 512:1024 + (t + 1) * 512],
                                 in_=pb[bank][:, :],
                                 func=AF.Copy).then_inc(s_act.h, 1) if last else
                    e.activation(out=hst[:, 1024 + t * 512:1024 + (t + 1) * 512],
                                 in_=pb[bank][:, :], func=AF.Copy)))
            act_h[oc] = s_act.inc()
            wait("vector", s_act, act_h[oc])
            if oc - 4 >= 0:
                wait("vector", s_pe, pe_t2.get(oc - 4, 0))
                wait("vector", s_dve, dve_fold.get(rc - 1, 0))
            for t in range(2):
                last = (t == 1)
                em("vector", lambda e, bo=bo, t=t, last=last:
                   (e.tensor_mul(out=gst[:, bo * 1024 + t * 512:bo * 1024 + (t + 1) * 512],
                                 in0=hst[:, t * 512:(t + 1) * 512],
                                 in1=hst[:, 1024 + t * 512:1024 + (t + 1) * 512]).then_inc(s_dve.h, 1)
                    if last else
                    e.tensor_mul(out=gst[:, bo * 1024 + t * 512:bo * 1024 + (t + 1) * 512],
                                 in0=hst[:, t * 512:(t + 1) * 512],
                                 in1=hst[:, 1024 + t * 512:1024 + (t + 1) * 512])))
            dve_g[oc] = s_dve.inc()
            wait("tensor", s_dve, dve_g[oc])
            for t in range(2):
                last = (t == 1)
                em("tensor", lambda e, oc=oc, bo=bo, t=t, last=last:
                   (e.matmul(pb[6 + t][0:2, :], a2s[:, oc * 2:(oc + 1) * 2],
                             gst[:, bo * 1024 + t * 512:bo * 1024 + (t + 1) * 512],
                             start=(oc == 0), stop=(oc == NOC_H - 1),
                             skip_group_check=True).then_inc(s_pe.h, 1) if last else
                    e.matmul(pb[6 + t][0:2, :], a2s[:, oc * 2:(oc + 1) * 2],
                             gst[:, bo * 1024 + t * 512:bo * 1024 + (t + 1) * 512],
                             start=(oc == 0), stop=(oc == NOC_H - 1),
                             skip_group_check=True)))
            pe_t2[oc] = s_pe.inc()
            if oc + 2 < NOC_H:
                p4a_load(oc + 2)
        wait("vector", s_dve, dve_g[rc * 4 + 3])
        if rc >= 1:
            wait("vector", s_dve, dve_fold[rc - 1])
        em("vector", lambda e:
           e.tensor_add(out=tmp32[:, 0:1024], in0=gst[:, 0:1024], in1=gst[:, 1024:2048]))
        em("vector", lambda e:
           e.tensor_add(out=tmp32[:, 1024:2048], in0=gst[:, 2048:3072],
                        in1=gst[:, 3072:4096]).then_inc(s_dve.h, 1))
        c = s_dve.inc()
        wait("vector", s_dve, c)
        em("vector", lambda e, rc=rc:
           e.tensor_add(out=bufB[:, rc * 1024:(rc + 1) * 1024],
                        in0=tmp32[:, 0:1024], in1=tmp32[:, 1024:2048]).then_inc(s_dve.h, 1))
        dve_fold[rc] = s_dve.inc()
    wait("vector", s_pe, pe_t2[NOC_H - 1])
    for t in range(2):
        last = (t == 1)
        em("vector", lambda e, t=t, last=last:
           (e.tensor_scalar_mul(out=trow[0:2, 2048 + t * 512:2048 + (t + 1) * 512],
                                in0=pb[6 + t][0:2, :],
                                scalar1=0.5).then_inc(s_dve.h, 1) if last else
            e.tensor_scalar_mul(out=trow[0:2, 2048 + t * 512:2048 + (t + 1) * 512],
                                in0=pb[6 + t][0:2, :], scalar1=0.5)))
    t2_done = s_dve.inc()
    p4a_pe_end = s_pe.n

    gw_w2, pe_main4, pe_st4, dve_add4, act_sq4, act_cast4 = {}, {}, {}, {}, {}, {}

    def p4b_load(o2):
        par = o2 % 2
        if o2 - 2 >= 0:
            wait("gpsimd", s_pe, pe_main4[(o2 - 2, 1)])
        else:
            wait("gpsimd", s_pe, p4a_pe_end)
        em("gpsimd", lambda e, o2=o2, par=par:
           e.dma_start(out=wbuf[:, par * 4096:par * 4096 + 2816],
                       in_=w2_t[o2 * 128:(o2 + 1) * 128, :]).then_inc(s_gwp[par].h, 16))
        s_gwp[par].inc(16)
        em("gpsimd", lambda e, o2=o2, par=par:
           e.dma_start(out=b2s[0:2, par * 128:(par + 1) * 128],
                       in_=b2_t[0:2, o2 * 128:(o2 + 1) * 128]).then_inc(s_gwp[par].h, 16))
        gw_w2[o2] = s_gwp[par].inc(16)

    p4b_load(0)
    p4b_load(1)
    for o2 in range(32):
        par = o2 % 2
        br2 = [(o2 * 2) % 4, (o2 * 2 + 1) % 4]
        wait("tensor", s_gwp[par], gw_w2[o2])
        if o2 == 0:
            wait("tensor", s_dve, t2_done)
            wait("tensor", s_dve, dve_fold[HCH - 1])
        for t in range(2):
            if (o2 * 2 + t) - 4 >= 0:
                prev = o2 * 2 + t - 4
                wait("tensor", s_dve, dve_add4[(prev // 2, prev % 2)])
        for kc in range(HCH):
            for t in range(2):
                em("tensor", lambda e, br=br2[t], par=par, kc=kc, t=t:
                   e.matmul(pb[br][:, :],
                            wbuf[:, par * 4096 + kc * 128:par * 4096 + (kc + 1) * 128],
                            bufB[:, kc * 1024 + t * 512:kc * 1024 + (t + 1) * 512],
                            start=(kc == 0), stop=False, skip_group_check=True))
        for t in range(2):
            em("tensor", lambda e, br=br2[t], par=par, t=t:
               e.matmul(pb[br][:, :], b2s[0:2, par * 128:(par + 1) * 128],
                        trow[0:2, 2048 + t * 512:2048 + (t + 1) * 512],
                        start=False, stop=True,
                        skip_group_check=True).then_inc(s_pe.h, 1))
            s_pe.inc()
        for t in range(2):
            pe_main4[(o2, t)] = s_pe.n
            wait("vector", s_pe, pe_main4[(o2, t)])
            if o2 - 2 >= 0:
                wait("vector", s_pe, pe_st4.get((o2 - 2, t), 0))
                wait("vector", s_act, act_cast4.get((o2 - 2, t), 0))
            em("vector", lambda e, br=br2[t], par=par, t=t, o2=o2:
               e.tensor_add(out=yst[:, par * 1024 + t * 512:par * 1024 + (t + 1) * 512],
                            in0=pb[br][:, :],
                            in1=bufA[:, o2 * 1024 + t * 512:o2 * 1024 + (t + 1) * 512]).then_inc(s_dve.h, 1))
            dve_add4[(o2, t)] = s_dve.inc()
            wait("scalar", s_dve, dve_add4[(o2, t)])
            if o2 - 2 >= 0:
                wait("scalar", s_pe, pe_st4.get((o2 - 2, t), 0))
            em("scalar", lambda e, par=par, t=t:
               e.activation(out=sqstb[:, par * 1024 + t * 512:par * 1024 + (t + 1) * 512],
                            in_=yst[:, par * 1024 + t * 512:par * 1024 + (t + 1) * 512],
                            func=AF.Square).then_inc(s_act.h, 1))
            act_sq4[(o2, t)] = s_act.inc()
            em("scalar", lambda e, par=par, t=t, o2=o2:
               e.activation(out=bufA[:, o2 * 1024 + t * 512:o2 * 1024 + (t + 1) * 512],
                            in_=yst[:, par * 1024 + t * 512:par * 1024 + (t + 1) * 512],
                            func=AF.Copy).then_inc(s_act.h, 1))
            act_cast4[(o2, t)] = s_act.inc()
        for so in ([o2 - 1] if o2 >= 1 else []) + ([31] if o2 == 31 else []):
            spar = so % 2
            for t in range(2):
                wait("tensor", s_act, act_cast4[(so, t)])
                em("tensor", lambda e, t=t, so=so:
                   e.matmul(pb[4 + t][0:1, :], ones_cb[:, 0:1],
                            bufA[:, so * 1024 + t * 512:so * 1024 + (t + 1) * 512],
                            start=(so == 0), stop=(so == 31),
                            skip_group_check=True))
                if t == 1:
                    em("tensor", lambda e, spar=spar, t=t, so=so:
                       e.matmul(pb[6 + t][0:1, :], ones_cb[:, 0:1],
                                sqstb[:, spar * 1024 + t * 512:spar * 1024 + (t + 1) * 512],
                                start=(so == 0), stop=(so == 31),
                                skip_group_check=True).then_inc(s_pe.h, 1))
                else:
                    em("tensor", lambda e, spar=spar, t=t, so=so:
                       e.matmul(pb[6 + t][0:1, :], ones_cb[:, 0:1],
                                sqstb[:, spar * 1024 + t * 512:spar * 1024 + (t + 1) * 512],
                                start=(so == 0), stop=(so == 31),
                                skip_group_check=True))
            pe_st4[(so, 0)] = pe_st4[(so, 1)] = s_pe.inc()
        if o2 + 2 < 32:
            p4b_load(o2 + 2)

    mbuf_done2 = ln_finalize(pe_st4[(31, 0)])

    ss_out = {}
    pl_mul2 = {}
    for oc in range(32):
        par = oc % 2
        if oc == 0:
            wait("vector", s_dve, mbuf_done2)
            wait("vector", s_act, act_cast4[(31, 1)])
        if oc - 2 >= 0:
            wait("vector", s_pl, pl_mul2[oc - 2])
        em("vector", lambda e, oc=oc, par=par:
           e.tensor_sub(out=tmp32[:, par * 1024:(par + 1) * 1024],
                        in0=bufA[:, oc * 1024:(oc + 1) * 1024],
                        in1=mbuf[:, 0:1024]).then_inc(s_dve.h, 1))
        c = s_dve.inc()
        if oc == 0:
            wait("gpsimd", s_dve, mbuf_done2)
        wait("gpsimd", s_dve, c)
        if oc - 2 >= 0:
            wait("gpsimd", s_ssp[par], ss_out[oc - 2])
        em("gpsimd", lambda e, oc=oc, par=par:
           e.tensor_mul(out=outst[:, par * 1024:(par + 1) * 1024],
                        in0=tmp32[:, par * 1024:(par + 1) * 1024],
                        in1=mbuf[:, 1024:2048]).then_inc(s_pl.h, 1))
        pl_mul2[oc] = s_pl.inc()
        wait("sync", s_pl, pl_mul2[oc])
        em("sync", lambda e, oc=oc, par=par:
           e.dma_start(out=out_fm[oc * 128:(oc + 1) * 128, :],
                       in_=outst[:, par * 1024:(par + 1) * 1024]).then_inc(s_ssp[par].h, 16))
        ss_out[oc] = s_ssp[par].inc(16)

    if taps:
        wait("gpsimd", s_cc[3], 1)
        wait("gpsimd", s_ssp[0], s_ssp[0].n)
        wait("gpsimd", s_ssp[1], s_ssp[1].n)
    for src, dst in taps:
        em("gpsimd", lambda e, a=src, b=dst:
           e.dma_start(out=b[:, :], in_=a[:, :]).then_inc(s_ssp[0].h, 16))
        s_ssp[0].inc(16)

    wait("sync", s_ssp[0], s_ssp[0].n)
    wait("sync", s_ssp[1], s_ssp[1].n)

    with nc.Block() as block:
        @block.tensor
        def _(e):
            for fn in plan["tensor"]:
                fn(e)

        @block.scalar
        def _(e):
            for fn in plan["scalar"]:
                fn(e)

        @block.vector
        def _(e):
            for fn in plan["vector"]:
                fn(e)

        @block.sync
        def _(e):
            for fn in plan["sync"]:
                fn(e)

        @block.gpsimd
        def _(e):
            for fn in plan["gpsimd"]:
                fn(e)

    return nc


def _tile_wT(w):
    n_o, n_i = w.shape[0] // 128, w.shape[1] // 128
    return np.ascontiguousarray(
        w.reshape(n_o, 128, n_i, 128).transpose(0, 3, 2, 1).reshape(n_o * 128, n_i * 128))


def _prep_w13(w):
    wr = np.zeros((4, HCH, 128, 1024), dtype=np.float32)
    for bo in range(4):
        seg = w[bo * 2752:(bo + 1) * 2752]
        wr[bo, :21] = seg[:2688].reshape(21, 128, 1024)
        wr[bo, 21, :64] = seg[2688:]
    arr = wr.reshape(4, HCH, 128, 8, 128).transpose(1, 0, 4, 3, 2)
    return np.ascontiguousarray(arr.reshape(NOC_H * 128, 1024))


def _prep_w2(w):
    wp = np.zeros((4096, HCH * 128), dtype=np.float32)
    wp[:, :2752] = w
    arr = wp.reshape(32, 128, HCH, 128).transpose(0, 3, 2, 1)
    return np.ascontiguousarray(arr.reshape(32 * 128, HCH * 128))


def _prep_b13(bmat):
    br = np.zeros((4, HCH, 128, 2), dtype=np.float32)
    for bo in range(4):
        seg = bmat[bo * 2752:(bo + 1) * 2752]
        br[bo, :21] = seg[:2688].reshape(21, 128, 2)
        br[bo, 21, :64] = seg[2688:]
    arr = br.transpose(1, 0, 3, 2)
    return np.ascontiguousarray(arr.reshape(NOC_H * 2, 128))


def _prep_a2(a2):
    ar = np.zeros((4, HCH, 128, 2), dtype=np.float32)
    for bo in range(4):
        seg = a2[:, bo * 2752:(bo + 1) * 2752].T
        ar[bo, :21] = seg[:2688].reshape(21, 128, 2)
        ar[bo, 21, :64] = seg[2688:]
    arr = ar.transpose(1, 0, 2, 3).reshape(NOC_H, 128, 2).transpose(1, 0, 2)
    return np.ascontiguousarray(arr.reshape(128, NOC_H * 2))


def kernel(**inputs):
    inp = {k: np.asarray(v) for k, v in inputs.items()}
    x = inp["L0_attn_x"]

    shared = {
        "wq_t": _tile_wT(inp["L0_attn_Wq"]),
        "wk_t": _tile_wT(inp["L0_attn_Wk"]),
        "wv_t": _tile_wT(inp["L0_attn_Wv"]),
        "wo_t": _tile_wT(inp["L0_attn_Wo"]),
        "bq_f": np.ascontiguousarray(inp["L0_attn_bq"].reshape(32, 128).T),
        "bk_f": np.ascontiguousarray(inp["L0_attn_bk"].reshape(32, 128).T),
        "bv_f": np.ascontiguousarray(inp["L0_attn_bv"].reshape(32, 128).T),
        "bo_f": np.ascontiguousarray(inp["L0_attn_bo"].reshape(32, 128).T),
        "w1_t": _prep_w13(inp["L0_ffn_w1q"]),
        "w3_t": _prep_w13(inp["L0_ffn_w3q"]),
        "w2_t": _prep_w2(inp["L0_ffn_w2q"]),
        "b1_t": _prep_b13(inp["L0_ffn_b1"]),
        "b3_t": _prep_b13(inp["L0_ffn_b3"]),
        "b2_t": np.ascontiguousarray(inp["L0_ffn_b2"].T),
        "a1_t": np.ascontiguousarray(
            inp["L0_ffn_a1"].reshape(2, 32, 128).transpose(2, 1, 0).reshape(128, 64)),
        "a3_t": np.ascontiguousarray(
            inp["L0_ffn_a3"].reshape(2, 32, 128).transpose(2, 1, 0).reshape(128, 64)),
        "a2_t": _prep_a2(inp["L0_ffn_a2"]),
    }

    xT = [np.ascontiguousarray(x[b_].T) for b_ in range(2)]
    in_maps = []
    for c in range(8):
        cc0, cc1 = c, 15 - c
        parts = []
        for cc in (cc0, cc1):
            bb, s0 = cc // 8, (cc % 8) * 512
            parts.append(xT[bb][:, s0:s0 + 512])
        x_c = np.ascontiguousarray(np.concatenate(parts, axis=1))
        m = {"x_fm": x_c}
        m.update(shared)
        in_maps.append(m)

    nc = build_program()
    res = run_bass_kernel_spmd(nc, in_maps, core_ids=list(range(8)))

    out = np.empty((2, 4096, 4096), dtype=np.float32)
    for c in range(8):
        oc_fm = res.results[c]["out_fm"]
        for i, cc in enumerate((c, 15 - c)):
            bb, s0 = cc // 8, (cc % 8) * 512
            out[bb, s0:s0 + 512, :] = oc_fm[:, i * 512:(i + 1) * 512].T
    if DEBUG_TAPS:
        kernel._tap_results = res
    return out
